# version 25
# baseline (speedup 1.0000x reference)
import numpy as np
import ml_dtypes
import concourse.bass as bass
import concourse.mybir as mybir
from concourse.bass_utils import run_bass_kernel_spmd

F32 = mybir.dt.float32
BF16 = mybir.dt.bfloat16
FP8 = mybir.dt.float8e4
I32 = mybir.dt.int32
AL = mybir.AluOpType
AF = mybir.ActivationFunctionType
AX = mybir.AxisListType.X
AP = bass.AP

S = 8192
D = 2048
MAGIC = float(np.float32(1.5 * 2 ** 23))

ENGINES = ("sync", "scalar", "vector", "tensor", "gpsimd")
DEBUG_TAPS = False


class Builder:

    NDMA = 12

    def __init__(self, nc):
        self.nc = nc
        self.q = {e: [] for e in ENGINES}
        self.sem_count = {}
        self.watermark = {}
        self.buf = {}
        self.sems = {}
        self.dma_rot = {e: 0 for e in ENGINES}
        self.pending_reads = {e: [] for e in ENGINES}
        self.pending_writes = {e: [] for e in ENGINES}
        self.ctx = None
        for e in ENGINES:
            self.sem_count["c_" + e] = 0
            for k in range(self.NDMA):
                self.sem_count[f"d_{e}_{k}"] = 0

    def _need_wait(self, eng, tok):
        if tok is None:
            return False
        sem, val = tok
        return self.watermark.get((eng, sem), 0) < val

    def _mark_wait(self, eng, tok):
        sem, val = tok
        if self.watermark.get((eng, sem), 0) < val:
            self.watermark[(eng, sem)] = val

    @staticmethod
    def _key(x, key=None):
        if isinstance(x, tuple):
            x, key = x
        t = getattr(x, "tensor", x)
        return (t.name, key)

    def op(self, eng, fn, reads=(), writes=(), dma=False, inc=True, tag=None):
        wmax = {}

        def addw(tok):
            if self._need_wait(eng, tok):
                wmax[tok[0]] = max(wmax.get(tok[0], 0), tok[1])

        for r in reads:
            k = self._key(r)
            st = self.buf.get(k)
            if st and st["w"]:
                addw(st["w"])
        wkeys = []
        for w in writes:
            k = self._key(w)
            wkeys.append(k)
            st = self.buf.get(k)
            if st:
                if st["w"]:
                    addw(st["w"])
                for sem2, val2 in st["r"].items():
                    addw((sem2, val2))
        if dma:
            kk = self.dma_rot[eng] % self.NDMA
            self.dma_rot[eng] += 1
            sem = f"d_{eng}_{kk}"
            incv = 16
            addw((sem, self.sem_count[sem]))
        elif tag == "cc":
            self.ncc = getattr(self, "ncc", 0) + 1
            sem = f"cc{self.ncc}"
            self.sem_count.setdefault(sem, 0)
            incv = 1
        else:
            sem = "c_" + eng
            incv = 1
        waits = sorted(wmax.items())
        for tokw in waits:
            self._mark_wait(eng, tokw)
        if inc:
            self.sem_count[sem] += incv
            tok = (sem, self.sem_count[sem])
        else:
            tok = None
        b = self

        def emit(e, fn=fn, waits=tuple(waits), sem=sem, incv=incv, inc=inc):
            for s, v in waits:
                e.wait_ge(b.sems[s], v)
            inst = fn(e)
            if inc:
                inst.then_inc(b.sems[sem], incv)
            return inst

        self.q[eng].append(emit)
        if inc:
            rl = list(reads) + self.pending_reads[eng]
            wl = wkeys + self.pending_writes[eng]
            self.pending_reads[eng] = []
            self.pending_writes[eng] = []
            for r in rl:
                k = self._key(r)
                st = self.buf.setdefault(k, {"w": None, "r": {}})
                st["r"][tok[0]] = max(st["r"].get(tok[0], 0), tok[1])
            for k in wl:
                self.buf[k] = {"w": tok, "r": {}}
        else:
            self.pending_reads[eng] += list(reads)
            self.pending_writes[eng] += wkeys
        return tok

    def wait(self, eng, tok):
        if not self._need_wait(eng, tok):
            return
        self._mark_wait(eng, tok)
        b = self

        def emit(e, tok=tok):
            e.wait_ge(b.sems[tok[0]], tok[1])
        self.q[eng].append(emit)

    def drain_dma(self, wait_eng, issue_eng):
        for k in range(self.NDMA):
            sem = f"d_{issue_eng}_{k}"
            v = self.sem_count[sem]
            if v:
                self.wait(wait_eng, (sem, v))

    def tok_of(self, x, key=None):
        st = self.buf.get(self._key(x, key))
        return st["w"] if st else None

    def flush(self):
        nc = self.nc
        for name in list(self.sem_count):
            self.sems[name] = self.ctx.enter_context(nc.semaphore(name))
        with nc.Block() as block:
            for ename in ENGINES:
                queue = self.q[ename]
                section = getattr(block, ename)

                def body(e, queue=queue):
                    for f in queue:
                        f(e)
                section(body)


def cdiv(a, b):
    return (a + b - 1) // b


WSPECS = [
    ("qkv", "L0_attn_qkv_w", "tern", 6144, 2048),
    ("ow",  "L0_attn_o_w", "tern", 2048, 2048),
    ("dn0", "L0_attn_ffn_down_w", "tern", 2048, 4096),
    ("up0", "L0_attn_ffn_up_w", "cast", 8192, 2048),
    ("g8",  "L0_ffn_w_gate", "fp8", 5632, 2048),
    ("u8",  "L0_ffn_w_up", "fp8", 5632, 2048),
    ("d8",  "L0_ffn_w_down", "fp8", 2048, 5632),
    ("wq",  "L1_attn_Wq", "rowcast", 2048, 2048),
    ("wk",  "L1_attn_Wk", "rowcast", 2048, 512),
    ("wv",  "L1_attn_Wv", "rowcast", 2048, 512),
    ("wo",  "L1_attn_Wo", "rowcast", 2048, 2048),
    ("ig",  "L1_ffn_wq_gate", "int8", 5632, 2048),
    ("iu",  "L1_ffn_wq_up", "int8", 5632, 2048),
    ("id8", "L1_ffn_wq_down", "int8", 2048, 5632),
]
TERN = [w for w in WSPECS if w[2] == "tern"]
FP8W = [w for w in WSPECS if w[2] == "fp8"]


def host_constants(core, ncores, T=1024):
    bf = ml_dtypes.bfloat16
    c = {}
    c["c_ident"] = np.eye(128, dtype=bf)
    c["c_identf"] = np.eye(128, dtype=np.float32)
    c["c_ones"] = np.ones((128, 8), dtype=bf)
    masks = np.zeros((4, 128, 512), dtype=np.float32)
    for di, delta in enumerate((0, 128, 256, 384)):
        k = np.arange(128)[:, None] + delta
        q = np.arange(512)[None, :]
        masks[di] = np.where(k > q, -1e30, 0.0)
    c["c_masks"] = masks.astype(bf)
    m01 = np.zeros((4, 128, 512), dtype=np.float32)
    for di, delta in enumerate((0, 128, 256, 384)):
        k = np.arange(128)[:, None] + delta
        q = np.arange(512)[None, :]
        m01[di] = np.where(k > q, 0.0, 1.0)
    c["c_mask01"] = m01.astype(bf)
    pos = np.arange(core * T, (core + 1) * T, dtype=np.float64)
    inv0 = 1.0 / (10000.0 ** (np.arange(0, 128, 2, dtype=np.float64) / 128.0))
    f0 = inv0[:, None] * pos[None, :]
    c["c_cos0"] = np.concatenate([np.cos(f0), np.cos(f0)], 0).astype(bf)
    c["c_sin0"] = np.concatenate([-np.sin(f0), np.sin(f0)], 0).astype(bf)
    rot = np.zeros((128, 128), dtype=np.float32)
    for i in range(128):
        rot[(i + 64) % 128, i] = 1.0
    c["c_rot"] = rot.astype(bf)
    inv1 = 1.0 / (10000.0 ** (np.arange(0, 256, 2, dtype=np.float64) / 256.0))
    f1 = inv1[:, None] * pos[None, :]
    c["c_cos1"] = np.cos(f1).astype(bf)
    c["c_sin1"] = np.sin(f1).astype(bf)
    return c


def host_shards(inputs, core, ncores):
    sh = {}
    for nm, src, kind, d0, d1 in WSPECS:
        w = np.asarray(inputs[src])
        if kind == "rowcast":
            r = d0 // ncores
            sh["sh_" + nm] = np.ascontiguousarray(w[core * r:(core + 1) * r, :])
        else:
            cshard = d1 // ncores
            sh["sh_" + nm] = np.ascontiguousarray(w[:, core * cshard:(core + 1) * cshard])
    return sh



class P:
    pass


class P:
    pass


class Arena:

    def __init__(self, t, nslots):
        self.t = t
        self.t32 = t.ap().bitcast(F32).tensor
        self.t8 = t.ap().bitcast(FP8).tensor
        self.nslots = nslots
        self.free_ = list(range(nslots))

    def alloc(self, n):
        fs = sorted(self.free_)
        runs = []
        start, prev = None, None
        for v in fs:
            if start is None:
                start, prev = v, v
            elif v == prev + 1:
                prev = v
            else:
                runs.append((start, prev - start + 1))
                start, prev = v, v
        if start is not None:
            runs.append((start, prev - start + 1))
        cands = [r for r in runs if r[1] >= n]
        if not cands:
            raise RuntimeError(f"arena: no {n} contiguous slots free (runs={runs})")
        base = min(cands, key=lambda r: r[1])[0]
        for s in range(base, base + n):
            self.free_.remove(s)
        return Region(self, base, n)

    def free(self, r):
        self.free_ += list(range(r.base, r.base + r.n))


class Region:
    def __init__(self, ar, base, n):
        self.ar, self.base, self.n = ar, base, n

    def keys(self, c0=0, nc=None):
        nc = self.n if nc is None else nc
        return [(self.ar.t, self.base + c0 + i) for i in range(nc)]

    def bf(self, chunk=0, t0=0, nt=None, nchunk=1):
        ar = self.ar
        nt = 1024 if nt is None else nt
        if nchunk == 1:
            return AP(ar.t, (self.base + chunk) * 1024 + t0,
                      [[ar.nslots * 1024, 128], [1, nt]])
        return AP(ar.t, (self.base + chunk) * 1024 + t0,
                  [[ar.nslots * 1024, 128], [1024, nchunk], [1, nt]])

    def f32(self, chunk=0, t0=0, nt=None, nchunk=1):
        ar = self.ar
        nt = 512 if nt is None else nt
        if nchunk == 1:
            return AP(ar.t32, (self.base + chunk) * 512 + t0,
                      [[ar.nslots * 512, 128], [1, nt]])
        return AP(ar.t32, (self.base + chunk) * 512 + t0,
                  [[ar.nslots * 512, 128], [512, nchunk], [1, nt]])

    def fp8(self, chunk=0, t0=0, nt=None):
        ar = self.ar
        nt = 2048 if nt is None else nt
        return AP(ar.t8, (self.base + chunk) * 2048 + t0,
                  [[ar.nslots * 2048, 128], [1, nt]])


def emit_staging_setup(b, p, ncores):
    nc = b.nc
    rg = [list(range(ncores))]

    p.wt, p.agin, p.shin = {}, {}, {}
    for nm, src, kind, d0, d1 in WSPECS:
        if kind == "rowcast":
            r = d0 // ncores
            p.shin[nm] = nc.dram_tensor("sh_" + nm, [r, d1], F32, kind="ExternalInput").ap()
            p.agin[nm] = nc.dram_tensor("agi_" + nm, [r, d1], BF16).ap()
            p.wt[nm] = nc.dram_tensor("ag_" + nm, [d0, d1], BF16,
                                      addr_space="Shared" if ncores == 8 else "Local").ap()
        else:
            csh = d1 // ncores
            dt = I32 if kind == "int8" else F32
            p.shin[nm] = nc.dram_tensor("sh_" + nm, [d0, csh], dt, kind="ExternalInput").ap()
            p.agin[nm] = nc.dram_tensor("agi_" + nm, [csh, d0], BF16).ap()
            p.wt[nm] = nc.dram_tensor("ag_" + nm, [d1, d0], BF16,
                                      addr_space="Shared" if ncores == 8 else "Local").ap()
    dri = nc.dram_tensor("dri", [5632], I32, kind="ExternalInput").ap()
    p.dricol = p.sb("dricol", [128, 44], I32)
    b.op("scalar", lambda e: e.dma_start(
        AP(p.dricol, 0, [[44, 128], [1, 44]]),
        AP(dri.tensor, 0, [[1, 128], [128, 44]]),
        allow_slow_non_contiguous=True), writes=[p.dricol], dma=True)
    ar_add_i = nc.dram_tensor("ar_add_i", [1, 8], F32).ap()
    ar_add_o = nc.dram_tensor("ar_add_o", [1, 8], F32).ap()
    NMX = 104
    ar_max_i = nc.dram_tensor("ar_max_i", [128, NMX], F32).ap()
    ar_max_o = nc.dram_tensor("ar_max_o", [128, NMX], F32).ap()

    nwsh = nc.dram_tensor("sh_nw", [1, D // ncores], F32, kind="ExternalInput").ap()
    nwreg = p.arena.alloc(1)
    BUF = {}

    def _alloc_bufs():
        BUF["in"] = [p.arena.alloc(4) for _ in range(2)]
        BUF["f2"] = [p.arena.alloc(4) for _ in range(2)]
        BUF["q"] = [p.arena.alloc(2) for _ in range(2)]
        BUF["t"] = [p.arena.alloc(1) for _ in range(2)]
    _alloc_bufs()
    rtmp = p.sb("st_rtmp", [128, 64], F32)
    acc_sum = p.sb("st_acc", [128, 4], F32)
    rmx = p.sb("st_rmx", [128, NMX], F32)
    rinv = p.sb("st_rinv", [128, NMX], F32)
    swcol = p.sb("st_swcol", [128, 4], F32)
    srow = p.sb("st_srow", [1, 16], F32)
    p.rmx, p.rinv, p.swcol, p.swrow = rmx, rinv, swcol, srow

    RMX_OFF = {"g8": 0, "u8": 44, "d8": 88}
    TIDX = {nm: i for i, (nm, *_r) in enumerate(TERN)}
    NUMEL = {nm: d0 * d1 for nm, _s, _k, d0, d1 in WSPECS}

    def chunk_iter(d0, csh):
        assert csh <= 2048, "staging chunker: shard too wide (use 8 cores)"
        jtot = d0 // 128
        jg = max(1, min(2048 // csh, jtot))
        for j0 in range(0, jtot, jg):
            yield j0, min(jg, jtot - j0)

    b.op("vector", lambda e: e.memset(srow[0:1, 0:16], 0.0), writes=[(srow, "sums"), (srow, "gsums")])

    PERM = {"g8", "u8"}

    def load_chunk(nm, kind, j0, jn, csh, inb, d0, q="scalar"):
        if nm in PERM:
            for j in range(jn):
                dst = inb.f32(0, j * csh, csh)
                b.op("gpsimd", lambda e, dst=dst, j=j, j0=j0, csh=csh, nm=nm: e.indirect_dma_start(
                    dst, None, p.shin[nm],
                    bass.IndirectOffsetOnAxis(p.dricol[:, j0 + j:j0 + j + 1], 0)),
                    reads=[p.shin[nm], p.dricol], writes=inb.keys(), dma=True)
        else:
            src_ap = AP(p.shin[nm].tensor, j0 * 128 * csh,
                        [[csh, 128], [128 * csh, jn], [1, csh]])
            if kind == "int8":
                d = inb.f32(0, 0, jn * csh)
                d = AP(d.tensor, d.offset, d.ap).bitcast(I32)
            else:
                d = inb.f32(0, 0, jn * csh)
            b.op(q, lambda e, a=src_ap, d=d: e.dma_start(d, a),
                 writes=inb.keys(), dma=True)

    ptr = {"i": 0}

    def stage_one(nm, src, kind, d0, d1):
        csh = d1 // ncores if kind != "rowcast" else d1
        if kind == "rowcast":
            b.op("gpsimd", lambda e: e.dma_start(p.agin[nm], p.shin[nm]),
                 reads=[p.shin[nm]], writes=[p.agin[nm]], dma=True)
            return
        ti = TIDX.get(nm)
        for j0, jn in chunk_iter(d0, csh):
            bi = ptr["i"] % 2
            ptr["i"] += 1
            inb, f2b, qb, tb = BUF["in"][bi], BUF["f2"][bi], BUF["q"][bi], BUF["t"][bi]
            n = jn * csh
            load_chunk(nm, kind, j0, jn, csh, inb, d0, q="gpsimd")
            if kind == "int8":
                src_i = inb.f32(0, 0, n)
                src_i = AP(src_i.tensor, src_i.offset, src_i.ap).bitcast(I32)
                b.op("vector", lambda e, s=src_i, dq=qb.bf(0, 0, n): e.tensor_scalar(
                    dq, s, -128.0, None, AL.add), reads=inb.keys(), writes=qb.keys())
            else:
                if kind == "cast":
                    s3 = inb.f32(0, 0, csh)
                    s3 = AP(s3.tensor, s3.offset, [s3.ap[0], [csh, jn], [1, csh]])
                    nb = nwreg.f32(0, 0, csh)
                    nb = AP(nb.tensor, nb.offset, [nb.ap[0], [0, jn], [1, csh]])
                    d3 = qb.bf(0, 0, csh)
                    d3 = AP(d3.tensor, d3.offset, [d3.ap[0], [csh, jn], [1, csh]])
                    b.op("vector", lambda e, s3=s3, nb=nb, d3=d3: e.tensor_tensor(
                        d3, s3, nb, AL.mult),
                        reads=inb.keys() + nwreg.keys(), writes=qb.keys())
                elif kind == "tern":
                    b.op("vector", lambda e, s=inb.f32(0, 0, n), d2=f2b.f32(0, 0, n), ti=ti:
                         e.tensor_scalar(d2, s, swcol[:, ti:ti + 1], MAGIC, AL.mult, AL.add),
                         reads=inb.keys() + [(swcol, None)], writes=f2b.keys())
                    b.op("vector", lambda e, d2=f2b.f32(0, 0, n): e.tensor_scalar(
                        d2, d2, -MAGIC, 1.0, AL.add, AL.min),
                        reads=f2b.keys(), writes=f2b.keys())
                    b.op("vector", lambda e, d2=f2b.f32(0, 0, n), dq=qb.bf(0, 0, n):
                         e.tensor_scalar(dq, d2, -1.0, None, AL.max),
                         reads=f2b.keys(), writes=qb.keys())
                else:
                    off = RMX_OFF[nm]
                    for j in range(jn):
                        b.op("scalar", lambda e, o=off + j0 + j,
                             d8=f2b.fp8(0, j * csh, csh), s3=inb.f32(0, j * csh, csh):
                             e.activation(d8, s3, AF.Copy, scale=rinv[:, o:o + 1]),
                             reads=inb.keys() + [(rinv, None)], writes=f2b.keys())
                    b.op("scalar", lambda e, s8=f2b.fp8(0, 0, n), dq=qb.bf(0, 0, n):
                         e.activation(dq, s8, AF.Copy), reads=f2b.keys(), writes=qb.keys())
            ncc = cdiv(csh, 128)
            for cc in range(ncc):
                cn = min(128, csh - cc * 128)
                for j in range(jn):
                    b.op("tensor", lambda e, j=j, cc=cc, cn=cn, qb=qb, csh=csh: e.transpose(
                        p.pbtr[0:cn, j * 128:(j + 1) * 128],
                        qb.bf(0, j * csh + cc * 128, cn), p.ident[:, :]),
                        reads=qb.keys() + [p.ident], writes=[(p.pb[7], None)],
                        inc=(j == jn - 1))
                b.op("vector", lambda e, cn=cn, jn=jn, tb=tb: e.tensor_copy(
                    AP(tb.ar.t, tb.base * 1024, [[tb.ar.nslots * 1024, cn], [1, jn * 128]]),
                    p.pbtr[0:cn, 0:jn * 128]),
                    reads=[(p.pb[7], None)], writes=tb.keys())
                dst = AP(p.agin[nm].tensor, (cc * 128) * d0 + j0 * 128,
                         [[d0, cn], [1, jn * 128]])
                src_t = AP(tb.ar.t, tb.base * 1024,
                           [[tb.ar.nslots * 1024, cn], [1, jn * 128]])
                b.op("scalar", lambda e, dst=dst, src_t=src_t: e.dma_start(dst, src_t),
                     reads=tb.keys(), writes=[(p.agin[nm], (cc, j0))], dma=True)

    WD = {w[0]: w for w in WSPECS}

    def ag2(names):
        reads = []
        for nm in names:
            _, _, kind, d0, d1 = WD[nm]
            if kind == "rowcast":
                reads.append(p.agin[nm])
            else:
                csh = d1 // ncores
                for cc in range(cdiv(csh, 128)):
                    for j0, _jn in chunk_iter(d0, csh):
                        reads.append((p.agin[nm], (cc, j0)))
        for nm in names:
            b.op("gpsimd", lambda e, nm=nm: e.collective_compute(
                "AllGather", AL.bypass, replica_groups=rg,
                ins=[p.agin[nm]], outs=[p.wt[nm]]),
                reads=reads, writes=[p.wt[nm]], tag="cc")


    def _passA():
        par = 0
        for nm, src, kind, d0, d1 in [w for w in WSPECS if w[2] == "tern"]:
            csh = d1 // ncores
            ti = TIDX.get(nm)
            first = True
            for j0, jn in chunk_iter(d0, csh):
                bi = par % 2
                par += 1
                inb = BUF["in"][bi]
                load_chunk(nm, kind, j0, jn, csh, inb, d0)
                c3 = inb.f32(0, 0, csh)
                c3 = AP(c3.tensor, c3.offset, [c3.ap[0], [csh, jn], [1, csh]])
                b.op("vector", lambda e, c3=c3, jn=jn: e.reduce_sum(
                    rtmp[:, 0:jn], c3, axis=AX, apply_absolute_value=True),
                    reads=inb.keys(), writes=[(rtmp, "a")])
                b.op("vector", lambda e, jn=jn: e.reduce_sum(
                    rtmp[:, 48:49], rtmp[:, 0:jn], axis=AX),
                    reads=[(rtmp, "a")], writes=[(rtmp, "b")])
                if first:
                    b.op("vector", lambda e, ti=ti: e.tensor_copy(
                        acc_sum[:, ti:ti + 1], rtmp[:, 48:49]),
                        reads=[(rtmp, "b")], writes=[(acc_sum, ti)])
                else:
                    b.op("vector", lambda e, ti=ti: e.tensor_add(
                        acc_sum[:, ti:ti + 1], acc_sum[:, ti:ti + 1], rtmp[:, 48:49]),
                        reads=[(rtmp, "b"), (acc_sum, ti)], writes=[(acc_sum, ti)])
                first = False
        for ti in range(len(TERN)):
            b.op("tensor", lambda e, ti=ti: e.matmul(
                p.pb[6][0:1, ti:ti + 1], p.onesf[:, 0:1], acc_sum[:, ti:ti + 1],
                start=(ti == 0), stop=(ti == len(TERN) - 1), skip_group_check=True),
                reads=[(acc_sum, ti), p.onesf], writes=[(p.pb[6], None)],
                inc=(ti == len(TERN) - 1))
        b.op("vector", lambda e: e.tensor_copy(srow[0:1, 0:4], p.pb[6][0:1, 0:4]),
             reads=[(p.pb[6], None)], writes=[(srow, "sums")])
        b.op("gpsimd", lambda e: e.dma_start(ar_add_i[0:1, 0:8], srow[0:1, 0:8]),
             reads=[(srow, "sums")], writes=[ar_add_i], dma=True)
        b.op("gpsimd", lambda e: e.collective_compute(
            "AllReduce", AL.add, replica_groups=rg, ins=[ar_add_i], outs=[ar_add_o]),
            reads=[ar_add_i], writes=[ar_add_o], tag="cc")
        b.op("gpsimd", lambda e: e.dma_start(srow[0:1, 4:8], ar_add_o[0:1, 0:4]),
             reads=[ar_add_o], writes=[(srow, "gsums")], dma=True)

        for nm, src, kind, d0, d1 in [w for w in WSPECS if w[2] == "fp8"]:
            csh = d1 // ncores
            for j0, jn in chunk_iter(d0, csh):
                bi = par % 2
                par += 1
                inb = BUF["in"][bi]
                load_chunk(nm, kind, j0, jn, csh, inb, d0)
                c3 = inb.f32(0, 0, csh)
                c3 = AP(c3.tensor, c3.offset, [c3.ap[0], [csh, jn], [1, csh]])
                off = RMX_OFF[nm]
                b.op("vector", lambda e, c3=c3, jn=jn, o=off + j0: e.reduce_max(
                    AP(rmx, o, [[NMX, 128], [1, jn]]), c3,
                    axis=AX, apply_absolute_value=True),
                    reads=inb.keys(), writes=[(rmx, (nm, j0))])
        b.op("gpsimd", lambda e: e.dma_start(ar_max_i, rmx[:, :]),
             reads=[(rmx, (nm2, j2)) for nm2, _s, k2, d02, d12 in WSPECS if k2 == "fp8"
                    for j2, _jn in chunk_iter(d02, d12 // ncores)],
             writes=[ar_max_i], dma=True)
        b.op("gpsimd", lambda e: e.collective_compute(
            "AllReduce", AL.max, replica_groups=rg, ins=[ar_max_i], outs=[ar_max_o]),
            reads=[ar_max_i], writes=[ar_max_o], tag="cc")
        b.op("gpsimd", lambda e: e.dma_start(rmx[:, :], ar_max_o),
             reads=[ar_max_o], writes=[(rmx, "g")], dma=True)

    def _scales():
        for ti, (nm, _s, _k, d0, d1) in enumerate(TERN):
            b.op("vector", lambda e, ti=ti, n=NUMEL[TERN[ti][0]]: e.tensor_scalar(
                srow[0:1, 8 + ti:9 + ti], srow[0:1, 4 + ti:5 + ti], 1.0 / n, 1e-5,
                AL.mult, AL.max), reads=[(srow, "gsums")], writes=[(srow, "m" + str(ti))])
        NTN = len(TERN)
        b.op("vector", lambda e: e.reciprocal(srow[0:1, 12:12 + NTN], srow[0:1, 8:8 + NTN]),
             reads=[(srow, "m0"), (srow, "m1"), (srow, "m2")], writes=[(srow, "recip")])
        b.op("tensor", lambda e: e.matmul(
            p.pb[6][:, 0:NTN], p.onesf[0:1, 0:128], srow[0:1, 12:12 + NTN],
            start=True, stop=True, skip_group_check=True),
            reads=[(srow, "recip"), p.onesf], writes=[(p.pb[6], None)])
        b.op("vector", lambda e: e.tensor_copy(swcol[:, 0:NTN], p.pb[6][:, 0:NTN]),
             reads=[(p.pb[6], None)], writes=[swcol])
        b.op("vector", lambda e: e.reciprocal(rinv[:, :], rmx[:, :]),
             reads=[(rmx, "g")], writes=[(rinv, "a")])
        b.op("vector", lambda e: e.tensor_scalar(rinv[:, :], rinv[:, :], 224.0, None, AL.mult),
             reads=[(rinv, "a")], writes=[rinv])

        cshD = D // ncores
        b.op("scalar", lambda e: e.dma_start(p.rowp[0:1, 0:cshD], nwsh),
             writes=[rkey(p, 0)], dma=True)
        for g0 in range(0, cshD, 512):
            gn = min(512, cshD - g0)
            b.op("tensor", lambda e, g0=g0, gn=gn: e.matmul(
                p.pb[5][:, 0:gn], p.onesf[0:1, 0:128], p.rowp[0:1, g0:g0 + gn],
                start=True, stop=True, skip_group_check=True),
                reads=[rkey(p, 0), p.onesf], writes=[(p.pb[5], None)])
            b.op("vector", lambda e, g0=g0, gn=gn: e.tensor_copy(
                nwreg.f32(0, g0, gn), p.pb[5][:, 0:gn]),
                reads=[(p.pb[5], None)], writes=nwreg.keys())

    def _quant(names):
        for nm in names:
            stage_one(*WD[nm])
        ag2(names)

    def _free(nw=False):
        for r in BUF["in"] + BUF["f2"] + BUF["q"] + BUF["t"] + ([nwreg] if nw else []):
            p.arena.free(r)

    p.stg = {"passA": _passA, "scales": _scales, "quant": _quant, "free": _free,
             "realloc": _alloc_bufs}



def build_program(ncores=8, T=1024, upto="all"):
    assert T == 1024
    from contextlib import ExitStack
    nc = bass.Bass("TRN2", target_bir_lowering=False)
    b = Builder(nc)
    ctx = ExitStack()
    b.ctx = ctx
    p = P()
    p.ctx = ctx
    p.T = T
    p.ncores = ncores

    x_in = nc.dram_tensor("x", [T, D], F32, kind="ExternalInput").ap()
    out = nc.dram_tensor("out", [T, D], F32, kind="ExternalOutput").ap()
    ident_h = nc.dram_tensor("c_ident", [128, 128], BF16, kind="ExternalInput").ap()
    identf_h = nc.dram_tensor("c_identf", [128, 128], F32, kind="ExternalInput").ap()

    def sb(name, shape, dt=F32):
        return ctx.enter_context(nc.sbuf_tensor(name, shape, dt))
    p.sb = sb

    NT = D // 128
    p.ident = sb("ident", [128, 128], BF16)
    p.identf = sb("identf", [128, 128], F32)
    p.ones = sb("ones", [128, 128], BF16)
    p.onesf = sb("onesf", [128, 128], F32)
    p.xT = sb("xT", [128, NT * T], BF16)
    NSLOT = 71
    p.arena = Arena(sb("arena", [128, NSLOT * 1024], BF16), NSLOT)
    p.rowp = sb("rowp", [128, 3072], F32)
    p.pb = [ctx.enter_context(nc.psum_tensor(f"pb{i}", [128, 512], F32)) for i in range(8)]
    p.pbtr = p.pb[7].ap().bitcast(BF16).tensor
    p.tr8 = sb("tr8", [128, 128], F32)
    p.mcol = sb("mcol", [128, 16], F32)

    def xTap(c0, nchunk=1, t0=0, nt=None):
        nt = T if nt is None else nt
        if nchunk == 1:
            return AP(p.xT, c0 * T + t0, [[NT * T, 128], [1, nt]])
        return AP(p.xT, c0 * T + t0, [[NT * T, 128], [T, nchunk], [1, nt]])
    p.xTap = xTap
    p.rowp = p.rowp


    with ctx:
        b.op("sync", lambda e: e.dma_start(p.ident[:, :], ident_h), writes=[p.ident], dma=True)
        b.op("sync", lambda e: e.dma_start(p.identf[:, :], identf_h), writes=[p.identf], dma=True)
        b.op("vector", lambda e: e.memset(p.ones[:, :], 1.0), writes=[p.ones])
        b.op("vector", lambda e: e.memset(p.onesf[:, :], 1.0), writes=[p.onesf])

        xio = p.arena.alloc(4)
        xio2 = p.arena.alloc(4)
        for i in range(T // 128):
            buf = xio if i % 2 == 0 else xio2
            b.op("sync", lambda e, i=i, buf=buf: e.dma_start(
                buf.f32(0, 0, D), x_in[i * 128:(i + 1) * 128, :]),
                writes=buf.keys(), dma=True)
            for f in range(NT):
                b.op("tensor", lambda e, i=i, f=f, buf=buf: e.transpose(
                    p.pb[6][:, 0:128], buf.f32(0, f * 128, 128), p.identf[:, :]),
                    reads=buf.keys() + [p.identf], writes=[(p.pb[6], None)])
                b.op("vector", lambda e, i=i, f=f: e.tensor_copy(
                    xTap(f, 1, i * 128, 128), p.pb[6][:, 0:128]),
                    reads=[(p.pb[6], None)], writes=[(p.xT, f)])
        p.arena.free(xio)
        p.arena.free(xio2)

        do_l0a = upto in ("l0a", "l0f", "all", "l1f_test")
        if upto != "m0":
            emit_staging_setup(b, p, ncores)
            p.stg["passA"]()
        if upto != "m0":
            p.stg["scales"]()
            p.stg["quant"](["qkv"])
        if do_l0a:
            dbg = emit_l0_attn_head(b, p)
            emit_l0_attn_quant(b, p)
        if upto != "m0":
            p.stg["quant"](["ow", "up0", "dn0"])
            p.stg["quant"](["wq", "wk", "wv", "wo"])
            p.stg["free"](nw=True)
        if do_l0a:
            emit_l0_attn_waves(b, p)
            emit_l0_attn_rest(b, p, dbg)
            p.dbg = dbg
            tapx(b, p, "x_l0a")
        if upto != "m0":
            p.stg["realloc"]()
            p.stg["quant"](["g8", "u8", "d8"])
            p.stg["free"]()
        if upto in ("l0f", "all", "l1f_test"):
            emit_l0_ffn(b, p)
        if upto != "m0":
            p.stg["realloc"]()
            p.stg["quant"](["ig", "iu", "id8"])
            p.stg["free"]()
        if upto in ("all",):
            emit_l1_attn(b, p, ncores)
        if upto in ("l1f_test", "all"):
            emit_l1_ffn(b, p)

        xo = p.arena.alloc(4)
        for i in range(T // 128):
            for f in range(NT):
                b.op("tensor", lambda e, i=i, f=f: e.transpose(
                    p.pbtr[:, 0:128], xTap(f, 1, i * 128, 128), p.ident[:, :]),
                    reads=[(p.xT, f), p.ident], writes=[(p.pb[7], None)])
                b.op("vector", lambda e, f=f: e.tensor_copy(
                    xo.f32(0, f * 128, 128), p.pbtr[:, 0:128]),
                    reads=[(p.pb[7], None)], writes=xo.keys())
            b.op("sync", lambda e, i=i, xo=xo: e.dma_start(
                out[i * 128:(i + 1) * 128, :], xo.f32(0, 0, D)),
                reads=xo.keys(), dma=True)
        p.arena.free(xo)
        for eng in ENGINES:
            b.drain_dma(eng, eng)
        from concourse.aot_env import allow_ici_all_to_all
        with allow_ici_all_to_all():
            b.flush()
    return nc


def input_names(nc):
    names = set()
    for a in nc.m.functions[0].allocations:
        if isinstance(a, mybir.MemoryLocationSet) and a.kind == "ExternalInput":
            nm = a.memorylocations[0].name
            if nm != "partition_id":
                names.add(nm)
    return names


def kernel(**inputs):
    ncores, T = 8, 1024
    x = np.ascontiguousarray(np.asarray(inputs["L0_attn_x"], dtype=np.float32).reshape(S, D))
    nc = build_program(ncores=ncores, T=T)
    names = input_names(nc)
    in_maps = []
    for c in range(ncores):
        m = host_inmap(inputs, x, c, ncores, T)
        m = {k: v for k, v in m.items() if k in names}
        missing = names - set(m)
        assert not missing, f"missing inputs: {missing}"
        in_maps.append(m)
    res = run_bass_kernel_spmd(nc, in_maps, core_ids=list(range(ncores)))
    out = np.concatenate([np.asarray(r["out"]) for r in res.results], axis=0)
    return out.reshape(1, S, D)


def rowap(p, slot, g0=0, gn=1024):
    pb_ = 32 * (slot // 3)
    off = 1024 * (slot % 3)
    return p.rowp[pb_:pb_ + 1, off + g0:off + g0 + gn]


def rkey(p, slot):
    return (p.rowp, ("row", slot))


def row_from_col(b, p, col_ap, ncol, slot, rdkeys):
    b.op("tensor", lambda e: e.transpose(p.pb[6][0:ncol, 0:128], col_ap, p.identf[:, :]),
         reads=rdkeys + [p.identf], writes=[(p.pb[6], None)])
    b.op("vector", lambda e: e.tensor_copy(p.tr8[0:ncol, 0:128], p.pb[6][0:ncol, 0:128]),
         reads=[(p.pb[6], None)], writes=[p.tr8])
    pb_ = 32 * (slot // 3)
    off = 1024 * (slot % 3)
    dst = p.rowp[pb_:pb_ + 1, off:off + ncol * 128]
    dst = AP(dst.tensor, dst.offset, [dst.ap[0], [128, ncol], [1, 128]])
    b.op("scalar", lambda e: e.dma_start(dst, p.tr8[0:ncol, 0:128]),
         reads=[p.tr8], writes=[rkey(p, slot)], dma=True)


def rop(b, p, eng, fn, rslots, wslot):
    outs = rowap(p, wslot)
    ins = [rowap(p, s) for s in rslots]
    b.op(eng, lambda e: fn(e, outs, *ins),
         reads=[rkey(p, s) for s in rslots], writes=[rkey(p, wslot)])


def bcast_row(b, p, slot, nt, dsts, t0=0, tok_off=0):
    pb_ = 32 * (slot // 3)
    for g0 in range(0, nt, 512):
        gn = min(512, nt - g0)
        b.op("tensor", lambda e, g0=g0, gn=gn: e.matmul(
            p.pb[5][:, 0:gn], p.onesf[pb_:pb_ + 1, 0:128],
            rowap(p, slot, t0 + g0, gn),
            start=True, stop=True, skip_group_check=True),
            reads=[rkey(p, slot), p.onesf], writes=[(p.pb[5], None)])
        for apf, wkeys in dsts:
            b.op("vector", lambda e, apf=apf, g0=g0, gn=gn: e.tensor_copy(
                apf(g0 + tok_off, gn), p.pb[5][:, 0:gn]),
                reads=[(p.pb[5], None)], writes=list(wkeys))


def colmax_abs(b, p, apfn, nch, slot, scratch, rdkeys, eng2="vector", T=None):
    T = T or p.T
    t_ = scratch.ar.t
    off0 = scratch.base * 1024
    mx = t_[0:128, off0:off0 + T]
    mn = t_[0:128, off0 + 1024:off0 + 1024 + T]
    aps = [apfn(c) for c in range(nch)]
    kmx, kmn = [scratch.keys(0, 1)[0]], [scratch.keys(1, 1)[0]]
    b.op("vector", lambda e, a0=aps[0], a1=aps[1]: e.tensor_max(mx, a0, a1),
         reads=rdkeys, writes=kmx)
    b.op(eng2, lambda e, a0=aps[0], a1=aps[1]: e.tensor_tensor(mn, a0, a1, AL.min),
         reads=rdkeys, writes=kmn)
    for c in range(2, nch):
        b.op("vector", lambda e, a=aps[c]: e.tensor_max(mx, mx, a),
             reads=rdkeys + kmx, writes=kmx)
        b.op(eng2, lambda e, a=aps[c]: e.tensor_tensor(mn, mn, a, AL.min),
             reads=rdkeys + kmn, writes=kmn)
    b.op(eng2, lambda e: e.tensor_scalar(mn, mn, -1.0, None, AL.mult), reads=kmn, writes=kmn)
    b.op("vector", lambda e: e.tensor_max(mx, mx, mn), reads=kmx + kmn, writes=kmx)
    ntt = T // 128
    for tt in range(ntt):
        b.op("tensor", lambda e, tt=tt: e.transpose(
            p.pbtr[0:128, tt * 128:(tt + 1) * 128],
            t_[0:128, off0 + tt * 128:off0 + (tt + 1) * 128], p.ident[:, :]),
            reads=kmx + [p.ident], writes=[(p.pb[7], None)], inc=(tt == ntt - 1))
    b.op("vector", lambda e: e.reduce_max(
        p.mcol[:, 0:ntt], AP(p.pbtr, 0, [[1024, 128], [128, ntt], [1, 128]]), axis=AX),
        reads=[(p.pb[7], None)], writes=[p.mcol])
    row_from_col(b, p, p.mcol[:, 0:ntt], ntt, slot, [(p.mcol, None)])


def sumrow(b, p, apfn, nch, slot, rdkeys, square=False, sq_scratch=None, T=None):
    T = T or p.T
    pb_ = 32 * (slot // 3)
    off = 1024 * (slot % 3)
    for g0 in range(0, T, 512):
        for c in range(nch):
            if square:
                sl = sq_scratch.bf(c % 2, 0, 512)
                b.op("scalar", lambda e, sl=sl, a=apfn(c, g0, 512): e.activation(
                    sl, a, AF.Square),
                    reads=rdkeys, writes=[sq_scratch.keys(c % 2, 1)[0]])
                rhs, rks = sl, [sq_scratch.keys(c % 2, 1)[0]]
            else:
                rhs, rks = apfn(c, g0, 512), rdkeys
            b.op("tensor", lambda e, rhs=rhs, c=c: e.matmul(
                p.pb[6][:, 0:512], p.ones[:, 0:128], rhs,
                start=(c == 0), stop=(c == nch - 1), skip_group_check=True),
                reads=rks + [p.ones], writes=[(p.pb[6], None)])
        b.op("vector", lambda e, g0=g0: e.tensor_copy(
            p.rowp[pb_:pb_ + 1, off + g0:off + g0 + 512], p.pb[6][0:1, 0:512]),
            reads=[(p.pb[6], None)], writes=[rkey(p, slot)])


def quant_int8_inplace(b, p, reg, nch, mult_bc, mult_keys, tmp, T=None, c0=0):
    T = T or p.T
    for ci in range(nch):
        c = c0 + ci
        tf = tmp.f32(2 * (ci % 2), 0, T)
        tk = tmp.keys(2 * (ci % 2), 2)
        b.op("vector", lambda e, c=c, tf=tf: e.tensor_tensor(
            tf, reg.bf(c, 0, T), mult_bc, AL.mult),
            reads=reg.keys(c, 1) + list(mult_keys), writes=tk)
        b.op("vector", lambda e, c=c, tf=tf: e.tensor_scalar(
            reg.bf(c, 0, T), tf, MAGIC, -MAGIC, AL.add, AL.add),
            reads=tk, writes=reg.keys(c, 1))


def proj(b, p, wt, CIN, DOUT, xr, evac, dout_ranges=None, slab=512, banks=(0, 1, 2, 3),
         wt_key=None):
    T = p.T
    ncin = CIN // 128
    slabslots = cdiv(ncin * slab, 1024)
    wbuf = [p.arena.alloc(slabslots), p.arena.alloc(slabslots)]
    ranges = dout_ranges or [(0, DOUT)]
    si = 0
    bi = 0
    ngr = T // 512
    for (r0, r1) in ranges:
        for d0 in range(r0, r1, slab):
            sl = min(slab, r1 - d0)
            reg = wbuf[si % 2]
            si += 1
            src = AP(wt.tensor, d0, [[DOUT, 128], [128 * DOUT, ncin], [1, sl]])
            dst = AP(reg.ar.t, reg.base * 1024,
                     [[reg.ar.nslots * 1024, 128], [sl, ncin], [1, sl]])
            b.op("sync", lambda e, src=src, dst=dst: e.dma_start(dst, src),
                 reads=[wt if wt_key is None else wt_key], writes=reg.keys(), dma=True)
            for dc in range(sl // 128):
                bks = [banks[(bi + g) % len(banks)] for g in range(ngr)]
                bi += ngr
                for c in range(ncin):
                    wap = AP(reg.ar.t, reg.base * 1024 + c * sl + dc * 128,
                             [[reg.ar.nslots * 1024, 128], [1, 128]])
                    for g in range(ngr):
                        xap, xkeys = xr(c, g * 512)
                        b.op("tensor", lambda e, wap=wap, xap=xap, c=c, g=g, bk=bks[g]: e.matmul(
                            p.pb[bk][:, 0:512], wap, xap,
                            start=(c == 0), stop=(c == ncin - 1)),
                            reads=reg.keys() + xkeys, writes=[(p.pb[bks[g]], None)],
                            inc=(c == ncin - 1))
                for g in range(ngr):
                    evac(bks[g], d0 + dc * 128, g * 512)
    for r in wbuf:
        p.arena.free(r)


def ln_rows(b, p, sl_sum, sl_sq, sl_m, sl_r, n, eps=1e-5):
    rop(b, p, "vector", lambda e, o, a: e.tensor_scalar(o, a, 1.0 / n, None, AL.mult),
        [sl_sum], sl_m)
    rop(b, p, "vector", lambda e, o, a: e.tensor_scalar(o, a, 1.0 / n, None, AL.mult),
        [sl_sq], sl_sq)
    assert sl_sum // 3 == sl_sq // 3
    rop(b, p, "vector", lambda e, o, m: e.tensor_tensor(o, m, m, AL.mult), [sl_m], sl_sum)
    rop(b, p, "vector", lambda e, o, sq, m2: e.tensor_sub(o, sq, m2), [sl_sq, sl_sum], sl_r)
    rop(b, p, "vector", lambda e, o, v: e.tensor_scalar(o, v, eps, None, AL.add), [sl_r], sl_r)
    rop(b, p, "scalar", lambda e, o, v: e.activation(o, v, AF.Sqrt), [sl_r], sl_r)
    rop(b, p, "vector", lambda e, o, sd: e.reciprocal(o, sd), [sl_r], sl_r)


def tapx(b, p, name):
    if not DEBUG_TAPS:
        return
    T, NT = p.T, 16
    d = b.nc.dram_tensor("dbg_" + name, [128, NT * T], BF16, kind="ExternalOutput").ap()
    for c in range(NT):
        b.op("gpsimd", lambda e, c=c: e.dma_start(
            AP(d.tensor, c * T, [[NT * T, 128], [1, T]]),
            AP(p.xT, c * T, [[NT * T, 128], [1, T]])),
            reads=[(p.xT, c)], dma=True)


def tap(b, p, name, reg, nslots, t0=0, nt=None):
    if not DEBUG_TAPS:
        return
    nt = nt if nt is not None else 1024
    d = b.nc.dram_tensor("dbg_" + name, [128, nslots * 1024], BF16, kind="ExternalOutput").ap()
    for c in range(nslots):
        b.op("gpsimd", lambda e, c=c: e.dma_start(
            AP(d.tensor, c * 1024, [[nslots * 1024, 128], [1, 1024]]),
            reg.bf(c, 0, 1024)), reads=[reg.keys(c, 1)[0]], dma=True)


def emit_l0_attn_head(b, p):
    nc, T, NT = b.nc, p.T, 16
    A = p.arena
    dbg = {}

    nwa = nc.dram_tensor("nw_attn", [D], F32, kind="ExternalInput").ap()
    p.nwa_c = p.sb("nwa_c", [128, 16], F32)
    b.op("scalar", lambda e: e.dma_start(
        AP(p.nwa_c, 0, [[16, 128], [1, 16]]),
        AP(nwa.tensor, 0, [[1, 128], [128, 16]]),
        allow_slow_non_contiguous=True), writes=[p.nwa_c], dma=True)

    cos0 = nc.dram_tensor("c_cos0", [128, T], BF16, kind="ExternalInput").ap()
    sin0 = nc.dram_tensor("c_sin0", [128, T], BF16, kind="ExternalInput").ap()
    rot0 = nc.dram_tensor("c_rot", [128, 128], BF16, kind="ExternalInput").ap()
    p.cos0 = p.sb("cos0", [128, T], BF16)
    p.sin0 = p.sb("sin0", [128, T], BF16)
    p.rotm = p.sb("rotm", [128, 128], BF16)
    for t, h in ((p.cos0, cos0), (p.sin0, sin0), (p.rotm, rot0)):
        b.op("scalar", lambda e, t=t, h=h: e.dma_start(t[:, :], h), writes=[t], dma=True)
    maskh = nc.dram_tensor("c_masks", [4, 128, 512], BF16, kind="ExternalInput").ap()
    p.maskt = p.sb("maskt", [128, 4 * 512], BF16)
    b.op("scalar", lambda e: e.dma_start(
        AP(p.maskt, 0, [[2048, 128], [512, 4], [1, 512]]),
        AP(maskh.tensor, 0, [[512, 128], [65536, 4], [1, 512]])),
        writes=[p.maskt], dma=True)

    SL_P0, SL_M1, SL_R1 = 0, 1, 2
    SL_S, SL_RRMS, SL_T2, SL_MULT, SL_L = 3, 4, 5, 6, 7
    sqs = A.alloc(2)
    xap = lambda c, g0, gn: AP(p.xT, c * T + g0, [[NT * T, 128], [1, gn]])
    xkeys = [(p.xT, f) for f in range(NT)]
    sumrow(b, p, xap, NT, SL_S, xkeys)
    sumrow(b, p, xap, NT, SL_T2, xkeys, square=True, sq_scratch=sqs)
    ln_rows(b, p, SL_S, SL_T2, SL_M1, SL_R1, D)

    ln0 = A.alloc(16)
    mbc = A.alloc(2)
    rbc = A.alloc(1)
    bcast_row(b, p, SL_M1, T, [(lambda g0, gn: mbc.f32(0, g0, gn), mbc.keys())])
    bcast_row(b, p, SL_R1, T, [(lambda g0, gn: rbc.bf(0, g0, gn), rbc.keys())])
    tmp4 = A.alloc(4)
    for c in range(NT):
        tf = tmp4.f32(2 * (c % 2), 0, T)
        tk = tmp4.keys(2 * (c % 2), 2)
        b.op("vector", lambda e, c=c, tf=tf: e.tensor_tensor(
            tf, xap(c, 0, T), mbc.f32(0, 0, T), AL.subtract),
            reads=[(p.xT, c)] + mbc.keys(), writes=tk)
        b.op("vector", lambda e, c=c, tf=tf: e.tensor_tensor(
            ln0.bf(c, 0, T), tf, rbc.bf(0, 0, T), AL.mult),
            reads=tk + rbc.keys(), writes=ln0.keys(c, 1))
    A.free(mbc)
    A.free(rbc)
    dbg["ln0"] = ln0
    tap(b, p, "ln0", ln0, 16)
    dbg["sqs"] = sqs
    dbg["ln0"] = ln0
    dbg["tmp4"] = tmp4
    p.l0dbg = dbg
    return dbg


def emit_l0_attn_quant(b, p):
    nc, T, NT = b.nc, p.T, 16
    A = p.arena
    dbg = p.l0dbg
    SL_P0, SL_M1, SL_R1 = 0, 1, 2
    SL_S, SL_RRMS, SL_T2, SL_MULT, SL_L = 3, 4, 5, 6, 7
    xap = lambda c, g0, gn: AP(p.xT, c * T + g0, [[NT * T, 128], [1, gn]])
    xkeys = [(p.xT, f) for f in range(NT)]
    sqs = dbg["sqs"]

    ln0 = dbg["ln0"]
    tmp4 = dbg["tmp4"]

    sumrow(b, p, lambda c, g0, gn: ln0.bf(c, g0, gn), NT, SL_S, ln0.keys(),
           square=True, sq_scratch=sqs)
    rop(b, p, "vector", lambda e, o, a: e.tensor_scalar(o, a, 1.0 / D, 1e-5, AL.mult, AL.add),
        [SL_S], SL_RRMS)
    rop(b, p, "scalar", lambda e, o, v: e.activation(o, v, AF.Sqrt), [SL_RRMS], SL_RRMS)
    rop(b, p, "vector", lambda e, o, v: e.reciprocal(o, v), [SL_RRMS], SL_RRMS)
    for c in range(NT):
        b.op("vector", lambda e, c=c: e.tensor_scalar(
            ln0.bf(c, 0, T), ln0.bf(c, 0, T), p.nwa_c[:, c:c + 1], None, AL.mult),
            reads=ln0.keys(c, 1) + [p.nwa_c], writes=ln0.keys(c, 1))
    hq = ln0
    dbg["hw"] = hq

    colmax_abs(b, p, lambda c: hq.bf(c, 0, T), NT, SL_T2, sqs, hq.keys())
    rop(b, p, "vector", lambda e, o, mx, rr: e.tensor_tensor(o, mx, rr, AL.mult),
        [SL_T2, SL_RRMS], SL_T2)
    rop(b, p, "vector", lambda e, o, a: e.tensor_scalar(o, a, 1e-5, None, AL.max),
        [SL_T2], SL_T2)
    rop(b, p, "vector", lambda e, o, a: e.reciprocal(o, a), [SL_T2], SL_S)
    rop(b, p, "vector", lambda e, o, a, rr: e.scalar_tensor_tensor(
        o, a, 127.0, rr, AL.mult, AL.mult), [SL_S, SL_RRMS], SL_MULT)
    mulbc = A.alloc(2)
    bcast_row(b, p, SL_MULT, T, [(lambda g0, gn: mulbc.f32(0, g0, gn), mulbc.keys())])
    quant_int8_inplace(b, p, hq, NT, mulbc.f32(0, 0, T), mulbc.keys(), tmp4)
    A.free(tmp4)
    A.free(mulbc)
    dbg["hq"] = hq
    tap(b, p, "hq", hq, 16)

    return dbg

def emit_l0_attn_waves(b, p):
    nc, T, NT = b.nc, p.T, 16
    A = p.arena
    dbg = p.l0dbg
    SL_P0, SL_M1, SL_R1 = 0, 1, 2
    SL_S, SL_RRMS, SL_T2, SL_MULT, SL_L = 3, 4, 5, 6, 7
    xap = lambda c, g0, gn: AP(p.xT, c * T + g0, [[NT * T, 128], [1, gn]])
    xkeys = [(p.xT, f) for f in range(NT)]
    sqs = dbg["sqs"]

    hq = dbg["hq"]
    b.op("vector", lambda e: e.tensor_copy(rowap(p, SL_P0), rowap(p, SL_T2)),
         reads=[rkey(p, SL_T2)], writes=[rkey(p, SL_P0)])
    b.op("vector", lambda e: e.tensor_scalar(
        rowap(p, SL_P0), rowap(p, SL_P0), p.swrow[0:1, 8:9], 1.0 / 127.0,
        AL.mult, AL.mult),
        reads=[rkey(p, SL_P0), (p.swrow, "m0")], writes=[rkey(p, SL_P0)])
    osbc = A.alloc(1)
    bcast_row(b, p, SL_P0, T, [(lambda g0, gn: osbc.bf(0, g0, gn), osbc.keys())])
    dbg["osbc"] = osbc
    ctx = A.alloc(16)
    qw_ = A.alloc(4)
    kw_ = A.alloc(4)
    vw_ = A.alloc(4)
    rotw = A.alloc(4)
    vtmp = A.alloc(1)
    pstrip = A.alloc(2)
    ibc = A.alloc(1)
    hkeys = hq.keys()

    def xr(c, g0):
        return hq.bf(c, g0, 512), [hq.keys(c, 1)[0]]

    for w in range(4):
        def qkv_evac(bank, dabs, g0, w=w):
            qkvi = dabs // 2048
            h = (dabs % 2048) // 128
            sl = h - w * 4
            if qkvi < 2:
                dst = (qw_ if qkvi == 0 else kw_)
                b.op("vector", lambda e, dst=dst, sl=sl, g0=g0, bank=bank: e.tensor_tensor(
                    dst.bf(sl, g0, 512), p.pb[bank][:, 0:512], osbc.bf(0, g0, 512), AL.mult),
                    reads=[(p.pb[bank], None)] + osbc.keys(), writes=dst.keys(sl, 1))
            else:
                b.op("vector", lambda e, g0=g0, bank=bank: e.tensor_tensor(
                    vtmp.bf(0, 0, 512), p.pb[bank][:, 0:512], osbc.bf(0, g0, 512), AL.mult),
                    reads=[(p.pb[bank], None)] + osbc.keys(), writes=vtmp.keys())
                for q4 in range(4):
                    b.op("tensor", lambda e, q4=q4: e.transpose(
                        p.pbtr[:, q4 * 128:(q4 + 1) * 128], vtmp.bf(0, q4 * 128, 128),
                        p.ident[:, :]),
                        reads=vtmp.keys() + [p.ident], writes=[(p.pb[7], None)],
                        inc=(q4 == 3))
                dst = AP(vw_.ar.t, (vw_.base) * 1024 + ((g0 // 128) * 4 + sl) * 128,
                         [[vw_.ar.nslots * 1024, 128], [512, 4], [1, 128]])
                b.op("vector", lambda e, dst=dst: e.tensor_copy(dst, p.pbtr[:, 0:512]),
                     reads=[(p.pb[7], None)], writes=vw_.keys())

        ranges = [(i * 2048 + w * 512, i * 2048 + (w + 1) * 512) for i in range(3)]
        proj(b, p, p.wt["qkv"], D, 6144, xr, qkv_evac, dout_ranges=ranges,
             slab=256, banks=(0, 1, 2, 3))

        for reg in (qw_, kw_):
            for sl in range(4):
                for g0 in (0, 512):
                    b.op("tensor", lambda e, reg=reg, sl=sl, g0=g0: e.matmul(
                        p.pb[4][:, 0:512], p.rotm[:, :], reg.bf(sl, g0, 512),
                        start=True, stop=True),
                        reads=reg.keys(sl, 1) + [p.rotm], writes=[(p.pb[4], None)])
                    b.op("vector", lambda e, reg=reg, sl=sl, g0=g0: e.tensor_copy(
                        rotw.bf(sl, g0, 512), p.pb[4][:, 0:512]),
                        reads=[(p.pb[4], None)], writes=rotw.keys(sl, 1))
            ap4 = lambda r, c0: AP(r.ar.t, r.base * 1024, [[r.ar.nslots * 1024, 128], [1024, 4], [1, 1024]])
            cosb = AP(p.cos0, 0, [[T, 128], [0, 4], [1, T]])
            sinb = AP(p.sin0, 0, [[T, 128], [0, 4], [1, T]])
            b.op("vector", lambda e, reg=reg: e.tensor_tensor(
                ap4(rotw, 0), ap4(rotw, 0), sinb, AL.mult),
                reads=rotw.keys() + [p.sin0], writes=rotw.keys())
            b.op("vector", lambda e, reg=reg: e.tensor_tensor(
                ap4(reg, 0), ap4(reg, 0), cosb, AL.mult),
                reads=reg.keys() + [p.cos0], writes=reg.keys())
            b.op("vector", lambda e, reg=reg: e.tensor_tensor(
                ap4(reg, 0), ap4(reg, 0), ap4(rotw, 0), AL.add),
                reads=reg.keys() + rotw.keys(), writes=reg.keys())

        if w == 0:
            tap(b, p, "qw0", qw_, 4)
            tap(b, p, "kw0", kw_, 4)
            tap(b, p, "vw0", vw_, 4)
        for hw2 in range(4):
            h = w * 4 + hw2
            for tg in range(2):
                g0 = tg * 512
                ktmax = (tg + 1) * 4

                def sc_emit(kt, hw2=hw2, g0=g0, tg=tg):
                    sb_ = kt % 2
                    diag = kt >= 4 * tg
                    b.op("tensor", lambda e, hw2=hw2, kt=kt, g0=g0, sb_=sb_, tg=tg: e.matmul(
                        p.pb[sb_][:, 0:512], kw_.bf(hw2, kt * 128, 128),
                        qw_.bf(hw2, g0, 512), start=True, stop=not (kt >= 4 * tg)),
                        reads=kw_.keys(hw2, 1) + qw_.keys(hw2, 1),
                        writes=[(p.pb[sb_], None)], inc=not diag)
                    if diag:
                        di = kt - 4 * tg
                        b.op("tensor", lambda e, di=di, sb_=sb_: e.matmul(
                            p.pb[sb_][:, 0:512], p.ident[:, :],
                            p.maskt[:, di * 512:(di + 1) * 512],
                            start=False, stop=True),
                            reads=[p.maskt, p.ident], writes=[(p.pb[sb_], None)])

                def ex_emit(kt):
                    sb_ = kt % 2
                    pst = pstrip.bf(kt % 2, 0, 512)
                    pkey = [pstrip.keys(kt % 2, 1)[0]]
                    b.op("scalar", lambda e, sb_=sb_, pst=pst: e.activation(
                        pst, p.pb[sb_][:, 0:512], AF.Exp, scale=float(1.0 / np.sqrt(128.0))),
                        reads=[(p.pb[sb_], None)], writes=pkey)

                def lav_emit(kt, hw2=hw2, ktmax=ktmax):
                    pst = pstrip.bf(kt % 2, 0, 512)
                    pkey = [pstrip.keys(kt % 2, 1)[0]]
                    b.op("tensor", lambda e, pst=pst, kt=kt, ktmax=ktmax: e.matmul(
                        p.pb[4][:, 0:512], p.ones[:, 0:128], pst,
                        start=(kt == 0), stop=(kt == ktmax - 1), skip_group_check=True),
                        reads=pkey + [p.ones], writes=[(p.pb[4], None)])
                    vap = AP(vw_.ar.t, vw_.base * 1024 + (kt * 4 + hw2) * 128,
                             [[vw_.ar.nslots * 1024, 128], [1, 128]])
                    b.op("tensor", lambda e, vap=vap, pst=pst, kt=kt, ktmax=ktmax: e.matmul(
                        p.pb[3][:, 0:512], vap, pst,
                        start=(kt == 0), stop=(kt == ktmax - 1)),
                        reads=vw_.keys() + pkey, writes=[(p.pb[3], None)])

                sc_emit(0)
                ex_emit(0)
                for kt in range(ktmax):
                    if kt + 1 < ktmax:
                        sc_emit(kt + 1)
                        ex_emit(kt + 1)
                    lav_emit(kt)
                b.op("vector", lambda e: e.reciprocal(ibc.f32(0, 0, 512), p.pb[4][:, 0:512]),
                     reads=[(p.pb[4], None)], writes=ibc.keys())
                b.op("vector", lambda e, h=h, g0=g0: e.tensor_tensor(
                    ctx.bf(h, g0, 512), p.pb[3][:, 0:512], ibc.f32(0, 0, 512), AL.mult),
                    reads=[(p.pb[3], None)] + ibc.keys(), writes=ctx.keys(h, 1))
    dbg["ctx"] = ctx
    tap(b, p, "ctx", ctx, 16)
    dbg["sqs"] = sqs
    dbg["osbc_qkv"] = osbc

    for r in (qw_, kw_, vw_, rotw, vtmp, pstrip, ibc, osbc):
        A.free(r)
    return dbg

def emit_l0_attn_rest(b, p, dbg):
    nc, T, NT = b.nc, p.T, 16
    A = p.arena
    SL_P0, SL_M1, SL_R1 = 0, 1, 2
    SL_S, SL_RRMS, SL_T2, SL_MULT, SL_L = 3, 4, 5, 6, 7
    ctx = dbg["ctx"]
    hq = dbg["hq"]
    sqs = dbg["sqs"]
    xap = lambda c, g0, gn: AP(p.xT, c * T + g0, [[NT * T, 128], [1, gn]])
    A.free(hq)

    colmax_abs(b, p, lambda c: ctx.bf(c, 0, T), NT, SL_T2, sqs, ctx.keys())
    rop(b, p, "vector", lambda e, o, a: e.tensor_scalar(o, a, 1e-5, None, AL.max),
        [SL_T2], SL_T2)
    rop(b, p, "vector", lambda e, o, a: e.reciprocal(o, a), [SL_T2], SL_MULT)
    rop(b, p, "vector", lambda e, o, a: e.tensor_scalar(o, a, 127.0, None, AL.mult),
        [SL_MULT], SL_MULT)
    b.op("vector", lambda e: e.tensor_copy(rowap(p, SL_P0), rowap(p, SL_T2)),
         reads=[rkey(p, SL_T2)], writes=[rkey(p, SL_P0)])
    b.op("vector", lambda e: e.tensor_scalar(
        rowap(p, SL_P0), rowap(p, SL_P0), p.swrow[0:1, 9:10], 1.0 / 127.0,
        AL.mult, AL.mult),
        reads=[rkey(p, SL_P0), (p.swrow, "m1")], writes=[rkey(p, SL_P0)])
    mulbc = A.alloc(2)
    osbc = A.alloc(1)
    tmp4 = A.alloc(4)
    bcast_row(b, p, SL_MULT, T, [(lambda g0, gn: mulbc.f32(0, g0, gn), mulbc.keys())])
    bcast_row(b, p, SL_P0, T, [(lambda g0, gn: osbc.bf(0, g0, gn), osbc.keys())])
    quant_int8_inplace(b, p, ctx, NT, mulbc.f32(0, 0, T), mulbc.keys(), tmp4)
    A.free(mulbc)
    A.free(tmp4)
    dbg["ctxq"] = ctx

    ln0 = A.alloc(16)
    mbc = A.alloc(2)
    rbc = A.alloc(1)
    bcast_row(b, p, SL_M1, T, [(lambda g0, gn: mbc.f32(0, g0, gn), mbc.keys())])
    bcast_row(b, p, SL_R1, T, [(lambda g0, gn: rbc.bf(0, g0, gn), rbc.keys())])
    t2s = A.alloc(4)
    for c in range(NT):
        tf = t2s.f32(2 * (c % 2), 0, T)
        tk = t2s.keys(2 * (c % 2), 2)
        b.op("vector", lambda e, c=c, tf=tf: e.tensor_tensor(
            tf, xap(c, 0, T), mbc.f32(0, 0, T), AL.subtract),
            reads=[(p.xT, c)] + mbc.keys(), writes=tk)
        b.op("vector", lambda e, c=c, tf=tf: e.tensor_tensor(
            ln0.bf(c, 0, T), tf, rbc.bf(0, 0, T), AL.mult),
            reads=tk + rbc.keys(), writes=ln0.keys(c, 1))
    A.free(mbc)
    A.free(rbc)
    A.free(t2s)

    x1 = A.alloc(16)
    sc2 = A.alloc(1)

    def o_evac(bank, dabs, g0):
        c = dabs // 128
        b.op("vector", lambda e, c=c, g0=g0, bank=bank: e.tensor_tensor(
            sc2.bf(0, 0, 512), p.pb[bank][:, 0:512], osbc.bf(0, g0, 512), AL.mult),
            reads=[(p.pb[bank], None)] + osbc.keys(), writes=sc2.keys())
        b.op("vector", lambda e, c=c, g0=g0: e.tensor_tensor(
            x1.bf(c, g0, 512), sc2.bf(0, 0, 512), ln0.bf(c, g0, 512), AL.add),
            reads=sc2.keys() + ln0.keys(c, 1), writes=x1.keys(c, 1))

    proj(b, p, p.wt["ow"], D, D,
         lambda c, g0: (ctx.bf(c, g0, 512), [ctx.keys(c, 1)[0]]),
         o_evac, slab=256, banks=(0, 1, 2, 3))
    A.free(ctx)
    A.free(ln0)
    A.free(osbc)
    A.free(sc2)
    dbg["x1"] = x1
    tap(b, p, "x1", x1, 16)

    if getattr(p, "stg_hook", None):
        p.stg_hook()
        p.stg_hook = None

    sumrow(b, p, lambda c, g0, gn: x1.bf(c, g0, gn), NT, SL_S, x1.keys(),
           square=True, sq_scratch=sqs)
    rop(b, p, "vector", lambda e, o, a: e.tensor_scalar(o, a, 1.0 / D, 1e-5, AL.mult, AL.add),
        [SL_S], SL_RRMS)
    rop(b, p, "scalar", lambda e, o, v: e.activation(o, v, AF.Sqrt), [SL_RRMS], SL_RRMS)
    rop(b, p, "vector", lambda e, o, v: e.reciprocal(o, v), [SL_RRMS], SL_RRMS)

    x2 = A.alloc(16)
    IN2 = 4096
    for th in range(2):
        t0 = th * 512
        r2bc = A.alloc(1)
        bcast_row(b, p, SL_RRMS, 512, [(lambda g0, gn: r2bc.bf(0, g0, gn), r2bc.keys())],
                  t0=t0)
        treg = A.alloc(16)
        def tka(ci, treg=treg):
            return (treg.ar.t, ("tch", treg.base, ci))
        def tapn(ci, g0=0, gn=512, treg=treg):
            return AP(treg.ar.t, treg.base * 1024 + ci * 512 + g0,
                      [[treg.ar.nslots * 1024, 128], [1, gn]])
        wg = [A.alloc(2), A.alloc(2)]
        wv2 = [A.alloc(2), A.alloc(2)]
        sg = A.alloc(1)
        sv = A.alloc(1)
        for di, d0 in enumerate(range(0, IN2, 128)):
            rg_ = wg[di % 2]
            rv_ = wv2[di % 2]
            for (reg, base) in ((rg_, d0), (rv_, IN2 + d0)):
                src = AP(p.wt["up0"].tensor, base, [[8192, 128], [128 * 8192, 16], [1, 128]])
                dst = AP(reg.ar.t, reg.base * 1024,
                         [[reg.ar.nslots * 1024, 128], [128, 16], [1, 128]])
                b.op("sync", lambda e, src=src, dst=dst: e.dma_start(dst, src),
                     reads=[p.wt["up0"]], writes=reg.keys(), dma=True)
            bkA, bkB = di % 2, 2 + di % 2
            for c in range(16):
                for (reg, bk) in ((rg_, bkA), (rv_, bkB)):
                    wap = AP(reg.ar.t, reg.base * 1024 + c * 128,
                             [[reg.ar.nslots * 1024, 128], [1, 128]])
                    b.op("tensor", lambda e, wap=wap, c=c, bk=bk, t0=t0: e.matmul(
                        p.pb[bk][:, 0:512], wap, x1.bf(c, t0, 512),
                        start=(c == 0), stop=(c == 15)),
                        reads=reg.keys() + [x1.keys(c, 1)[0]], writes=[(p.pb[bk], None)],
                        inc=(c == 15))
            b.op("vector", lambda e, bkA=bkA, sg=sg, r2bc=r2bc: e.tensor_tensor(
                sg.bf(0, 0, 512), p.pb[bkA][:, 0:512], r2bc.bf(0, 0, 512), AL.mult),
                reads=[(p.pb[bkA], None)] + r2bc.keys(), writes=sg.keys())
            b.op("scalar", lambda e, sg=sg: e.activation(
                sg.bf(0, 512, 512), sg.bf(0, 0, 512), AF.Silu),
                reads=sg.keys(), writes=sg.keys())
            b.op("vector", lambda e, bkB=bkB, sv=sv, r2bc=r2bc: e.tensor_tensor(
                sv.bf(0, 0, 512), p.pb[bkB][:, 0:512], r2bc.bf(0, 0, 512), AL.mult),
                reads=[(p.pb[bkB], None)] + r2bc.keys(), writes=sv.keys())
            b.op("vector", lambda e, di=di, sg=sg, sv=sv, tapn=tapn: e.tensor_tensor(
                tapn(di), sg.bf(0, 512, 512), sv.bf(0, 0, 512), AL.mult),
                reads=sg.keys() + sv.keys(), writes=[tka(di)])
        for r in wg + wv2 + [sg, sv]:
            A.free(r)

        colmax_abs(b, p, lambda ci: tapn(ci), 32, SL_T2, sqs,
                   [tka(ci) for ci in range(32)], T=512)
        rop(b, p, "vector", lambda e, o, a: e.tensor_scalar(o, a, 1e-5, None, AL.max),
            [SL_T2], SL_T2)
        rop(b, p, "vector", lambda e, o, a: e.reciprocal(o, a), [SL_T2], SL_MULT)
        rop(b, p, "vector", lambda e, o, a: e.tensor_scalar(o, a, 127.0, None, AL.mult),
            [SL_MULT], SL_MULT)
        b.op("vector", lambda e: e.tensor_copy(rowap(p, SL_P0, 0, 512),
                                               rowap(p, SL_T2, 0, 512)),
             reads=[rkey(p, SL_T2)], writes=[rkey(p, SL_P0)])
        b.op("vector", lambda e: e.tensor_scalar(
            rowap(p, SL_P0, 0, 512), rowap(p, SL_P0, 0, 512), p.swrow[0:1, 10:11],
            1.0 / 127.0, AL.mult, AL.mult),
            reads=[rkey(p, SL_P0), (p.swrow, "m2")], writes=[rkey(p, SL_P0)])
        mb2 = A.alloc(1)
        ob2 = A.alloc(1)
        bcast_row(b, p, SL_MULT, 512, [(lambda g0, gn: mb2.f32(0, g0, gn), mb2.keys())])
        bcast_row(b, p, SL_P0, 512, [(lambda g0, gn: ob2.bf(0, g0, gn), ob2.keys())])
        tq4 = A.alloc(2)
        for ci in range(32):
            tf = tq4.f32(ci % 2, 0, 512)
            tk2 = [tq4.keys(ci % 2, 1)[0]]
            b.op("vector", lambda e, ci=ci, tf=tf, tapn=tapn, mb2=mb2: e.tensor_tensor(
                tf, tapn(ci), mb2.f32(0, 0, 512), AL.mult),
                reads=[tka(ci)] + mb2.keys(), writes=tk2)
            b.op("vector", lambda e, ci=ci, tf=tf, tapn=tapn: e.tensor_scalar(
                tapn(ci), tf, MAGIC, -MAGIC, AL.add, AL.add),
                reads=tk2, writes=[tka(ci)])
        A.free(mb2)
        A.free(tq4)

        sc3 = A.alloc(1)

        def dn_evac(bank, dabs, g0, th=th, t0=t0, ob2=ob2, sc3=sc3):
            c = dabs // 128
            b.op("vector", lambda e, bank=bank: e.tensor_tensor(
                sc3.bf(0, 0, 512), p.pb[bank][:, 0:512], ob2.bf(0, 0, 512), AL.mult),
                reads=[(p.pb[bank], None)] + ob2.keys(), writes=sc3.keys())
            b.op("vector", lambda e, c=c: e.tensor_tensor(
                x2.bf(c, t0, 512), sc3.bf(0, 0, 512), x1.bf(c, t0, 512), AL.add),
                reads=sc3.keys() + x1.keys(c, 1), writes=[x2.keys(c, 1)[0]])

        projT(b, p, p.wt["dn0"], IN2, D,
              lambda c, g0: (tapn(c), [tka(c)]), dn_evac, 512, slab=128,
              banks=(0, 1, 2, 3))
        A.free(sc3)
        A.free(ob2)
        A.free(r2bc)
        A.free(treg)
    A.free(x1)
    dbg["x2"] = x2
    tap(b, p, "x2", x2, 16)

    sumrow(b, p, lambda c, g0, gn: x2.bf(c, g0, gn), NT, SL_S, x2.keys())
    sumrow(b, p, lambda c, g0, gn: x2.bf(c, g0, gn), NT, SL_T2, x2.keys(),
           square=True, sq_scratch=sqs)
    ln_rows(b, p, SL_S, SL_T2, SL_M1, SL_R1, D)
    mbc2 = A.alloc(2)
    rbc2 = A.alloc(1)
    bcast_row(b, p, SL_M1, T, [(lambda g0, gn: mbc2.f32(0, g0, gn), mbc2.keys())])
    bcast_row(b, p, SL_R1, T, [(lambda g0, gn: rbc2.bf(0, g0, gn), rbc2.keys())])
    t3s = A.alloc(4)
    for c in range(NT):
        tf = t3s.f32(2 * (c % 2), 0, T)
        tk3 = t3s.keys(2 * (c % 2), 2)
        b.op("vector", lambda e, c=c, tf=tf: e.tensor_tensor(
            tf, x2.bf(c, 0, T), mbc2.f32(0, 0, T), AL.subtract),
            reads=x2.keys(c, 1) + mbc2.keys(), writes=tk3)
        b.op("vector", lambda e, c=c, tf=tf: e.tensor_tensor(
            tf, tf, rbc2.bf(0, 0, T), AL.mult),
            reads=tk3 + rbc2.keys(), writes=tk3)
        b.op("vector", lambda e, c=c, tf=tf: e.tensor_tensor(
            xap(c, 0, T), xap(c, 0, T), tf, AL.add),
            reads=tk3 + [(p.xT, c)], writes=[(p.xT, c)])
    A.free(mbc2)
    A.free(rbc2)
    A.free(t3s)
    A.free(x2)
    A.free(sqs)


def projT(b, p, wt, CIN, DOUT, xr, evac, TT_, slab=128, banks=(0, 1, 2, 3)):
    ncin = CIN // 128
    slabslots = cdiv(ncin * slab, 1024)
    wbuf = [p.arena.alloc(slabslots), p.arena.alloc(slabslots)]
    si = 0
    bi = 0
    for d0 in range(0, DOUT, slab):
        sl = min(slab, DOUT - d0)
        reg = wbuf[si % 2]
        si += 1
        src = AP(wt.tensor, d0, [[DOUT, 128], [128 * DOUT, ncin], [1, sl]])
        dst = AP(reg.ar.t, reg.base * 1024,
                 [[reg.ar.nslots * 1024, 128], [sl, ncin], [1, sl]])
        b.op("sync", lambda e, src=src, dst=dst: e.dma_start(dst, src),
             reads=[wt], writes=reg.keys(), dma=True)
        for dc in range(sl // 128):
            bk = banks[bi % len(banks)]
            bi += 1
            for c in range(ncin):
                wap = AP(reg.ar.t, reg.base * 1024 + c * sl + dc * 128,
                         [[reg.ar.nslots * 1024, 128], [1, 128]])
                xap_, xkeys = xr(c, 0)
                b.op("tensor", lambda e, wap=wap, xap_=xap_, c=c, bk=bk: e.matmul(
                    p.pb[bk][:, 0:TT_], wap, xap_,
                    start=(c == 0), stop=(c == ncin - 1)),
                    reads=reg.keys() + xkeys, writes=[(p.pb[bk], None)],
                    inc=(c == ncin - 1))
            evac(bk, d0 + dc * 128, 0)
    for r in wbuf:
        p.arena.free(r)


def host_inmap(inputs, x, c, ncores, T):
    m = {"x": x[c * T:(c + 1) * T, :]}
    m.update(host_constants(c, ncores, T))
    m.update(host_shards(inputs, c, ncores))
    m["nw_attn"] = np.asarray(inputs["L0_attn_attn_norm_w"], dtype=np.float32)
    m["dri"] = np.asarray(inputs["L0_ffn_down_reorder_index"], dtype=np.int32)
    dirs = np.array([[sx_, sy_, sz_] for sx_ in (1.0, -1.0) for sy_ in (1.0, -1.0)
                     for sz_ in (1.0, -1.0)], dtype=np.float32) / np.sqrt(3.0)
    dv = np.zeros((4, 1), np.float32)
    dv[0:3, 0] = dirs[c]
    m["c_dirsh"] = np.repeat(dv.astype(ml_dtypes.bfloat16), 128, axis=1)
    m["c_hs"] = np.asarray(inputs["L1_attn_head_scales"], dtype=np.float32)[c].reshape(1, 1)
    m["s_g"] = np.asarray(inputs["L1_ffn_s_gate"], dtype=np.float32)
    m["s_u"] = np.asarray(inputs["L1_ffn_s_up"], dtype=np.float32)
    m["s_d"] = np.asarray(inputs["L1_ffn_s_down"], dtype=np.float32)
    csh = D // ncores
    m["sh_nw"] = np.asarray(inputs["L0_attn_ffn_norm_w"], dtype=np.float32)[
        None, c * csh:(c + 1) * csh]
    return m


def emit_outer_preln(b, p, out_reg, sqs):
    T, NT = p.T, 16
    A = p.arena
    SL_M1, SL_R1, SL_S, SL_T2 = 1, 2, 3, 5
    xap = lambda c, g0, gn: AP(p.xT, c * T + g0, [[NT * T, 128], [1, gn]])
    xkeys = [(p.xT, f) for f in range(NT)]
    sumrow(b, p, xap, NT, SL_S, xkeys)
    sumrow(b, p, xap, NT, SL_T2, xkeys, square=True, sq_scratch=sqs)
    ln_rows(b, p, SL_S, SL_T2, SL_M1, SL_R1, D)
    mbc = A.alloc(2)
    rbc = A.alloc(1)
    bcast_row(b, p, SL_M1, T, [(lambda g0, gn: mbc.f32(0, g0, gn), mbc.keys())])
    bcast_row(b, p, SL_R1, T, [(lambda g0, gn: rbc.bf(0, g0, gn), rbc.keys())])
    tmp4 = A.alloc(4)
    for c in range(NT):
        tf = tmp4.f32(2 * (c % 2), 0, T)
        tk = tmp4.keys(2 * (c % 2), 2)
        b.op("vector", lambda e, c=c, tf=tf, mbc=mbc: e.tensor_tensor(
            tf, xap(c, 0, T), mbc.f32(0, 0, T), AL.subtract),
            reads=[(p.xT, c)] + mbc.keys(), writes=tk)
        b.op("vector", lambda e, c=c, tf=tf, rbc=rbc, out_reg=out_reg: e.tensor_tensor(
            out_reg.bf(c, 0, T), tf, rbc.bf(0, 0, T), AL.mult),
            reads=tk + rbc.keys(), writes=out_reg.keys(c, 1))
    A.free(mbc)
    A.free(rbc)
    A.free(tmp4)


def emit_outer_postln(b, p, y_reg, sqs, nch=16):
    T, NT = p.T, 16
    A = p.arena
    SL_M1, SL_R1, SL_S, SL_T2 = 1, 2, 3, 5
    xap = lambda c, g0, gn: AP(p.xT, c * T + g0, [[NT * T, 128], [1, gn]])
    sumrow(b, p, lambda c, g0, gn: y_reg.bf(c, g0, gn), nch, SL_S, y_reg.keys())
    sumrow(b, p, lambda c, g0, gn: y_reg.bf(c, g0, gn), nch, SL_T2, y_reg.keys(),
           square=True, sq_scratch=sqs)
    ln_rows(b, p, SL_S, SL_T2, SL_M1, SL_R1, D)
    mbc = A.alloc(2)
    rbc = A.alloc(1)
    bcast_row(b, p, SL_M1, T, [(lambda g0, gn: mbc.f32(0, g0, gn), mbc.keys())])
    bcast_row(b, p, SL_R1, T, [(lambda g0, gn: rbc.bf(0, g0, gn), rbc.keys())])
    t3s = A.alloc(4)
    for c in range(NT):
        tf = t3s.f32(2 * (c % 2), 0, T)
        tk3 = t3s.keys(2 * (c % 2), 2)
        b.op("vector", lambda e, c=c, tf=tf, mbc=mbc, y_reg=y_reg: e.tensor_tensor(
            tf, y_reg.bf(c, 0, T), mbc.f32(0, 0, T), AL.subtract),
            reads=y_reg.keys(c, 1) + mbc.keys(), writes=tk3)
        b.op("vector", lambda e, c=c, tf=tf, rbc=rbc: e.tensor_tensor(
            tf, tf, rbc.bf(0, 0, T), AL.mult),
            reads=tk3 + rbc.keys(), writes=tk3)
        b.op("vector", lambda e, c=c, tf=tf: e.tensor_tensor(
            xap(c, 0, T), xap(c, 0, T), tf, AL.add),
            reads=tk3 + [(p.xT, c)], writes=[(p.xT, c)])
    A.free(mbc)
    A.free(rbc)
    A.free(t3s)


def emit_l0_ffn(b, p):
    nc, T, NT = b.nc, p.T, 16
    A = p.arena
    SL_P0, SL_M1, SL_R1 = 0, 1, 2
    SL_S, SL_RRMS, SL_T2, SL_MULT, SL_L = 3, 4, 5, 6, 7
    HID = 5632
    NH2 = HID // 128
    sqs = A.alloc(2)

    z = A.alloc(16)
    emit_outer_preln(b, p, z, sqs)

    colmax_abs(b, p, lambda c: z.bf(c, 0, T), NT, SL_T2, sqs, z.keys())
    rop(b, p, "vector", lambda e, o, a: e.reciprocal(o, a), [SL_T2], SL_MULT)
    rop(b, p, "vector", lambda e, o, a: e.tensor_scalar(o, a, 224.0, None, AL.mult),
        [SL_MULT], SL_MULT)
    rop(b, p, "vector", lambda e, o, a: e.tensor_scalar(o, a, 1.0 / 50176.0, None, AL.mult),
        [SL_T2], SL_P0)
    i224 = A.alloc(2)
    bcast_row(b, p, SL_MULT, T, [(lambda g0, gn: i224.f32(0, g0, gn), i224.keys())])
    f8s = A.alloc(2)
    for c in range(NT):
        s8 = f8s.fp8(0, (c % 2) * 1024, 1024)
        k8 = [f8s.keys(c % 2, 1)[0]]
        b.op("vector", lambda e, c=c, s8=s8, i224=i224: e.tensor_tensor(
            s8, z.bf(c, 0, T), i224.f32(0, 0, T), AL.mult),
            reads=z.keys(c, 1) + i224.keys(), writes=k8)
        b.op("vector", lambda e, c=c, s8=s8: e.tensor_copy(z.bf(c, 0, T), s8),
            reads=k8, writes=z.keys(c, 1))
    A.free(i224)
    A.free(f8s)
    qx = z

    y = A.alloc(16)
    for th in range(2):
        t0 = th * 512
        treg = A.alloc(22)
        mr1 = A.alloc(1)
        bcast_row(b, p, SL_P0, 512, [(lambda g0, gn, mr1=mr1: mr1.f32(0, g0, gn), mr1.keys())], t0=t0)

        def tka(ci, treg=treg):
            return (treg.ar.t, treg.base + ci // 2)

        def tapn(ci, g0=0, gn=512, treg=treg):
            return AP(treg.ar.t, treg.base * 1024 + ci * 512 + g0,
                      [[treg.ar.nslots * 1024, 128], [1, gn]])

        wg = [A.alloc(2), A.alloc(2)]
        wu = [A.alloc(2), A.alloc(2)]
        sg = A.alloc(1)
        sv = A.alloc(1)
        for di in range(NH2):
            d0 = di * 128
            rg_ = wg[di % 2]
            ru_ = wu[di % 2]
            for (reg, wtn) in ((rg_, "g8"), (ru_, "u8")):
                src = AP(p.wt[wtn].tensor, d0, [[HID, 128], [128 * HID, 16], [1, 128]])
                dst = AP(reg.ar.t, reg.base * 1024,
                         [[reg.ar.nslots * 1024, 128], [128, 16], [1, 128]])
                b.op("sync", lambda e, src=src, dst=dst: e.dma_start(dst, src),
                     reads=[p.wt[wtn]], writes=reg.keys(), dma=True)
            bkA, bkB = di % 2, 2 + di % 2
            for c in range(16):
                for (reg, bk) in ((rg_, bkA), (ru_, bkB)):
                    wap = AP(reg.ar.t, reg.base * 1024 + c * 128,
                             [[reg.ar.nslots * 1024, 128], [1, 128]])
                    b.op("tensor", lambda e, wap=wap, c=c, bk=bk, t0=t0, reg=reg: e.matmul(
                        p.pb[bk][:, 0:512], wap, qx.bf(c, t0, 512),
                        start=(c == 0), stop=(c == 15)),
                        reads=reg.keys() + [qx.keys(c, 1)[0]], writes=[(p.pb[bk], None)],
                        inc=(c == 15))
            rc = (RMX_OFF_G["g8"] + di, RMX_OFF_G["u8"] + di)
            b.op("vector", lambda e, bkA=bkA, rc=rc, sg=sg, mr1=mr1: e.scalar_tensor_tensor(
                sg.bf(0, 0, 512), p.pb[bkA][:, 0:512], p.rmx[:, rc[0]:rc[0] + 1],
                mr1.f32(0, 0, 512), AL.mult, AL.mult),
                reads=[(p.pb[bkA], None), (p.rmx, "g")] + mr1.keys(), writes=sg.keys())
            b.op("scalar", lambda e, sg=sg: e.activation(
                sg.bf(0, 512, 512), sg.bf(0, 0, 512), AF.Silu),
                reads=sg.keys(), writes=sg.keys())
            b.op("vector", lambda e, bkB=bkB, rc=rc, sv=sv, mr1=mr1: e.scalar_tensor_tensor(
                sv.bf(0, 0, 512), p.pb[bkB][:, 0:512], p.rmx[:, rc[1]:rc[1] + 1],
                mr1.f32(0, 0, 512), AL.mult, AL.mult),
                reads=[(p.pb[bkB], None), (p.rmx, "g")] + mr1.keys(), writes=sv.keys())
            b.op("vector", lambda e, di=di, sg=sg, sv=sv, tapn=tapn: e.tensor_tensor(
                tapn(di), sg.bf(0, 512, 512), sv.bf(0, 0, 512), AL.mult),
                reads=sg.keys() + sv.keys(), writes=[tka(di)])
        for r in wg + wu + [sg, sv]:
            A.free(r)

        colmax_abs(b, p, lambda ci: tapn(ci), NH2, SL_T2, sqs,
                   [tka(ci) for ci in range(NH2)], T=512)
        rop(b, p, "vector", lambda e, o, a: e.reciprocal(
            rowap(p, SL_MULT, 0, 512), rowap(p, SL_T2, 0, 512)), [SL_T2], SL_MULT)
        rop(b, p, "vector", lambda e, o, a: e.tensor_scalar(
            rowap(p, SL_MULT, 0, 512), rowap(p, SL_MULT, 0, 512), 224.0, None, AL.mult),
            [SL_MULT], SL_MULT)
        rop(b, p, "vector", lambda e, o, a: e.tensor_scalar(
            rowap(p, SL_P0, 0, 512), rowap(p, SL_T2, 0, 512), 1.0 / 50176.0, None, AL.mult),
            [SL_T2], SL_P0)
        i224b = A.alloc(1)
        mr2 = A.alloc(1)
        bcast_row(b, p, SL_MULT, 512, [(lambda g0, gn, i224b=i224b: i224b.f32(0, g0, gn), i224b.keys())])
        bcast_row(b, p, SL_P0, 512, [(lambda g0, gn, mr2=mr2: mr2.f32(0, g0, gn), mr2.keys())])
        f8b = A.alloc(1)
        for ci in range(NH2):
            s8 = f8b.fp8(0, 0, 512)
            k8 = [f8b.keys()[0]]
            b.op("vector", lambda e, ci=ci, s8=s8, tapn=tapn, i224b=i224b: e.tensor_tensor(
                s8, tapn(ci), i224b.f32(0, 0, 512), AL.mult),
                reads=[tka(ci)] + i224b.keys(), writes=k8)
            b.op("vector", lambda e, ci=ci, s8=s8, tapn=tapn: e.tensor_copy(tapn(ci), s8),
                reads=k8, writes=[tka(ci)])
        A.free(i224b)
        A.free(f8b)

        def dn_evac(bank, dabs, g0, t0=t0, mr2=mr2):
            c = dabs // 128
            rc = RMX_OFF_G["d8"] + c // 8
            rcol = p.rmx[:, 88 + c:88 + c + 1]
            b.op("vector", lambda e, c=c, bank=bank, rcol=rcol, mr2=mr2, t0=t0: e.scalar_tensor_tensor(
                y.bf(c, t0, 512), p.pb[bank][:, 0:512], rcol, mr2.f32(0, 0, 512),
                AL.mult, AL.mult),
                reads=[(p.pb[bank], None), (p.rmx, "g")] + mr2.keys(),
                writes=[y.keys(c, 1)[0]])

        projT(b, p, p.wt["d8"], HID, D,
              lambda c, g0: (tapn(c), [tka(c)]), dn_evac, 512, slab=128,
              banks=(0, 1, 2, 3))
        A.free(mr2)
        A.free(mr1)
        A.free(treg)
    A.free(qx)

    emit_outer_postln(b, p, y, sqs)
    A.free(y)
    A.free(sqs)


RMX_OFF_G = {"g8": 0, "u8": 44, "d8": 88}


def emit_l1_ffn(b, p):
    nc, T, NT = b.nc, p.T, 16
    A = p.arena
    HID = 5632
    NH2 = HID // 128
    sqs = A.alloc(2)

    sg_h = nc.dram_tensor("s_g", [HID], F32, kind="ExternalInput").ap()
    su_h = nc.dram_tensor("s_u", [HID], F32, kind="ExternalInput").ap()
    sd_h = nc.dram_tensor("s_d", [D], F32, kind="ExternalInput").ap()
    p.sgcol = p.sb("sgcol", [128, 44], F32)
    p.sucol = p.sb("sucol", [128, 44], F32)
    p.sdcol = p.sb("sdcol", [128, 16], F32)
    for t, h, n in ((p.sgcol, sg_h, 44), (p.sucol, su_h, 44), (p.sdcol, sd_h, 16)):
        b.op("scalar", lambda e, t=t, h=h, n=n: e.dma_start(
            AP(t, 0, [[n, 128], [1, n]]),
            AP(h.tensor, 0, [[1, 128], [128, n]]),
            allow_slow_non_contiguous=True), writes=[t], dma=True)

    tapx(b, p, "l1f_x")
    z = A.alloc(16)
    emit_outer_preln(b, p, z, sqs)
    tap(b, p, "l1f_z", z, 16)

    y = A.alloc(16)
    for th in range(2):
        t0 = th * 512
        treg = A.alloc(22)

        def tka(ci, treg=treg):
            return (treg.ar.t, treg.base + ci // 2)

        def tapn(ci, g0=0, gn=512, treg=treg):
            return AP(treg.ar.t, treg.base * 1024 + ci * 512 + g0,
                      [[treg.ar.nslots * 1024, 128], [1, gn]])

        wg = [A.alloc(2), A.alloc(2)]
        wu = [A.alloc(2), A.alloc(2)]
        sg = A.alloc(1)
        sv = A.alloc(1)
        for di in range(NH2):
            d0 = di * 128
            rg_ = wg[di % 2]
            ru_ = wu[di % 2]
            for (reg, wtn) in ((rg_, "ig"), (ru_, "iu")):
                src = AP(p.wt[wtn].tensor, d0, [[HID, 128], [128 * HID, 16], [1, 128]])
                dst = AP(reg.ar.t, reg.base * 1024,
                         [[reg.ar.nslots * 1024, 128], [128, 16], [1, 128]])
                b.op("sync", lambda e, src=src, dst=dst: e.dma_start(dst, src),
                     reads=[p.wt[wtn]], writes=reg.keys(), dma=True)
            bkA, bkB = di % 2, 2 + di % 2
            for c in range(16):
                for (reg, bk) in ((rg_, bkA), (ru_, bkB)):
                    wap = AP(reg.ar.t, reg.base * 1024 + c * 128,
                             [[reg.ar.nslots * 1024, 128], [1, 128]])
                    b.op("tensor", lambda e, wap=wap, c=c, bk=bk, t0=t0, reg=reg: e.matmul(
                        p.pb[bk][:, 0:512], wap, z.bf(c, t0, 512),
                        start=(c == 0), stop=(c == 15)),
                        reads=reg.keys() + [z.keys(c, 1)[0]], writes=[(p.pb[bk], None)],
                        inc=(c == 15))
            b.op("vector", lambda e, bkA=bkA, di=di, sg=sg: e.tensor_scalar(
                sg.bf(0, 0, 512), p.pb[bkA][:, 0:512], p.sgcol[:, di:di + 1], None, AL.mult),
                reads=[(p.pb[bkA], None), p.sgcol], writes=sg.keys())
            b.op("scalar", lambda e, sg=sg: e.activation(
                sg.bf(0, 512, 512), sg.bf(0, 0, 512), AF.Silu),
                reads=sg.keys(), writes=sg.keys())
            b.op("vector", lambda e, bkB=bkB, di=di, sv=sv: e.tensor_scalar(
                sv.bf(0, 0, 512), p.pb[bkB][:, 0:512], p.sucol[:, di:di + 1], None, AL.mult),
                reads=[(p.pb[bkB], None), p.sucol], writes=sv.keys())
            b.op("vector", lambda e, di=di, sg=sg, sv=sv, tapn=tapn: e.tensor_tensor(
                tapn(di), sg.bf(0, 512, 512), sv.bf(0, 0, 512), AL.mult),
                reads=sg.keys() + sv.keys(), writes=[tka(di)])
        for r in wg + wu + [sg, sv]:
            A.free(r)

        def dn_evac(bank, dabs, g0, t0=t0):
            c = dabs // 128
            b.op("vector", lambda e, c=c, bank=bank, t0=t0: e.tensor_scalar(
                y.bf(c, t0, 512), p.pb[bank][:, 0:512], p.sdcol[:, c:c + 1], None, AL.mult),
                reads=[(p.pb[bank], None), p.sdcol], writes=[y.keys(c, 1)[0]])

        projT(b, p, p.wt["id8"], HID, D,
              lambda c, g0: (tapn(c), [tka(c)]), dn_evac, 512, slab=128,
              banks=(0, 1, 2, 3))
        if th == 1:
            tap(b, p, "l1f_t1", treg, 22)
        A.free(treg)
    A.free(z)
    tap(b, p, "l1f_y", y, 16)
    emit_outer_postln(b, p, y, sqs)
    A.free(y)
    A.free(sqs)


def emit_l1_attn(b, p, ncores):
    nc, T, NT = b.nc, p.T, 16
    A = p.arena
    SL_P0, SL_M1, SL_R1 = 0, 1, 2
    SL_S, SL_RRMS, SL_T2, SL_MULT, SL_L = 3, 4, 5, 6, 7
    rg = [list(range(ncores))]
    DH = 256
    NG = S // 512

    dirs_h = nc.dram_tensor("c_dirsh", [4, 128], BF16, kind="ExternalInput").ap()
    hs_h = nc.dram_tensor("c_hs", [1, 1], F32, kind="ExternalInput").ap()
    cos1 = nc.dram_tensor("c_cos1", [128, T], BF16, kind="ExternalInput").ap()
    sin1 = nc.dram_tensor("c_sin1", [128, T], BF16, kind="ExternalInput").ap()
    mask01 = nc.dram_tensor("c_mask01", [4, 128, 512], BF16, kind="ExternalInput").ap()
    p.dirsv = p.sb("dirsv", [4, 128], BF16)
    p.hsv = p.sb("hsv", [1, 1], F32)
    p.cos1 = p.sb("cos1", [128, T], BF16)
    p.sin1 = p.sb("sin1", [128, T], BF16)
    p.mask01t = p.sb("mask01t", [128, 4 * 512], BF16)
    for t, h in ((p.dirsv, dirs_h), (p.hsv, hs_h), (p.cos1, cos1), (p.sin1, sin1)):
        b.op("scalar", lambda e, t=t, h=h: e.dma_start(t[:, :] if t.shape[0] > 1 else t[0:1, :], h),
             writes=[t], dma=True)
    b.op("scalar", lambda e: e.dma_start(
        AP(p.mask01t, 0, [[2048, 128], [512, 4], [1, 512]]),
        AP(mask01.tensor, 0, [[512, 128], [65536, 4], [1, 512]])),
        writes=[p.mask01t], dma=True)

    a2aq_i = nc.dram_tensor("a2aq_i", [ncores, 2, 128, T], BF16).ap()
    a2aq_o = nc.dram_tensor("a2aq_o", [ncores, 2, 128, T], BF16).ap()
    KVSLOT = 4 * 128 * T
    a2akv_i = nc.dram_tensor("a2akv_i", [ncores, KVSLOT], BF16).ap()
    a2akv_o = nc.dram_tensor("a2akv_o", [ncores, KVSLOT], BF16).ap()
    a2ac_i = nc.dram_tensor("a2ac_i", [ncores, 2, 128, T], BF16).ap()
    a2ac_o = nc.dram_tensor("a2ac_o", [ncores, 2, 128, T], BF16).ap()

    sqs = A.alloc(2)
    tapx(b, p, "l1a_x")
    z = A.alloc(16)
    emit_outer_preln(b, p, z, sqs)
    tap(b, p, "l1a_z", z, 16)

    qA = A.alloc(8)
    qB = A.alloc(8)
    kA = A.alloc(2)
    kB = A.alloc(2)
    vreg = A.alloc(4)
    vtmp = A.alloc(1)

    def xr(c, g0):
        return z.bf(c, g0, 512), [z.keys(c, 1)[0]]

    def q_evac(bank, dabs, g0):
        dc = dabs // 128
        h2, half = dc // 2, dc % 2
        dst = qA if half == 0 else qB
        b.op("vector", lambda e, dst=dst, h2=h2, g0=g0, bank=bank: e.tensor_copy(
            dst.bf(h2, g0, 512), p.pb[bank][:, 0:512]),
            reads=[(p.pb[bank], None)], writes=dst.keys(h2, 1))

    proj(b, p, p.wt["wq"], D, D, xr, q_evac, slab=256, banks=(0, 1, 2, 3))

    def k_evac(bank, dabs, g0):
        dc = dabs // 128
        kv2, half = dc // 2, dc % 2
        dst = kA if half == 0 else kB
        b.op("vector", lambda e, dst=dst, kv2=kv2, g0=g0, bank=bank: e.tensor_copy(
            dst.bf(kv2, g0, 512), p.pb[bank][:, 0:512]),
            reads=[(p.pb[bank], None)], writes=dst.keys(kv2, 1))

    proj(b, p, p.wt["wk"], D, 512, xr, k_evac, slab=256, banks=(0, 1, 2, 3))

    def v_evac(bank, dabs, g0):
        dc = dabs // 128
        b.op("vector", lambda e, g0=g0, bank=bank: e.tensor_copy(
            vtmp.bf(0, 0, 512), p.pb[bank][:, 0:512]),
            reads=[(p.pb[bank], None)], writes=vtmp.keys())
        for q4 in range(4):
            b.op("tensor", lambda e, q4=q4: e.transpose(
                p.pbtr[:, q4 * 128:(q4 + 1) * 128], vtmp.bf(0, q4 * 128, 128),
                p.ident[:, :]),
                reads=vtmp.keys() + [p.ident], writes=[(p.pb[7], None)],
                inc=(q4 == 3))
        dst = AP(vreg.ar.t, vreg.base * 1024 + (g0 // 128) * 512 + dc * 128,
                 [[vreg.ar.nslots * 1024, 128], [512, 4], [1, 128]])
        b.op("vector", lambda e, dst=dst: e.tensor_copy(dst, p.pbtr[:, 0:512]),
             reads=[(p.pb[7], None)], writes=vreg.keys())

    proj(b, p, p.wt["wv"], D, 512, xr, v_evac, slab=256, banks=(0, 1, 2, 3))
    A.free(z)

    rtmp1 = A.alloc(8)
    rtmp2 = A.alloc(8)
    for (ra, rb, t1, t2, nh) in ((qA, qB, rtmp1, rtmp2, 8), (kA, kB, rtmp1, rtmp2, 2)):
        a3 = lambda r, nh=nh: AP(r.ar.t, r.base * 1024, [[r.ar.nslots * 1024, 128], [1024, nh], [1, T]])
        cosb = AP(p.cos1, 0, [[T, 128], [0, nh], [1, T]])
        sinb = AP(p.sin1, 0, [[T, 128], [0, nh], [1, T]])
        b.op("vector", lambda e, ra=ra, t1=t1, sinb=sinb, a3=a3: e.tensor_tensor(
            a3(t1), a3(ra), sinb, AL.mult), reads=ra.keys() + [p.sin1], writes=t1.keys())
        b.op("vector", lambda e, rb=rb, t2=t2, sinb=sinb, a3=a3: e.tensor_tensor(
            a3(t2), a3(rb), sinb, AL.mult), reads=rb.keys() + [p.sin1], writes=t2.keys())
        b.op("vector", lambda e, ra=ra, cosb=cosb, a3=a3: e.tensor_tensor(
            a3(ra), a3(ra), cosb, AL.mult), reads=ra.keys() + [p.cos1], writes=ra.keys())
        b.op("vector", lambda e, rb=rb, cosb=cosb, a3=a3: e.tensor_tensor(
            a3(rb), a3(rb), cosb, AL.mult), reads=rb.keys() + [p.cos1], writes=rb.keys())
        b.op("vector", lambda e, ra=ra, t2=t2, a3=a3: e.tensor_tensor(
            a3(ra), a3(ra), a3(t2), AL.subtract), reads=ra.keys() + t2.keys(), writes=ra.keys())
        b.op("vector", lambda e, rb=rb, t1=t1, a3=a3: e.tensor_tensor(
            a3(rb), a3(rb), a3(t1), AL.add), reads=rb.keys() + t1.keys(), writes=rb.keys())
    A.free(rtmp1)
    A.free(rtmp2)
    tap(b, p, "l1a_qA", qA, 8)
    tap(b, p, "l1a_qB", qB, 8)
    tap(b, p, "l1a_kA", kA, 2)
    tap(b, p, "l1a_kB", kB, 2)
    tap(b, p, "l1a_v", vreg, 4)

    for h2 in range(ncores):
        for (reg, half) in ((qA, 0), (qB, 1)):
            dst = AP(a2aq_i.tensor, (h2 * 2 + half) * 128 * T, [[T, 128], [1, T]])
            b.op("sync", lambda e, dst=dst, reg=reg, h2=h2: e.dma_start(dst, reg.bf(h2, 0, T)),
                 reads=reg.keys(h2, 1), writes=[(a2aq_i, h2 * 2 + half)], dma=True)
    for j in range(ncores):
        kv2 = j // 4
        for (reg, half) in ((kA, 0), (kB, 1)):
            dst = AP(a2akv_i.tensor, j * KVSLOT + half * 128 * T, [[T, 128], [1, T]])
            b.op("sync", lambda e, dst=dst, reg=reg, kv2=kv2: e.dma_start(dst, reg.bf(kv2, 0, T)),
                 reads=reg.keys(kv2, 1), writes=[(a2akv_i, (j, half))], dma=True)
        src = AP(vreg.ar.t, vreg.base * 1024 + kv2 * 256,
                 [[vreg.ar.nslots * 1024, 128], [512, 8], [1, 256]])
        dst = AP(a2akv_i.tensor, j * KVSLOT + 2 * 128 * T, [[2048, 128], [256, 8], [1, 256]])
        b.op("sync", lambda e, dst=dst, src=src: e.dma_start(dst, src),
             reads=vreg.keys(), writes=[(a2akv_i, (j, "v"))], dma=True)
    b.op("gpsimd", lambda e: e.collective_compute(
        "AllToAll", AL.bypass, replica_groups=rg, ins=[a2aq_i], outs=[a2aq_o]),
        reads=[(a2aq_i, i) for i in range(2 * ncores)], writes=[a2aq_o], tag="cc")
    b.op("gpsimd", lambda e: e.collective_compute(
        "AllToAll", AL.bypass, replica_groups=rg, ins=[a2akv_i], outs=[a2akv_o]),
        reads=[(a2akv_i, (j, x)) for j in range(ncores) for x in (0, 1, "v")],
        writes=[a2akv_o], tag="cc")
    for r in (qA, qB, kA, kB, vreg, vtmp):
        A.free(r)

    kAf = A.alloc(8)
    kBf = A.alloc(8)
    vf = A.alloc(16)
    for r in range(ncores):
        for (reg, half) in ((kAf, 0), (kBf, 1)):
            src = AP(a2akv_o.tensor, r * KVSLOT + half * 128 * T, [[T, 128], [1, T]])
            dst = AP(reg.ar.t, reg.base * 1024 + r * T, [[reg.ar.nslots * 1024, 128], [1, T]])
            b.op("sync", lambda e, dst=dst, src=src: e.dma_start(dst, src),
                 reads=[a2akv_o], writes=[(reg.ar.t, reg.base + r)], dma=True)
        src = AP(a2akv_o.tensor, r * KVSLOT + 2 * 128 * T, [[2048, 128], [256, 8], [1, 256]])
        dst = AP(vf.ar.t, vf.base * 1024 + r * 8 * 256, [[vf.ar.nslots * 1024, 128], [256, 8], [1, 256]])
        b.op("sync", lambda e, dst=dst, src=src: e.dma_start(dst, src),
             reads=[a2akv_o], writes=vf.keys(2 * r, 2), dma=True)
    kkeys = kAf.keys() + kBf.keys()
    tap(b, p, "l1a_kAf", kAf, 8)
    tap(b, p, "l1a_vf", vf, 16)

    p.kpcol = p.sb("kpcol", [128, 64], F32)
    NKT = S // 128
    for kt in range(NKT):
        b.op("tensor", lambda e, kt=kt: e.matmul(
            p.pb[4][:, kt % 32:kt % 32 + 1],
            AP(kAf.ar.t, kAf.base * 1024 + kt * 128, [[kAf.ar.nslots * 1024, 3], [1, 128]]),
            p.dirsv[0:3, 0:1], start=True, stop=True, skip_group_check=True),
            reads=kkeys + [p.dirsv], writes=[(p.pb[4], None)], inc=(kt % 32 == 31))
        if kt % 32 == 31:
            b.op("vector", lambda e, kt=kt: e.tensor_copy(
                p.kpcol[:, kt - 31 + 0:kt + 1], p.pb[4][:, 0:32]),
                reads=[(p.pb[4], None)], writes=[p.kpcol])
    p.scol = p.sb("scol", [128, 1], F32)
    b.op("tensor", lambda e: e.matmul(
        p.pb[4][:, 0:1], p.onesf[0:1, 0:128], p.hsv[0:1, 0:1],
        start=True, stop=True, skip_group_check=True),
        reads=[p.hsv, p.onesf], writes=[(p.pb[4], None)])
    b.op("vector", lambda e: e.tensor_copy(p.scol[:, :], p.pb[4][:, 0:1]),
         reads=[(p.pb[4], None)], writes=[p.scol])

    ctxA = A.alloc(8)
    ctxB = A.alloc(8)
    qg = A.alloc(2)
    pstrip = A.alloc(4)
    qpb = A.alloc(2)
    outm = A.alloc(1)
    ibc = A.alloc(2)
    SL_QP = 3
    for g in range(NG):
        r, half = g // 2, g % 2
        gs = g % 2
        for cchunk in range(2):
            src = AP(a2aq_o.tensor, (r * 2 + cchunk) * 128 * T + half * 512,
                     [[T, 128], [1, 512]])
            b.op("sync", lambda e, src=src, cchunk=cchunk, gs=gs: e.dma_start(
                qg.bf(gs, cchunk * 512, 512), src),
                reads=[a2aq_o], writes=[(qg.ar.t, (qg.base + gs, cchunk))], dma=True)
        qgk = [(qg.ar.t, (qg.base + gs, 0)), (qg.ar.t, (qg.base + gs, 1))]
        b.op("tensor", lambda e, gs=gs: e.matmul(
            p.pb[5][:, 0:512], p.dirsv[0:3, 0:128],
            AP(qg.ar.t, (qg.base + gs) * 1024, [[qg.ar.nslots * 1024, 3], [1, 512]]),
            start=True, stop=True, skip_group_check=True),
            reads=qgk + [p.dirsv], writes=[(p.pb[5], None)])
        b.op("vector", lambda e, gs=gs: e.tensor_copy(qpb.bf(gs, 0, 512), p.pb[5][:, 0:512]),
             reads=[(p.pb[5], None)], writes=qpb.keys(gs, 1))
        ktmax = 4 * (g + 1)
        for path in ("main", "geo"):
            NSL = 4

            SBANKS = (0, 1, 6, 7)

            def sc_emit(kt, gs=gs, ktmax=ktmax):
                sb_ = SBANKS[kt % 4]
                diag = kt >= ktmax - 4
                di = kt - (ktmax - 4)
                b.op("tensor", lambda e, kt=kt, sb_=sb_, diag=diag, gs=gs: e.matmul(
                    p.pb[sb_][:, 0:512],
                    AP(kAf.ar.t, kAf.base * 1024 + kt * 128, [[kAf.ar.nslots * 1024, 128], [1, 128]]),
                    qg.bf(gs, 0, 512), start=True, stop=False),
                    reads=kkeys + qgk, writes=[(p.pb[sb_], None)], inc=False)
                b.op("tensor", lambda e, kt=kt, sb_=sb_, diag=diag, gs=gs: e.matmul(
                    p.pb[sb_][:, 0:512],
                    AP(kBf.ar.t, kBf.base * 1024 + kt * 128, [[kBf.ar.nslots * 1024, 128], [1, 128]]),
                    qg.bf(gs, 512, 512), start=False, stop=not diag),
                    reads=kkeys + qgk, writes=[(p.pb[sb_], None)], inc=not diag)
                if diag:
                    b.op("tensor", lambda e, di=di, sb_=sb_: e.matmul(
                        p.pb[sb_][:, 0:512], p.ident[:, :],
                        p.maskt[:, di * 512:(di + 1) * 512],
                        start=False, stop=True),
                        reads=[p.maskt, p.ident], writes=[(p.pb[sb_], None)])

            def ex_emit(kt, path=path, gs=gs, ktmax=ktmax):
                sb_ = SBANKS[kt % 4]
                diag = kt >= ktmax - 4
                di = kt - (ktmax - 4)
                pst = pstrip.bf(kt % NSL, 0, 512)
                pkey = [pstrip.keys(kt % NSL, 1)[0]]
                if path == "main":
                    b.op("scalar", lambda e, sb_=sb_, pst=pst: e.activation(
                        pst, p.pb[sb_][:, 0:512], AF.Exp, scale=float(1.0 / 16.0)),
                        reads=[(p.pb[sb_], None)], writes=pkey)
                else:
                    b.op("scalar", lambda e, kt=kt, pst=pst, gs=gs: e.activation(
                        pst, qpb.bf(gs, 0, 512), AF.Exp, scale=p.kpcol[:, kt:kt + 1]),
                        reads=qpb.keys(gs, 1) + [p.kpcol], writes=pkey)
                    if diag:
                        b.op("vector", lambda e, di=di, pst=pst: e.tensor_tensor(
                            pst, pst, p.mask01t[:, di * 512:(di + 1) * 512], AL.mult),
                            reads=pkey + [p.mask01t], writes=pkey)

            def lav_emit(kt, ktmax=ktmax):
                pst = pstrip.bf(kt % NSL, 0, 512)
                pkey = [pstrip.keys(kt % NSL, 1)[0]]
                b.op("tensor", lambda e, pst=pst, kt=kt, ktmax=ktmax: e.matmul(
                    p.pb[4][:, 0:512], p.ones[:, 0:128], pst,
                    start=(kt == 0), stop=(kt == ktmax - 1), skip_group_check=True),
                    reads=pkey + [p.ones], writes=[(p.pb[4], None)])
                for hc in range(2):
                    vap = AP(vf.ar.t, vf.base * 1024 + kt * 256 + hc * 128,
                             [[vf.ar.nslots * 1024, 128], [1, 128]])
                    b.op("tensor", lambda e, vap=vap, pst=pst, kt=kt, ktmax=ktmax, hc=hc: e.matmul(
                        p.pb[2 + hc][:, 0:512], vap, pst,
                        start=(kt == 0), stop=(kt == ktmax - 1)),
                        reads=vf.keys() + pkey, writes=[(p.pb[2 + hc], None)])

            LA = 2
            if path == "main":
                for k0 in range(min(LA, ktmax)):
                    sc_emit(k0)
            for k0 in range(min(LA, ktmax)):
                ex_emit(k0)
            for kt in range(ktmax):
                nxt = kt + LA
                if nxt < ktmax:
                    if path == "main":
                        sc_emit(nxt)
                    ex_emit(nxt)
                lav_emit(kt)
            ps_ = 0 if path == "main" else 1
            b.op("vector", lambda e, ps_=ps_: e.reciprocal(
                ibc.f32(ps_, 0, 512), p.pb[4][:, 0:512]),
                reads=[(p.pb[4], None)], writes=ibc.keys(ps_, 1))
            for hc in range(2):
                if path == "main":
                    b.op("vector", lambda e, hc=hc, ps_=ps_: e.tensor_tensor(
                        outm.bf(0, hc * 512, 512), p.pb[2 + hc][:, 0:512],
                        ibc.f32(ps_, 0, 512), AL.mult),
                        reads=[(p.pb[2 + hc], None)] + ibc.keys(ps_, 1), writes=outm.keys())
                else:
                    dstc = (ctxA if hc == 0 else ctxB)
                    pk0 = [pstrip.keys(0, 1)[0]]
                    b.op("vector", lambda e, hc=hc, ps_=ps_: e.tensor_tensor(
                        pstrip.bf(0, hc * 512, 512), p.pb[2 + hc][:, 0:512],
                        ibc.f32(ps_, 0, 512), AL.mult),
                        reads=[(p.pb[2 + hc], None)] + ibc.keys(ps_, 1), writes=pk0)
                    b.op("vector", lambda e, hc=hc: e.tensor_tensor(
                        pstrip.bf(0, hc * 512, 512), pstrip.bf(0, hc * 512, 512),
                        outm.bf(0, hc * 512, 512), AL.subtract),
                        reads=pk0 + outm.keys(), writes=pk0)
                    b.op("vector", lambda e, hc=hc, dstc=dstc, g=g: e.scalar_tensor_tensor(
                        dstc.bf(g // 2, (g % 2) * 512, 512),
                        pstrip.bf(0, hc * 512, 512), p.scol[:, 0:1],
                        outm.bf(0, hc * 512, 512), AL.mult, AL.add),
                        reads=pk0 + outm.keys() + [p.scol],
                        writes=[(dstc.ar.t, dstc.base + g // 2)])
        if g % 2 == 1:
            c = g // 2
            for (reg, half) in ((ctxA, 0), (ctxB, 1)):
                srcx = AP(reg.ar.t, reg.base * 1024 + c * T, [[reg.ar.nslots * 1024, 128], [1, T]])
                dstx = AP(a2ac_i.tensor, (c * 2 + half) * 128 * T, [[T, 128], [1, T]])
                b.op("sync", lambda e, dstx=dstx, srcx=srcx: e.dma_start(dstx, srcx),
                     reads=[(reg.ar.t, reg.base + c)], writes=[(a2ac_i, (c, half))], dma=True)
    tap(b, p, "l1a_ctxA", ctxA, 8)
    tap(b, p, "l1a_ctxB", ctxB, 8)
    for r in (qg, pstrip, qpb, outm, ibc, kAf, kBf, vf):
        A.free(r)

    b.op("gpsimd", lambda e: e.collective_compute(
        "AllToAll", AL.bypass, replica_groups=rg, ins=[a2ac_i], outs=[a2ac_o]),
        reads=[(a2ac_i, (c, x)) for c in range(ncores) for x in (0, 1)],
        writes=[a2ac_o], tag="cc")
    A.free(ctxA)
    A.free(ctxB)

    ctxf = A.alloc(16)
    for r in range(ncores):
        for half in range(2):
            src = AP(a2ac_o.tensor, (r * 2 + half) * 128 * T, [[T, 128], [1, T]])
            b.op("sync", lambda e, src=src, r=r, half=half: e.dma_start(
                ctxf.bf(r * 2 + half, 0, T), src),
                reads=[a2ac_o], writes=ctxf.keys(r * 2 + half, 1), dma=True)
    y = A.alloc(16)

    def wo_evac(bank, dabs, g0):
        c = dabs // 128
        b.op("vector", lambda e, c=c, g0=g0, bank=bank: e.tensor_copy(
            y.bf(c, g0, 512), p.pb[bank][:, 0:512]),
            reads=[(p.pb[bank], None)], writes=[y.keys(c, 1)[0]])

    tap(b, p, "l1a_ctxf", ctxf, 16)
    proj(b, p, p.wt["wo"], D, D,
         lambda c, g0: (ctxf.bf(c, g0, 512), [ctxf.keys(c, 1)[0]]),
         wo_evac, slab=256, banks=(0, 1, 2, 3))
    tap(b, p, "l1a_y", y, 16)
    A.free(ctxf)
    emit_outer_postln(b, p, y, sqs)
    A.free(y)
    A.free(sqs)


# version 26
# speedup vs baseline: 1.0264x; 1.0264x over previous
import numpy as np
import ml_dtypes
import concourse.bass as bass
import concourse.mybir as mybir
from concourse.bass_utils import run_bass_kernel_spmd

F32 = mybir.dt.float32
BF16 = mybir.dt.bfloat16
FP8 = mybir.dt.float8e4
I32 = mybir.dt.int32
AL = mybir.AluOpType
AF = mybir.ActivationFunctionType
AX = mybir.AxisListType.X
AP = bass.AP

S = 8192
D = 2048
MAGIC = float(np.float32(1.5 * 2 ** 23))

ENGINES = ("sync", "scalar", "vector", "tensor", "gpsimd")
DEBUG_TAPS = False


class Builder:

    NDMA = 12

    def __init__(self, nc):
        self.nc = nc
        self.q = {e: [] for e in ENGINES}
        self.sem_count = {}
        self.watermark = {}
        self.buf = {}
        self.sems = {}
        self.dma_rot = {e: 0 for e in ENGINES}
        self.pending_reads = {e: [] for e in ENGINES}
        self.pending_writes = {e: [] for e in ENGINES}
        self.ctx = None
        for e in ENGINES:
            self.sem_count["c_" + e] = 0
            for k in range(self.NDMA):
                self.sem_count[f"d_{e}_{k}"] = 0

    def _need_wait(self, eng, tok):
        if tok is None:
            return False
        sem, val = tok
        return self.watermark.get((eng, sem), 0) < val

    def _mark_wait(self, eng, tok):
        sem, val = tok
        if self.watermark.get((eng, sem), 0) < val:
            self.watermark[(eng, sem)] = val

    @staticmethod
    def _key(x, key=None):
        if isinstance(x, tuple):
            x, key = x
        t = getattr(x, "tensor", x)
        return (t.name, key)

    def op(self, eng, fn, reads=(), writes=(), dma=False, inc=True, tag=None):
        wmax = {}

        def addw(tok):
            if self._need_wait(eng, tok):
                wmax[tok[0]] = max(wmax.get(tok[0], 0), tok[1])

        for r in reads:
            k = self._key(r)
            st = self.buf.get(k)
            if st and st["w"]:
                addw(st["w"])
        wkeys = []
        for w in writes:
            k = self._key(w)
            wkeys.append(k)
            st = self.buf.get(k)
            if st:
                if st["w"]:
                    addw(st["w"])
                for sem2, val2 in st["r"].items():
                    addw((sem2, val2))
        if dma:
            kk = self.dma_rot[eng] % self.NDMA
            self.dma_rot[eng] += 1
            sem = f"d_{eng}_{kk}"
            incv = 16
            addw((sem, self.sem_count[sem]))
        elif tag == "cc":
            self.ncc = getattr(self, "ncc", 0) + 1
            sem = f"cc{self.ncc}"
            self.sem_count.setdefault(sem, 0)
            incv = 1
        else:
            sem = "c_" + eng
            incv = 1
        waits = sorted(wmax.items())
        for tokw in waits:
            self._mark_wait(eng, tokw)
        if inc:
            self.sem_count[sem] += incv
            tok = (sem, self.sem_count[sem])
        else:
            tok = None
        b = self

        def emit(e, fn=fn, waits=tuple(waits), sem=sem, incv=incv, inc=inc):
            for s, v in waits:
                e.wait_ge(b.sems[s], v)
            inst = fn(e)
            if inc:
                inst.then_inc(b.sems[sem], incv)
            return inst

        self.q[eng].append(emit)
        if inc:
            rl = list(reads) + self.pending_reads[eng]
            wl = wkeys + self.pending_writes[eng]
            self.pending_reads[eng] = []
            self.pending_writes[eng] = []
            for r in rl:
                k = self._key(r)
                st = self.buf.setdefault(k, {"w": None, "r": {}})
                st["r"][tok[0]] = max(st["r"].get(tok[0], 0), tok[1])
            for k in wl:
                self.buf[k] = {"w": tok, "r": {}}
        else:
            self.pending_reads[eng] += list(reads)
            self.pending_writes[eng] += wkeys
        return tok

    def wait(self, eng, tok):
        if not self._need_wait(eng, tok):
            return
        self._mark_wait(eng, tok)
        b = self

        def emit(e, tok=tok):
            e.wait_ge(b.sems[tok[0]], tok[1])
        self.q[eng].append(emit)

    def drain_dma(self, wait_eng, issue_eng):
        for k in range(self.NDMA):
            sem = f"d_{issue_eng}_{k}"
            v = self.sem_count[sem]
            if v:
                self.wait(wait_eng, (sem, v))

    def tok_of(self, x, key=None):
        st = self.buf.get(self._key(x, key))
        return st["w"] if st else None

    def flush(self):
        nc = self.nc
        for name in list(self.sem_count):
            self.sems[name] = self.ctx.enter_context(nc.semaphore(name))
        with nc.Block() as block:
            for ename in ENGINES:
                queue = self.q[ename]
                section = getattr(block, ename)

                def body(e, queue=queue):
                    for f in queue:
                        f(e)
                section(body)


def cdiv(a, b):
    return (a + b - 1) // b


WSPECS = [
    ("qkv", "L0_attn_qkv_w", "tern", 6144, 2048),
    ("ow",  "L0_attn_o_w", "tern", 2048, 2048),
    ("dn0", "L0_attn_ffn_down_w", "tern", 2048, 4096),
    ("up0", "L0_attn_ffn_up_w", "cast", 8192, 2048),
    ("g8",  "L0_ffn_w_gate", "fp8", 5632, 2048),
    ("u8",  "L0_ffn_w_up", "fp8", 5632, 2048),
    ("d8",  "L0_ffn_w_down", "fp8", 2048, 5632),
    ("wq",  "L1_attn_Wq", "rowcast", 2048, 2048),
    ("wk",  "L1_attn_Wk", "rowcast", 2048, 512),
    ("wv",  "L1_attn_Wv", "rowcast", 2048, 512),
    ("wo",  "L1_attn_Wo", "rowcast", 2048, 2048),
    ("ig",  "L1_ffn_wq_gate", "int8", 5632, 2048),
    ("iu",  "L1_ffn_wq_up", "int8", 5632, 2048),
    ("id8", "L1_ffn_wq_down", "int8", 2048, 5632),
]
TERN = [w for w in WSPECS if w[2] == "tern"]
FP8W = [w for w in WSPECS if w[2] == "fp8"]


def host_constants(core, ncores, T=1024):
    bf = ml_dtypes.bfloat16
    c = {}
    c["c_ident"] = np.eye(128, dtype=bf)
    c["c_identf"] = np.eye(128, dtype=np.float32)
    c["c_ones"] = np.ones((128, 8), dtype=bf)
    masks = np.zeros((4, 128, 512), dtype=np.float32)
    for di, delta in enumerate((0, 128, 256, 384)):
        k = np.arange(128)[:, None] + delta
        q = np.arange(512)[None, :]
        masks[di] = np.where(k > q, -1e30, 0.0)
    c["c_masks"] = masks.astype(bf)
    m01 = np.zeros((4, 128, 512), dtype=np.float32)
    for di, delta in enumerate((0, 128, 256, 384)):
        k = np.arange(128)[:, None] + delta
        q = np.arange(512)[None, :]
        m01[di] = np.where(k > q, 0.0, 1.0)
    c["c_mask01"] = m01.astype(bf)
    pos = np.arange(core * T, (core + 1) * T, dtype=np.float64)
    inv0 = 1.0 / (10000.0 ** (np.arange(0, 128, 2, dtype=np.float64) / 128.0))
    f0 = inv0[:, None] * pos[None, :]
    c["c_cos0"] = np.concatenate([np.cos(f0), np.cos(f0)], 0).astype(bf)
    c["c_sin0"] = np.concatenate([-np.sin(f0), np.sin(f0)], 0).astype(bf)
    rot = np.zeros((128, 128), dtype=np.float32)
    for i in range(128):
        rot[(i + 64) % 128, i] = 1.0
    c["c_rot"] = rot.astype(bf)
    inv1 = 1.0 / (10000.0 ** (np.arange(0, 256, 2, dtype=np.float64) / 256.0))
    f1 = inv1[:, None] * pos[None, :]
    c["c_cos1"] = np.cos(f1).astype(bf)
    c["c_sin1"] = np.sin(f1).astype(bf)
    return c


def host_shards(inputs, core, ncores):
    sh = {}
    for nm, src, kind, d0, d1 in WSPECS:
        w = np.asarray(inputs[src])
        if kind == "rowcast":
            r = d0 // ncores
            sh["sh_" + nm] = np.ascontiguousarray(w[core * r:(core + 1) * r, :])
        else:
            cshard = d1 // ncores
            sh["sh_" + nm] = np.ascontiguousarray(w[:, core * cshard:(core + 1) * cshard])
    return sh



class P:
    pass


class P:
    pass


class Arena:

    def __init__(self, t, nslots):
        self.t = t
        self.t32 = t.ap().bitcast(F32).tensor
        self.t8 = t.ap().bitcast(FP8).tensor
        self.nslots = nslots
        self.free_ = list(range(nslots))

    def alloc(self, n):
        fs = sorted(self.free_)
        runs = []
        start, prev = None, None
        for v in fs:
            if start is None:
                start, prev = v, v
            elif v == prev + 1:
                prev = v
            else:
                runs.append((start, prev - start + 1))
                start, prev = v, v
        if start is not None:
            runs.append((start, prev - start + 1))
        cands = [r for r in runs if r[1] >= n]
        if not cands:
            raise RuntimeError(f"arena: no {n} contiguous slots free (runs={runs})")
        base = min(cands, key=lambda r: r[1])[0]
        for s in range(base, base + n):
            self.free_.remove(s)
        return Region(self, base, n)

    def free(self, r):
        self.free_ += list(range(r.base, r.base + r.n))


class Region:
    def __init__(self, ar, base, n):
        self.ar, self.base, self.n = ar, base, n

    def keys(self, c0=0, nc=None):
        nc = self.n if nc is None else nc
        return [(self.ar.t, self.base + c0 + i) for i in range(nc)]

    def bf(self, chunk=0, t0=0, nt=None, nchunk=1):
        ar = self.ar
        nt = 1024 if nt is None else nt
        if nchunk == 1:
            return AP(ar.t, (self.base + chunk) * 1024 + t0,
                      [[ar.nslots * 1024, 128], [1, nt]])
        return AP(ar.t, (self.base + chunk) * 1024 + t0,
                  [[ar.nslots * 1024, 128], [1024, nchunk], [1, nt]])

    def f32(self, chunk=0, t0=0, nt=None, nchunk=1):
        ar = self.ar
        nt = 512 if nt is None else nt
        if nchunk == 1:
            return AP(ar.t32, (self.base + chunk) * 512 + t0,
                      [[ar.nslots * 512, 128], [1, nt]])
        return AP(ar.t32, (self.base + chunk) * 512 + t0,
                  [[ar.nslots * 512, 128], [512, nchunk], [1, nt]])

    def fp8(self, chunk=0, t0=0, nt=None):
        ar = self.ar
        nt = 2048 if nt is None else nt
        return AP(ar.t8, (self.base + chunk) * 2048 + t0,
                  [[ar.nslots * 2048, 128], [1, nt]])


def emit_staging_setup(b, p, ncores):
    nc = b.nc
    rg = [list(range(ncores))]

    p.wt, p.agin, p.shin = {}, {}, {}
    for nm, src, kind, d0, d1 in WSPECS:
        if kind == "rowcast":
            r = d0 // ncores
            p.shin[nm] = nc.dram_tensor("sh_" + nm, [r, d1], F32, kind="ExternalInput").ap()
            p.agin[nm] = nc.dram_tensor("agi_" + nm, [r, d1], BF16).ap()
            p.wt[nm] = nc.dram_tensor("ag_" + nm, [d0, d1], BF16,
                                      addr_space="Shared" if ncores == 8 else "Local").ap()
        else:
            csh = d1 // ncores
            dt = I32 if kind == "int8" else F32
            p.shin[nm] = nc.dram_tensor("sh_" + nm, [d0, csh], dt, kind="ExternalInput").ap()
            p.agin[nm] = nc.dram_tensor("agi_" + nm, [csh, d0], BF16).ap()
            p.wt[nm] = nc.dram_tensor("ag_" + nm, [d1, d0], BF16,
                                      addr_space="Shared" if ncores == 8 else "Local").ap()
    dri = nc.dram_tensor("dri", [5632], I32, kind="ExternalInput").ap()
    p.dricol = p.sb("dricol", [128, 44], I32)
    b.op("scalar", lambda e: e.dma_start(
        AP(p.dricol, 0, [[44, 128], [1, 44]]),
        AP(dri.tensor, 0, [[1, 128], [128, 44]]),
        allow_slow_non_contiguous=True), writes=[p.dricol], dma=True)
    ar_add_i = nc.dram_tensor("ar_add_i", [1, 8], F32).ap()
    ar_add_o = nc.dram_tensor("ar_add_o", [1, 8], F32).ap()
    NMX = 104
    ar_max_i = nc.dram_tensor("ar_max_i", [128, NMX], F32).ap()
    ar_max_o = nc.dram_tensor("ar_max_o", [128, NMX], F32).ap()

    nwsh = nc.dram_tensor("sh_nw", [1, D // ncores], F32, kind="ExternalInput").ap()
    nwreg = p.arena.alloc(1)
    BUF = {}

    def _alloc_bufs():
        BUF["in"] = [p.arena.alloc(4) for _ in range(2)]
        BUF["f2"] = [p.arena.alloc(4) for _ in range(2)]
        BUF["q"] = [p.arena.alloc(2) for _ in range(2)]
        BUF["t"] = [p.arena.alloc(1) for _ in range(2)]
    _alloc_bufs()
    rtmp = p.sb("st_rtmp", [128, 64], F32)
    acc_sum = p.sb("st_acc", [128, 4], F32)
    rmx = p.sb("st_rmx", [128, NMX], F32)
    rinv = p.sb("st_rinv", [128, NMX], F32)
    swcol = p.sb("st_swcol", [128, 4], F32)
    srow = p.sb("st_srow", [1, 16], F32)
    p.rmx, p.rinv, p.swcol, p.swrow = rmx, rinv, swcol, srow

    RMX_OFF = {"g8": 0, "u8": 44, "d8": 88}
    TIDX = {nm: i for i, (nm, *_r) in enumerate(TERN)}
    NUMEL = {nm: d0 * d1 for nm, _s, _k, d0, d1 in WSPECS}

    def chunk_iter(d0, csh):
        assert csh <= 2048, "staging chunker: shard too wide (use 8 cores)"
        jtot = d0 // 128
        jg = max(1, min(2048 // csh, jtot))
        for j0 in range(0, jtot, jg):
            yield j0, min(jg, jtot - j0)

    b.op("vector", lambda e: e.memset(srow[0:1, 0:16], 0.0), writes=[(srow, "sums"), (srow, "gsums")])

    PERM = {"g8", "u8"}

    def load_chunk(nm, kind, j0, jn, csh, inb, d0, q="scalar"):
        if nm in PERM:
            for j in range(jn):
                dst = inb.f32(0, j * csh, csh)
                b.op("gpsimd", lambda e, dst=dst, j=j, j0=j0, csh=csh, nm=nm: e.indirect_dma_start(
                    dst, None, p.shin[nm],
                    bass.IndirectOffsetOnAxis(p.dricol[:, j0 + j:j0 + j + 1], 0)),
                    reads=[p.shin[nm], p.dricol], writes=inb.keys(), dma=True)
        else:
            src_ap = AP(p.shin[nm].tensor, j0 * 128 * csh,
                        [[csh, 128], [128 * csh, jn], [1, csh]])
            if kind == "int8":
                d = inb.f32(0, 0, jn * csh)
                d = AP(d.tensor, d.offset, d.ap).bitcast(I32)
            else:
                d = inb.f32(0, 0, jn * csh)
            b.op(q, lambda e, a=src_ap, d=d: e.dma_start(d, a),
                 writes=inb.keys(), dma=True)

    ptr = {"i": 0}

    def stage_one(nm, src, kind, d0, d1):
        csh = d1 // ncores if kind != "rowcast" else d1
        if kind == "rowcast":
            b.op("gpsimd", lambda e: e.dma_start(p.agin[nm], p.shin[nm]),
                 reads=[p.shin[nm]], writes=[p.agin[nm]], dma=True)
            return
        ti = TIDX.get(nm)
        for j0, jn in chunk_iter(d0, csh):
            bi = ptr["i"] % 2
            ptr["i"] += 1
            inb, f2b, qb, tb = BUF["in"][bi], BUF["f2"][bi], BUF["q"][bi], BUF["t"][bi]
            n = jn * csh
            load_chunk(nm, kind, j0, jn, csh, inb, d0, q="gpsimd")
            if kind == "int8":
                src_i = inb.f32(0, 0, n)
                src_i = AP(src_i.tensor, src_i.offset, src_i.ap).bitcast(I32)
                b.op("vector", lambda e, s=src_i, dq=qb.bf(0, 0, n): e.tensor_scalar(
                    dq, s, -128.0, None, AL.add), reads=inb.keys(), writes=qb.keys())
            else:
                if kind == "cast":
                    s3 = inb.f32(0, 0, csh)
                    s3 = AP(s3.tensor, s3.offset, [s3.ap[0], [csh, jn], [1, csh]])
                    nb = nwreg.f32(0, 0, csh)
                    nb = AP(nb.tensor, nb.offset, [nb.ap[0], [0, jn], [1, csh]])
                    d3 = qb.bf(0, 0, csh)
                    d3 = AP(d3.tensor, d3.offset, [d3.ap[0], [csh, jn], [1, csh]])
                    b.op("vector", lambda e, s3=s3, nb=nb, d3=d3: e.tensor_tensor(
                        d3, s3, nb, AL.mult),
                        reads=inb.keys() + nwreg.keys(), writes=qb.keys())
                elif kind == "tern":
                    b.op("vector", lambda e, s=inb.f32(0, 0, n), d2=f2b.f32(0, 0, n), ti=ti:
                         e.tensor_scalar(d2, s, swcol[:, ti:ti + 1], MAGIC, AL.mult, AL.add),
                         reads=inb.keys() + [(swcol, None)], writes=f2b.keys())
                    b.op("vector", lambda e, d2=f2b.f32(0, 0, n): e.tensor_scalar(
                        d2, d2, -MAGIC, 1.0, AL.add, AL.min),
                        reads=f2b.keys(), writes=f2b.keys())
                    b.op("vector", lambda e, d2=f2b.f32(0, 0, n), dq=qb.bf(0, 0, n):
                         e.tensor_scalar(dq, d2, -1.0, None, AL.max),
                         reads=f2b.keys(), writes=qb.keys())
                else:
                    off = RMX_OFF[nm]
                    for j in range(jn):
                        b.op("scalar", lambda e, o=off + j0 + j,
                             d8=f2b.fp8(0, j * csh, csh), s3=inb.f32(0, j * csh, csh):
                             e.activation(d8, s3, AF.Copy, scale=rinv[:, o:o + 1]),
                             reads=inb.keys() + [(rinv, None)], writes=f2b.keys())
                    b.op("scalar", lambda e, s8=f2b.fp8(0, 0, n), dq=qb.bf(0, 0, n):
                         e.activation(dq, s8, AF.Copy), reads=f2b.keys(), writes=qb.keys())
            ncc = cdiv(csh, 128)
            for cc in range(ncc):
                cn = min(128, csh - cc * 128)
                for j in range(jn):
                    b.op("tensor", lambda e, j=j, cc=cc, cn=cn, qb=qb, csh=csh: e.transpose(
                        p.pbtr[0:cn, j * 128:(j + 1) * 128],
                        qb.bf(0, j * csh + cc * 128, cn), p.ident[:, :]),
                        reads=qb.keys() + [p.ident], writes=[(p.pb[7], None)],
                        inc=(j == jn - 1))
                b.op("vector", lambda e, cn=cn, jn=jn, tb=tb: e.tensor_copy(
                    AP(tb.ar.t, tb.base * 1024, [[tb.ar.nslots * 1024, cn], [1, jn * 128]]),
                    p.pbtr[0:cn, 0:jn * 128]),
                    reads=[(p.pb[7], None)], writes=tb.keys())
                dst = AP(p.agin[nm].tensor, (cc * 128) * d0 + j0 * 128,
                         [[d0, cn], [1, jn * 128]])
                src_t = AP(tb.ar.t, tb.base * 1024,
                           [[tb.ar.nslots * 1024, cn], [1, jn * 128]])
                b.op("scalar", lambda e, dst=dst, src_t=src_t: e.dma_start(dst, src_t),
                     reads=tb.keys(), writes=[(p.agin[nm], (cc, j0))], dma=True)

    WD = {w[0]: w for w in WSPECS}

    def ag2(names):
        reads = []
        for nm in names:
            _, _, kind, d0, d1 = WD[nm]
            if kind == "rowcast":
                reads.append(p.agin[nm])
            else:
                csh = d1 // ncores
                for cc in range(cdiv(csh, 128)):
                    for j0, _jn in chunk_iter(d0, csh):
                        reads.append((p.agin[nm], (cc, j0)))
        for nm in names:
            b.op("gpsimd", lambda e, nm=nm: e.collective_compute(
                "AllGather", AL.bypass, replica_groups=rg,
                ins=[p.agin[nm]], outs=[p.wt[nm]]),
                reads=reads, writes=[p.wt[nm]], tag="cc")


    def _passA():
        par = 0
        for nm, src, kind, d0, d1 in [w for w in WSPECS if w[2] == "tern"]:
            csh = d1 // ncores
            ti = TIDX.get(nm)
            first = True
            for j0, jn in chunk_iter(d0, csh):
                bi = par % 2
                par += 1
                inb = BUF["in"][bi]
                load_chunk(nm, kind, j0, jn, csh, inb, d0)
                c3 = inb.f32(0, 0, csh)
                c3 = AP(c3.tensor, c3.offset, [c3.ap[0], [csh, jn], [1, csh]])
                b.op("vector", lambda e, c3=c3, jn=jn: e.reduce_sum(
                    rtmp[:, 0:jn], c3, axis=AX, apply_absolute_value=True),
                    reads=inb.keys(), writes=[(rtmp, "a")])
                b.op("vector", lambda e, jn=jn: e.reduce_sum(
                    rtmp[:, 48:49], rtmp[:, 0:jn], axis=AX),
                    reads=[(rtmp, "a")], writes=[(rtmp, "b")])
                if first:
                    b.op("vector", lambda e, ti=ti: e.tensor_copy(
                        acc_sum[:, ti:ti + 1], rtmp[:, 48:49]),
                        reads=[(rtmp, "b")], writes=[(acc_sum, ti)])
                else:
                    b.op("vector", lambda e, ti=ti: e.tensor_add(
                        acc_sum[:, ti:ti + 1], acc_sum[:, ti:ti + 1], rtmp[:, 48:49]),
                        reads=[(rtmp, "b"), (acc_sum, ti)], writes=[(acc_sum, ti)])
                first = False
        for ti in range(len(TERN)):
            b.op("tensor", lambda e, ti=ti: e.matmul(
                p.pb[6][0:1, ti:ti + 1], p.onesf[:, 0:1], acc_sum[:, ti:ti + 1],
                start=(ti == 0), stop=(ti == len(TERN) - 1), skip_group_check=True),
                reads=[(acc_sum, ti), p.onesf], writes=[(p.pb[6], None)],
                inc=(ti == len(TERN) - 1))
        b.op("vector", lambda e: e.tensor_copy(srow[0:1, 0:4], p.pb[6][0:1, 0:4]),
             reads=[(p.pb[6], None)], writes=[(srow, "sums")])
        b.op("gpsimd", lambda e: e.dma_start(ar_add_i[0:1, 0:8], srow[0:1, 0:8]),
             reads=[(srow, "sums")], writes=[ar_add_i], dma=True)
        b.op("gpsimd", lambda e: e.collective_compute(
            "AllReduce", AL.add, replica_groups=rg, ins=[ar_add_i], outs=[ar_add_o]),
            reads=[ar_add_i], writes=[ar_add_o], tag="cc")
        b.op("gpsimd", lambda e: e.dma_start(srow[0:1, 4:8], ar_add_o[0:1, 0:4]),
             reads=[ar_add_o], writes=[(srow, "gsums")], dma=True)

        for nm, src, kind, d0, d1 in [w for w in WSPECS if w[2] == "fp8"]:
            csh = d1 // ncores
            for j0, jn in chunk_iter(d0, csh):
                bi = par % 2
                par += 1
                inb = BUF["in"][bi]
                load_chunk(nm, kind, j0, jn, csh, inb, d0)
                c3 = inb.f32(0, 0, csh)
                c3 = AP(c3.tensor, c3.offset, [c3.ap[0], [csh, jn], [1, csh]])
                off = RMX_OFF[nm]
                b.op("vector", lambda e, c3=c3, jn=jn, o=off + j0: e.reduce_max(
                    AP(rmx, o, [[NMX, 128], [1, jn]]), c3,
                    axis=AX, apply_absolute_value=True),
                    reads=inb.keys(), writes=[(rmx, (nm, j0))])
        b.op("gpsimd", lambda e: e.dma_start(ar_max_i, rmx[:, :]),
             reads=[(rmx, (nm2, j2)) for nm2, _s, k2, d02, d12 in WSPECS if k2 == "fp8"
                    for j2, _jn in chunk_iter(d02, d12 // ncores)],
             writes=[ar_max_i], dma=True)
        b.op("gpsimd", lambda e: e.collective_compute(
            "AllReduce", AL.max, replica_groups=rg, ins=[ar_max_i], outs=[ar_max_o]),
            reads=[ar_max_i], writes=[ar_max_o], tag="cc")
        b.op("gpsimd", lambda e: e.dma_start(rmx[:, :], ar_max_o),
             reads=[ar_max_o], writes=[(rmx, "g")], dma=True)

    def _scales():
        for ti, (nm, _s, _k, d0, d1) in enumerate(TERN):
            b.op("vector", lambda e, ti=ti, n=NUMEL[TERN[ti][0]]: e.tensor_scalar(
                srow[0:1, 8 + ti:9 + ti], srow[0:1, 4 + ti:5 + ti], 1.0 / n, 1e-5,
                AL.mult, AL.max), reads=[(srow, "gsums")], writes=[(srow, "m" + str(ti))])
        NTN = len(TERN)
        b.op("vector", lambda e: e.reciprocal(srow[0:1, 12:12 + NTN], srow[0:1, 8:8 + NTN]),
             reads=[(srow, "m0"), (srow, "m1"), (srow, "m2")], writes=[(srow, "recip")])
        b.op("tensor", lambda e: e.matmul(
            p.pb[6][:, 0:NTN], p.onesf[0:1, 0:128], srow[0:1, 12:12 + NTN],
            start=True, stop=True, skip_group_check=True),
            reads=[(srow, "recip"), p.onesf], writes=[(p.pb[6], None)])
        b.op("vector", lambda e: e.tensor_copy(swcol[:, 0:NTN], p.pb[6][:, 0:NTN]),
             reads=[(p.pb[6], None)], writes=[swcol])
        b.op("vector", lambda e: e.reciprocal(rinv[:, :], rmx[:, :]),
             reads=[(rmx, "g")], writes=[(rinv, "a")])
        b.op("vector", lambda e: e.tensor_scalar(rinv[:, :], rinv[:, :], 224.0, None, AL.mult),
             reads=[(rinv, "a")], writes=[rinv])

        cshD = D // ncores
        b.op("scalar", lambda e: e.dma_start(p.rowp[0:1, 0:cshD], nwsh),
             writes=[rkey(p, 0)], dma=True)
        for g0 in range(0, cshD, 512):
            gn = min(512, cshD - g0)
            b.op("tensor", lambda e, g0=g0, gn=gn: e.matmul(
                p.pb[5][:, 0:gn], p.onesf[0:1, 0:128], p.rowp[0:1, g0:g0 + gn],
                start=True, stop=True, skip_group_check=True),
                reads=[rkey(p, 0), p.onesf], writes=[(p.pb[5], None)])
            b.op("vector", lambda e, g0=g0, gn=gn: e.tensor_copy(
                nwreg.f32(0, g0, gn), p.pb[5][:, 0:gn]),
                reads=[(p.pb[5], None)], writes=nwreg.keys())

    def _quant(names):
        for nm in names:
            stage_one(*WD[nm])
        ag2(names)

    def _free(nw=False):
        for r in BUF["in"] + BUF["f2"] + BUF["q"] + BUF["t"] + ([nwreg] if nw else []):
            p.arena.free(r)

    p.stg = {"passA": _passA, "scales": _scales, "quant": _quant, "free": _free,
             "realloc": _alloc_bufs}



def build_program(ncores=8, T=1024, upto="all"):
    assert T == 1024
    from contextlib import ExitStack
    nc = bass.Bass("TRN2", target_bir_lowering=False)
    b = Builder(nc)
    ctx = ExitStack()
    b.ctx = ctx
    p = P()
    p.ctx = ctx
    p.T = T
    p.ncores = ncores

    x_in = nc.dram_tensor("x", [T, D], F32, kind="ExternalInput").ap()
    out = nc.dram_tensor("out", [T, D], F32, kind="ExternalOutput").ap()
    ident_h = nc.dram_tensor("c_ident", [128, 128], BF16, kind="ExternalInput").ap()
    identf_h = nc.dram_tensor("c_identf", [128, 128], F32, kind="ExternalInput").ap()

    def sb(name, shape, dt=F32):
        return ctx.enter_context(nc.sbuf_tensor(name, shape, dt))
    p.sb = sb

    NT = D // 128
    p.ident = sb("ident", [128, 128], BF16)
    p.identf = sb("identf", [128, 128], F32)
    p.ones = sb("ones", [128, 128], BF16)
    p.onesf = sb("onesf", [128, 128], F32)
    p.xT = sb("xT", [128, NT * T], BF16)
    NSLOT = 71
    p.arena = Arena(sb("arena", [128, NSLOT * 1024], BF16), NSLOT)
    p.rowp = sb("rowp", [128, 3072], F32)
    p.pb = [ctx.enter_context(nc.psum_tensor(f"pb{i}", [128, 512], F32)) for i in range(8)]
    p.pbtr = p.pb[7].ap().bitcast(BF16).tensor
    p.tr8 = sb("tr8", [128, 128], F32)
    p.mcol = sb("mcol", [128, 16], F32)

    def xTap(c0, nchunk=1, t0=0, nt=None):
        nt = T if nt is None else nt
        if nchunk == 1:
            return AP(p.xT, c0 * T + t0, [[NT * T, 128], [1, nt]])
        return AP(p.xT, c0 * T + t0, [[NT * T, 128], [T, nchunk], [1, nt]])
    p.xTap = xTap
    p.rowp = p.rowp


    with ctx:
        b.op("sync", lambda e: e.dma_start(p.ident[:, :], ident_h), writes=[p.ident], dma=True)
        b.op("sync", lambda e: e.dma_start(p.identf[:, :], identf_h), writes=[p.identf], dma=True)
        b.op("vector", lambda e: e.memset(p.ones[:, :], 1.0), writes=[p.ones])
        b.op("vector", lambda e: e.memset(p.onesf[:, :], 1.0), writes=[p.onesf])

        xio = p.arena.alloc(4)
        xio2 = p.arena.alloc(4)
        for i in range(T // 128):
            buf = xio if i % 2 == 0 else xio2
            b.op("sync", lambda e, i=i, buf=buf: e.dma_start(
                buf.f32(0, 0, D), x_in[i * 128:(i + 1) * 128, :]),
                writes=buf.keys(), dma=True)
            for f in range(NT):
                b.op("tensor", lambda e, i=i, f=f, buf=buf: e.transpose(
                    p.pb[6][:, 0:128], buf.f32(0, f * 128, 128), p.identf[:, :]),
                    reads=buf.keys() + [p.identf], writes=[(p.pb[6], None)])
                b.op("vector", lambda e, i=i, f=f: e.tensor_copy(
                    xTap(f, 1, i * 128, 128), p.pb[6][:, 0:128]),
                    reads=[(p.pb[6], None)], writes=[(p.xT, f)])
        p.arena.free(xio)
        p.arena.free(xio2)

        do_l0a = upto in ("l0a", "l0f", "all", "l1f_test")
        if upto != "m0":
            emit_staging_setup(b, p, ncores)
            p.stg["passA"]()
        if upto != "m0":
            p.stg["scales"]()
            p.stg["quant"](["qkv"])
        if do_l0a:
            dbg = emit_l0_attn_head(b, p)
            emit_l0_attn_quant(b, p)
        if upto != "m0":
            p.stg["quant"](["ow", "up0", "dn0"])
            p.stg["quant"](["wq", "wk", "wv", "wo"])
            p.stg["free"](nw=True)
        if do_l0a:
            emit_l0_attn_waves(b, p)
            emit_l0_attn_rest(b, p, dbg)
            p.dbg = dbg
            tapx(b, p, "x_l0a")
        if upto != "m0":
            p.stg["realloc"]()
            p.stg["quant"](["g8", "u8", "d8"])
            p.stg["free"]()
        if upto in ("l0f", "all", "l1f_test"):
            emit_l0_ffn(b, p)
        if upto != "m0":
            p.stg["realloc"]()
            p.stg["quant"](["ig", "iu", "id8"])
            p.stg["free"]()
        if upto in ("all",):
            emit_l1_attn(b, p, ncores)
        if upto in ("l1f_test", "all"):
            emit_l1_ffn(b, p)

        xo = p.arena.alloc(4)
        for i in range(T // 128):
            for f in range(NT):
                b.op("tensor", lambda e, i=i, f=f: e.transpose(
                    p.pbtr[:, 0:128], xTap(f, 1, i * 128, 128), p.ident[:, :]),
                    reads=[(p.xT, f), p.ident], writes=[(p.pb[7], None)])
                b.op("vector", lambda e, f=f: e.tensor_copy(
                    xo.f32(0, f * 128, 128), p.pbtr[:, 0:128]),
                    reads=[(p.pb[7], None)], writes=xo.keys())
            b.op("sync", lambda e, i=i, xo=xo: e.dma_start(
                out[i * 128:(i + 1) * 128, :], xo.f32(0, 0, D)),
                reads=xo.keys(), dma=True)
        p.arena.free(xo)
        for eng in ENGINES:
            b.drain_dma(eng, eng)
        from concourse.aot_env import allow_ici_all_to_all
        with allow_ici_all_to_all():
            b.flush()
    return nc


def input_names(nc):
    names = set()
    for a in nc.m.functions[0].allocations:
        if isinstance(a, mybir.MemoryLocationSet) and a.kind == "ExternalInput":
            nm = a.memorylocations[0].name
            if nm != "partition_id":
                names.add(nm)
    return names


def kernel(**inputs):
    ncores, T = 8, 1024
    x = np.ascontiguousarray(np.asarray(inputs["L0_attn_x"], dtype=np.float32).reshape(S, D))
    nc = build_program(ncores=ncores, T=T)
    names = input_names(nc)
    in_maps = []
    for c in range(ncores):
        m = host_inmap(inputs, x, c, ncores, T)
        m = {k: v for k, v in m.items() if k in names}
        missing = names - set(m)
        assert not missing, f"missing inputs: {missing}"
        in_maps.append(m)
    res = run_bass_kernel_spmd(nc, in_maps, core_ids=list(range(ncores)))
    out = np.concatenate([np.asarray(r["out"]) for r in res.results], axis=0)
    return out.reshape(1, S, D)


def rowap(p, slot, g0=0, gn=1024):
    pb_ = 32 * (slot // 3)
    off = 1024 * (slot % 3)
    return p.rowp[pb_:pb_ + 1, off + g0:off + g0 + gn]


def rkey(p, slot):
    return (p.rowp, ("row", slot))


def row_from_col(b, p, col_ap, ncol, slot, rdkeys):
    b.op("tensor", lambda e: e.transpose(p.pb[6][0:ncol, 0:128], col_ap, p.identf[:, :]),
         reads=rdkeys + [p.identf], writes=[(p.pb[6], None)])
    b.op("vector", lambda e: e.tensor_copy(p.tr8[0:ncol, 0:128], p.pb[6][0:ncol, 0:128]),
         reads=[(p.pb[6], None)], writes=[p.tr8])
    pb_ = 32 * (slot // 3)
    off = 1024 * (slot % 3)
    dst = p.rowp[pb_:pb_ + 1, off:off + ncol * 128]
    dst = AP(dst.tensor, dst.offset, [dst.ap[0], [128, ncol], [1, 128]])
    b.op("scalar", lambda e: e.dma_start(dst, p.tr8[0:ncol, 0:128]),
         reads=[p.tr8], writes=[rkey(p, slot)], dma=True)


def rop(b, p, eng, fn, rslots, wslot):
    outs = rowap(p, wslot)
    ins = [rowap(p, s) for s in rslots]
    b.op(eng, lambda e: fn(e, outs, *ins),
         reads=[rkey(p, s) for s in rslots], writes=[rkey(p, wslot)])


def bcast_row(b, p, slot, nt, dsts, t0=0, tok_off=0):
    pb_ = 32 * (slot // 3)
    for g0 in range(0, nt, 512):
        gn = min(512, nt - g0)
        b.op("tensor", lambda e, g0=g0, gn=gn: e.matmul(
            p.pb[5][:, 0:gn], p.onesf[pb_:pb_ + 1, 0:128],
            rowap(p, slot, t0 + g0, gn),
            start=True, stop=True, skip_group_check=True),
            reads=[rkey(p, slot), p.onesf], writes=[(p.pb[5], None)])
        for apf, wkeys in dsts:
            b.op("vector", lambda e, apf=apf, g0=g0, gn=gn: e.tensor_copy(
                apf(g0 + tok_off, gn), p.pb[5][:, 0:gn]),
                reads=[(p.pb[5], None)], writes=list(wkeys))


def colmax_abs(b, p, apfn, nch, slot, scratch, rdkeys, eng2="vector", T=None):
    T = T or p.T
    t_ = scratch.ar.t
    off0 = scratch.base * 1024
    mx = t_[0:128, off0:off0 + T]
    mn = t_[0:128, off0 + 1024:off0 + 1024 + T]
    aps = [apfn(c) for c in range(nch)]
    kmx, kmn = [scratch.keys(0, 1)[0]], [scratch.keys(1, 1)[0]]
    b.op("vector", lambda e, a0=aps[0], a1=aps[1]: e.tensor_max(mx, a0, a1),
         reads=rdkeys, writes=kmx)
    b.op(eng2, lambda e, a0=aps[0], a1=aps[1]: e.tensor_tensor(mn, a0, a1, AL.min),
         reads=rdkeys, writes=kmn)
    for c in range(2, nch):
        b.op("vector", lambda e, a=aps[c]: e.tensor_max(mx, mx, a),
             reads=rdkeys + kmx, writes=kmx)
        b.op(eng2, lambda e, a=aps[c]: e.tensor_tensor(mn, mn, a, AL.min),
             reads=rdkeys + kmn, writes=kmn)
    b.op(eng2, lambda e: e.tensor_scalar(mn, mn, -1.0, None, AL.mult), reads=kmn, writes=kmn)
    b.op("vector", lambda e: e.tensor_max(mx, mx, mn), reads=kmx + kmn, writes=kmx)
    ntt = T // 128
    for tt in range(ntt):
        b.op("tensor", lambda e, tt=tt: e.transpose(
            p.pbtr[0:128, tt * 128:(tt + 1) * 128],
            t_[0:128, off0 + tt * 128:off0 + (tt + 1) * 128], p.ident[:, :]),
            reads=kmx + [p.ident], writes=[(p.pb[7], None)], inc=(tt == ntt - 1))
    b.op("vector", lambda e: e.reduce_max(
        p.mcol[:, 0:ntt], AP(p.pbtr, 0, [[1024, 128], [128, ntt], [1, 128]]), axis=AX),
        reads=[(p.pb[7], None)], writes=[p.mcol])
    row_from_col(b, p, p.mcol[:, 0:ntt], ntt, slot, [(p.mcol, None)])


def sumrow(b, p, apfn, nch, slot, rdkeys, square=False, sq_scratch=None, T=None):
    T = T or p.T
    pb_ = 32 * (slot // 3)
    off = 1024 * (slot % 3)
    for g0 in range(0, T, 512):
        for c in range(nch):
            if square:
                sl = sq_scratch.bf(c % 2, 0, 512)
                b.op("scalar", lambda e, sl=sl, a=apfn(c, g0, 512): e.activation(
                    sl, a, AF.Square),
                    reads=rdkeys, writes=[sq_scratch.keys(c % 2, 1)[0]])
                rhs, rks = sl, [sq_scratch.keys(c % 2, 1)[0]]
            else:
                rhs, rks = apfn(c, g0, 512), rdkeys
            b.op("tensor", lambda e, rhs=rhs, c=c: e.matmul(
                p.pb[6][:, 0:512], p.ones[:, 0:128], rhs,
                start=(c == 0), stop=(c == nch - 1), skip_group_check=True),
                reads=rks + [p.ones], writes=[(p.pb[6], None)])
        b.op("vector", lambda e, g0=g0: e.tensor_copy(
            p.rowp[pb_:pb_ + 1, off + g0:off + g0 + 512], p.pb[6][0:1, 0:512]),
            reads=[(p.pb[6], None)], writes=[rkey(p, slot)])


def quant_int8_inplace(b, p, reg, nch, mult_bc, mult_keys, tmp, T=None, c0=0):
    T = T or p.T
    for ci in range(nch):
        c = c0 + ci
        tf = tmp.f32(2 * (ci % 2), 0, T)
        tk = tmp.keys(2 * (ci % 2), 2)
        b.op("vector", lambda e, c=c, tf=tf: e.tensor_tensor(
            tf, reg.bf(c, 0, T), mult_bc, AL.mult),
            reads=reg.keys(c, 1) + list(mult_keys), writes=tk)
        b.op("vector", lambda e, c=c, tf=tf: e.tensor_scalar(
            reg.bf(c, 0, T), tf, MAGIC, -MAGIC, AL.add, AL.add),
            reads=tk, writes=reg.keys(c, 1))


def proj(b, p, wt, CIN, DOUT, xr, evac, dout_ranges=None, slab=512, banks=(0, 1, 2, 3),
         wt_key=None):
    T = p.T
    ncin = CIN // 128
    slabslots = cdiv(ncin * slab, 1024)
    wbuf = [p.arena.alloc(slabslots), p.arena.alloc(slabslots)]
    ranges = dout_ranges or [(0, DOUT)]
    si = 0
    bi = 0
    ngr = T // 512
    for (r0, r1) in ranges:
        for d0 in range(r0, r1, slab):
            sl = min(slab, r1 - d0)
            reg = wbuf[si % 2]
            si += 1
            src = AP(wt.tensor, d0, [[DOUT, 128], [128 * DOUT, ncin], [1, sl]])
            dst = AP(reg.ar.t, reg.base * 1024,
                     [[reg.ar.nslots * 1024, 128], [sl, ncin], [1, sl]])
            b.op("sync", lambda e, src=src, dst=dst: e.dma_start(dst, src),
                 reads=[wt if wt_key is None else wt_key], writes=reg.keys(), dma=True)
            for dc in range(sl // 128):
                bks = [banks[(bi + g) % len(banks)] for g in range(ngr)]
                bi += ngr
                for c in range(ncin):
                    wap = AP(reg.ar.t, reg.base * 1024 + c * sl + dc * 128,
                             [[reg.ar.nslots * 1024, 128], [1, 128]])
                    for g in range(ngr):
                        xap, xkeys = xr(c, g * 512)
                        b.op("tensor", lambda e, wap=wap, xap=xap, c=c, g=g, bk=bks[g]: e.matmul(
                            p.pb[bk][:, 0:512], wap, xap,
                            start=(c == 0), stop=(c == ncin - 1)),
                            reads=reg.keys() + xkeys, writes=[(p.pb[bks[g]], None)],
                            inc=(c == ncin - 1))
                for g in range(ngr):
                    evac(bks[g], d0 + dc * 128, g * 512)
    for r in wbuf:
        p.arena.free(r)


def ln_rows(b, p, sl_sum, sl_sq, sl_m, sl_r, n, eps=1e-5):
    rop(b, p, "vector", lambda e, o, a: e.tensor_scalar(o, a, 1.0 / n, None, AL.mult),
        [sl_sum], sl_m)
    rop(b, p, "vector", lambda e, o, a: e.tensor_scalar(o, a, 1.0 / n, None, AL.mult),
        [sl_sq], sl_sq)
    assert sl_sum // 3 == sl_sq // 3
    rop(b, p, "vector", lambda e, o, m: e.tensor_tensor(o, m, m, AL.mult), [sl_m], sl_sum)
    rop(b, p, "vector", lambda e, o, sq, m2: e.tensor_sub(o, sq, m2), [sl_sq, sl_sum], sl_r)
    rop(b, p, "vector", lambda e, o, v: e.tensor_scalar(o, v, eps, None, AL.add), [sl_r], sl_r)
    rop(b, p, "scalar", lambda e, o, v: e.activation(o, v, AF.Sqrt), [sl_r], sl_r)
    rop(b, p, "vector", lambda e, o, sd: e.reciprocal(o, sd), [sl_r], sl_r)


def tapx(b, p, name):
    if not DEBUG_TAPS:
        return
    T, NT = p.T, 16
    d = b.nc.dram_tensor("dbg_" + name, [128, NT * T], BF16, kind="ExternalOutput").ap()
    for c in range(NT):
        b.op("gpsimd", lambda e, c=c: e.dma_start(
            AP(d.tensor, c * T, [[NT * T, 128], [1, T]]),
            AP(p.xT, c * T, [[NT * T, 128], [1, T]])),
            reads=[(p.xT, c)], dma=True)


def tap(b, p, name, reg, nslots, t0=0, nt=None):
    if not DEBUG_TAPS:
        return
    nt = nt if nt is not None else 1024
    d = b.nc.dram_tensor("dbg_" + name, [128, nslots * 1024], BF16, kind="ExternalOutput").ap()
    for c in range(nslots):
        b.op("gpsimd", lambda e, c=c: e.dma_start(
            AP(d.tensor, c * 1024, [[nslots * 1024, 128], [1, 1024]]),
            reg.bf(c, 0, 1024)), reads=[reg.keys(c, 1)[0]], dma=True)


def emit_l0_attn_head(b, p):
    nc, T, NT = b.nc, p.T, 16
    A = p.arena
    dbg = {}

    nwa = nc.dram_tensor("nw_attn", [D], F32, kind="ExternalInput").ap()
    p.nwa_c = p.sb("nwa_c", [128, 16], F32)
    b.op("scalar", lambda e: e.dma_start(
        AP(p.nwa_c, 0, [[16, 128], [1, 16]]),
        AP(nwa.tensor, 0, [[1, 128], [128, 16]]),
        allow_slow_non_contiguous=True), writes=[p.nwa_c], dma=True)

    cos0 = nc.dram_tensor("c_cos0", [128, T], BF16, kind="ExternalInput").ap()
    sin0 = nc.dram_tensor("c_sin0", [128, T], BF16, kind="ExternalInput").ap()
    rot0 = nc.dram_tensor("c_rot", [128, 128], BF16, kind="ExternalInput").ap()
    p.cos0 = p.sb("cos0", [128, T], BF16)
    p.sin0 = p.sb("sin0", [128, T], BF16)
    p.rotm = p.sb("rotm", [128, 128], BF16)
    for t, h in ((p.cos0, cos0), (p.sin0, sin0), (p.rotm, rot0)):
        b.op("scalar", lambda e, t=t, h=h: e.dma_start(t[:, :], h), writes=[t], dma=True)
    maskh = nc.dram_tensor("c_masks", [4, 128, 512], BF16, kind="ExternalInput").ap()
    p.maskt = p.sb("maskt", [128, 4 * 512], BF16)
    b.op("scalar", lambda e: e.dma_start(
        AP(p.maskt, 0, [[2048, 128], [512, 4], [1, 512]]),
        AP(maskh.tensor, 0, [[512, 128], [65536, 4], [1, 512]])),
        writes=[p.maskt], dma=True)

    SL_P0, SL_M1, SL_R1 = 0, 1, 2
    SL_S, SL_RRMS, SL_T2, SL_MULT, SL_L = 3, 4, 5, 6, 7
    sqs = A.alloc(2)
    xap = lambda c, g0, gn: AP(p.xT, c * T + g0, [[NT * T, 128], [1, gn]])
    xkeys = [(p.xT, f) for f in range(NT)]
    sumrow(b, p, xap, NT, SL_S, xkeys)
    sumrow(b, p, xap, NT, SL_T2, xkeys, square=True, sq_scratch=sqs)
    ln_rows(b, p, SL_S, SL_T2, SL_M1, SL_R1, D)

    ln0 = A.alloc(16)
    mbc = A.alloc(2)
    rbc = A.alloc(1)
    bcast_row(b, p, SL_M1, T, [(lambda g0, gn: mbc.f32(0, g0, gn), mbc.keys())])
    bcast_row(b, p, SL_R1, T, [(lambda g0, gn: rbc.bf(0, g0, gn), rbc.keys())])
    tmp4 = A.alloc(4)
    for c in range(NT):
        tf = tmp4.f32(2 * (c % 2), 0, T)
        tk = tmp4.keys(2 * (c % 2), 2)
        b.op("vector", lambda e, c=c, tf=tf: e.tensor_tensor(
            tf, xap(c, 0, T), mbc.f32(0, 0, T), AL.subtract),
            reads=[(p.xT, c)] + mbc.keys(), writes=tk)
        b.op("vector", lambda e, c=c, tf=tf: e.tensor_tensor(
            ln0.bf(c, 0, T), tf, rbc.bf(0, 0, T), AL.mult),
            reads=tk + rbc.keys(), writes=ln0.keys(c, 1))
    A.free(mbc)
    A.free(rbc)
    dbg["ln0"] = ln0
    tap(b, p, "ln0", ln0, 16)
    dbg["sqs"] = sqs
    dbg["ln0"] = ln0
    dbg["tmp4"] = tmp4
    p.l0dbg = dbg
    return dbg


def emit_l0_attn_quant(b, p):
    nc, T, NT = b.nc, p.T, 16
    A = p.arena
    dbg = p.l0dbg
    SL_P0, SL_M1, SL_R1 = 0, 1, 2
    SL_S, SL_RRMS, SL_T2, SL_MULT, SL_L = 3, 4, 5, 6, 7
    xap = lambda c, g0, gn: AP(p.xT, c * T + g0, [[NT * T, 128], [1, gn]])
    xkeys = [(p.xT, f) for f in range(NT)]
    sqs = dbg["sqs"]

    ln0 = dbg["ln0"]
    tmp4 = dbg["tmp4"]

    sumrow(b, p, lambda c, g0, gn: ln0.bf(c, g0, gn), NT, SL_S, ln0.keys(),
           square=True, sq_scratch=sqs)
    rop(b, p, "vector", lambda e, o, a: e.tensor_scalar(o, a, 1.0 / D, 1e-5, AL.mult, AL.add),
        [SL_S], SL_RRMS)
    rop(b, p, "scalar", lambda e, o, v: e.activation(o, v, AF.Sqrt), [SL_RRMS], SL_RRMS)
    rop(b, p, "vector", lambda e, o, v: e.reciprocal(o, v), [SL_RRMS], SL_RRMS)
    for c in range(NT):
        b.op("vector", lambda e, c=c: e.tensor_scalar(
            ln0.bf(c, 0, T), ln0.bf(c, 0, T), p.nwa_c[:, c:c + 1], None, AL.mult),
            reads=ln0.keys(c, 1) + [p.nwa_c], writes=ln0.keys(c, 1))
    hq = ln0
    dbg["hw"] = hq

    colmax_abs(b, p, lambda c: hq.bf(c, 0, T), NT, SL_T2, sqs, hq.keys())
    rop(b, p, "vector", lambda e, o, mx, rr: e.tensor_tensor(o, mx, rr, AL.mult),
        [SL_T2, SL_RRMS], SL_T2)
    rop(b, p, "vector", lambda e, o, a: e.tensor_scalar(o, a, 1e-5, None, AL.max),
        [SL_T2], SL_T2)
    rop(b, p, "vector", lambda e, o, a: e.reciprocal(o, a), [SL_T2], SL_S)
    rop(b, p, "vector", lambda e, o, a, rr: e.scalar_tensor_tensor(
        o, a, 127.0, rr, AL.mult, AL.mult), [SL_S, SL_RRMS], SL_MULT)
    mulbc = A.alloc(2)
    bcast_row(b, p, SL_MULT, T, [(lambda g0, gn: mulbc.f32(0, g0, gn), mulbc.keys())])
    quant_int8_inplace(b, p, hq, NT, mulbc.f32(0, 0, T), mulbc.keys(), tmp4)
    A.free(tmp4)
    A.free(mulbc)
    dbg["hq"] = hq
    tap(b, p, "hq", hq, 16)

    return dbg

def emit_l0_attn_waves(b, p):
    nc, T, NT = b.nc, p.T, 16
    A = p.arena
    dbg = p.l0dbg
    SL_P0, SL_M1, SL_R1 = 0, 1, 2
    SL_S, SL_RRMS, SL_T2, SL_MULT, SL_L = 3, 4, 5, 6, 7
    xap = lambda c, g0, gn: AP(p.xT, c * T + g0, [[NT * T, 128], [1, gn]])
    xkeys = [(p.xT, f) for f in range(NT)]
    sqs = dbg["sqs"]

    hq = dbg["hq"]
    b.op("vector", lambda e: e.tensor_copy(rowap(p, SL_P0), rowap(p, SL_T2)),
         reads=[rkey(p, SL_T2)], writes=[rkey(p, SL_P0)])
    b.op("vector", lambda e: e.tensor_scalar(
        rowap(p, SL_P0), rowap(p, SL_P0), p.swrow[0:1, 8:9], 1.0 / 127.0,
        AL.mult, AL.mult),
        reads=[rkey(p, SL_P0), (p.swrow, "m0")], writes=[rkey(p, SL_P0)])
    osbc = A.alloc(1)
    bcast_row(b, p, SL_P0, T, [(lambda g0, gn: osbc.bf(0, g0, gn), osbc.keys())])
    dbg["osbc"] = osbc
    ctx = A.alloc(16)
    qw_ = A.alloc(4)
    kw_ = A.alloc(4)
    vw_ = A.alloc(4)
    rotw = A.alloc(4)
    vtmp = A.alloc(1)
    pstrip = A.alloc(2)
    ibc = A.alloc(1)
    hkeys = hq.keys()

    def xr(c, g0):
        return hq.bf(c, g0, 512), [hq.keys(c, 1)[0]]

    for w in range(4):
        def qkv_evac(bank, dabs, g0, w=w):
            qkvi = dabs // 2048
            h = (dabs % 2048) // 128
            sl = h - w * 4
            if qkvi < 2:
                dst = (qw_ if qkvi == 0 else kw_)
                b.op("vector", lambda e, dst=dst, sl=sl, g0=g0, bank=bank: e.tensor_tensor(
                    dst.bf(sl, g0, 512), p.pb[bank][:, 0:512], osbc.bf(0, g0, 512), AL.mult),
                    reads=[(p.pb[bank], None)] + osbc.keys(), writes=dst.keys(sl, 1))
            else:
                b.op("vector", lambda e, g0=g0, bank=bank: e.tensor_tensor(
                    vtmp.bf(0, 0, 512), p.pb[bank][:, 0:512], osbc.bf(0, g0, 512), AL.mult),
                    reads=[(p.pb[bank], None)] + osbc.keys(), writes=vtmp.keys())
                for q4 in range(4):
                    b.op("tensor", lambda e, q4=q4: e.transpose(
                        p.pbtr[:, q4 * 128:(q4 + 1) * 128], vtmp.bf(0, q4 * 128, 128),
                        p.ident[:, :]),
                        reads=vtmp.keys() + [p.ident], writes=[(p.pb[7], None)],
                        inc=(q4 == 3))
                dst = AP(vw_.ar.t, (vw_.base) * 1024 + ((g0 // 128) * 4 + sl) * 128,
                         [[vw_.ar.nslots * 1024, 128], [512, 4], [1, 128]])
                b.op("vector", lambda e, dst=dst: e.tensor_copy(dst, p.pbtr[:, 0:512]),
                     reads=[(p.pb[7], None)], writes=vw_.keys())

        ranges = [(i * 2048 + w * 512, i * 2048 + (w + 1) * 512) for i in range(3)]
        proj(b, p, p.wt["qkv"], D, 6144, xr, qkv_evac, dout_ranges=ranges,
             slab=256, banks=(0, 1, 2, 3))

        for reg in (qw_, kw_):
            for sl in range(4):
                for g0 in (0, 512):
                    b.op("tensor", lambda e, reg=reg, sl=sl, g0=g0: e.matmul(
                        p.pb[4][:, 0:512], p.rotm[:, :], reg.bf(sl, g0, 512),
                        start=True, stop=True),
                        reads=reg.keys(sl, 1) + [p.rotm], writes=[(p.pb[4], None)])
                    b.op("vector", lambda e, reg=reg, sl=sl, g0=g0: e.tensor_copy(
                        rotw.bf(sl, g0, 512), p.pb[4][:, 0:512]),
                        reads=[(p.pb[4], None)], writes=rotw.keys(sl, 1))
            ap4 = lambda r, c0: AP(r.ar.t, r.base * 1024, [[r.ar.nslots * 1024, 128], [1024, 4], [1, 1024]])
            cosb = AP(p.cos0, 0, [[T, 128], [0, 4], [1, T]])
            sinb = AP(p.sin0, 0, [[T, 128], [0, 4], [1, T]])
            b.op("vector", lambda e, reg=reg: e.tensor_tensor(
                ap4(rotw, 0), ap4(rotw, 0), sinb, AL.mult),
                reads=rotw.keys() + [p.sin0], writes=rotw.keys())
            b.op("vector", lambda e, reg=reg: e.tensor_tensor(
                ap4(reg, 0), ap4(reg, 0), cosb, AL.mult),
                reads=reg.keys() + [p.cos0], writes=reg.keys())
            b.op("vector", lambda e, reg=reg: e.tensor_tensor(
                ap4(reg, 0), ap4(reg, 0), ap4(rotw, 0), AL.add),
                reads=reg.keys() + rotw.keys(), writes=reg.keys())

        if w == 0:
            tap(b, p, "qw0", qw_, 4)
            tap(b, p, "kw0", kw_, 4)
            tap(b, p, "vw0", vw_, 4)
        for hw2 in range(4):
            h = w * 4 + hw2
            for tg in range(2):
                g0 = tg * 512
                ktmax = (tg + 1) * 4

                def sc_emit(kt, hw2=hw2, g0=g0, tg=tg):
                    sb_ = kt % 2
                    diag = kt >= 4 * tg
                    b.op("tensor", lambda e, hw2=hw2, kt=kt, g0=g0, sb_=sb_, tg=tg: e.matmul(
                        p.pb[sb_][:, 0:512], kw_.bf(hw2, kt * 128, 128),
                        qw_.bf(hw2, g0, 512), start=True, stop=not (kt >= 4 * tg)),
                        reads=kw_.keys(hw2, 1) + qw_.keys(hw2, 1),
                        writes=[(p.pb[sb_], None)], inc=not diag)
                    if diag:
                        di = kt - 4 * tg
                        b.op("tensor", lambda e, di=di, sb_=sb_: e.matmul(
                            p.pb[sb_][:, 0:512], p.ident[:, :],
                            p.maskt[:, di * 512:(di + 1) * 512],
                            start=False, stop=True),
                            reads=[p.maskt, p.ident], writes=[(p.pb[sb_], None)])

                def ex_emit(kt):
                    sb_ = kt % 2
                    pst = pstrip.bf(kt % 2, 0, 512)
                    pkey = [pstrip.keys(kt % 2, 1)[0]]
                    b.op("scalar", lambda e, sb_=sb_, pst=pst: e.activation(
                        pst, p.pb[sb_][:, 0:512], AF.Exp, scale=float(1.0 / np.sqrt(128.0))),
                        reads=[(p.pb[sb_], None)], writes=pkey)

                def lav_emit(kt, hw2=hw2, ktmax=ktmax):
                    pst = pstrip.bf(kt % 2, 0, 512)
                    pkey = [pstrip.keys(kt % 2, 1)[0]]
                    b.op("tensor", lambda e, pst=pst, kt=kt, ktmax=ktmax: e.matmul(
                        p.pb[4][:, 0:512], p.ones[:, 0:128], pst,
                        start=(kt == 0), stop=(kt == ktmax - 1), skip_group_check=True),
                        reads=pkey + [p.ones], writes=[(p.pb[4], None)])
                    vap = AP(vw_.ar.t, vw_.base * 1024 + (kt * 4 + hw2) * 128,
                             [[vw_.ar.nslots * 1024, 128], [1, 128]])
                    b.op("tensor", lambda e, vap=vap, pst=pst, kt=kt, ktmax=ktmax: e.matmul(
                        p.pb[3][:, 0:512], vap, pst,
                        start=(kt == 0), stop=(kt == ktmax - 1)),
                        reads=vw_.keys() + pkey, writes=[(p.pb[3], None)])

                sc_emit(0)
                ex_emit(0)
                for kt in range(ktmax):
                    if kt + 1 < ktmax:
                        sc_emit(kt + 1)
                        ex_emit(kt + 1)
                    lav_emit(kt)
                b.op("vector", lambda e: e.reciprocal(ibc.f32(0, 0, 512), p.pb[4][:, 0:512]),
                     reads=[(p.pb[4], None)], writes=ibc.keys())
                b.op("vector", lambda e, h=h, g0=g0: e.tensor_tensor(
                    ctx.bf(h, g0, 512), p.pb[3][:, 0:512], ibc.f32(0, 0, 512), AL.mult),
                    reads=[(p.pb[3], None)] + ibc.keys(), writes=ctx.keys(h, 1))
    dbg["ctx"] = ctx
    tap(b, p, "ctx", ctx, 16)
    dbg["sqs"] = sqs
    dbg["osbc_qkv"] = osbc

    for r in (qw_, kw_, vw_, rotw, vtmp, pstrip, ibc, osbc):
        A.free(r)
    return dbg

def emit_l0_attn_rest(b, p, dbg):
    nc, T, NT = b.nc, p.T, 16
    A = p.arena
    SL_P0, SL_M1, SL_R1 = 0, 1, 2
    SL_S, SL_RRMS, SL_T2, SL_MULT, SL_L = 3, 4, 5, 6, 7
    ctx = dbg["ctx"]
    hq = dbg["hq"]
    sqs = dbg["sqs"]
    xap = lambda c, g0, gn: AP(p.xT, c * T + g0, [[NT * T, 128], [1, gn]])
    A.free(hq)

    colmax_abs(b, p, lambda c: ctx.bf(c, 0, T), NT, SL_T2, sqs, ctx.keys())
    rop(b, p, "vector", lambda e, o, a: e.tensor_scalar(o, a, 1e-5, None, AL.max),
        [SL_T2], SL_T2)
    rop(b, p, "vector", lambda e, o, a: e.reciprocal(o, a), [SL_T2], SL_MULT)
    rop(b, p, "vector", lambda e, o, a: e.tensor_scalar(o, a, 127.0, None, AL.mult),
        [SL_MULT], SL_MULT)
    b.op("vector", lambda e: e.tensor_copy(rowap(p, SL_P0), rowap(p, SL_T2)),
         reads=[rkey(p, SL_T2)], writes=[rkey(p, SL_P0)])
    b.op("vector", lambda e: e.tensor_scalar(
        rowap(p, SL_P0), rowap(p, SL_P0), p.swrow[0:1, 9:10], 1.0 / 127.0,
        AL.mult, AL.mult),
        reads=[rkey(p, SL_P0), (p.swrow, "m1")], writes=[rkey(p, SL_P0)])
    mulbc = A.alloc(2)
    osbc = A.alloc(1)
    tmp4 = A.alloc(4)
    bcast_row(b, p, SL_MULT, T, [(lambda g0, gn: mulbc.f32(0, g0, gn), mulbc.keys())])
    bcast_row(b, p, SL_P0, T, [(lambda g0, gn: osbc.bf(0, g0, gn), osbc.keys())])
    quant_int8_inplace(b, p, ctx, NT, mulbc.f32(0, 0, T), mulbc.keys(), tmp4)
    A.free(mulbc)
    A.free(tmp4)
    dbg["ctxq"] = ctx

    ln0 = A.alloc(16)
    mbc = A.alloc(2)
    rbc = A.alloc(1)
    bcast_row(b, p, SL_M1, T, [(lambda g0, gn: mbc.f32(0, g0, gn), mbc.keys())])
    bcast_row(b, p, SL_R1, T, [(lambda g0, gn: rbc.bf(0, g0, gn), rbc.keys())])
    t2s = A.alloc(4)
    for c in range(NT):
        tf = t2s.f32(2 * (c % 2), 0, T)
        tk = t2s.keys(2 * (c % 2), 2)
        b.op("vector", lambda e, c=c, tf=tf: e.tensor_tensor(
            tf, xap(c, 0, T), mbc.f32(0, 0, T), AL.subtract),
            reads=[(p.xT, c)] + mbc.keys(), writes=tk)
        b.op("vector", lambda e, c=c, tf=tf: e.tensor_tensor(
            ln0.bf(c, 0, T), tf, rbc.bf(0, 0, T), AL.mult),
            reads=tk + rbc.keys(), writes=ln0.keys(c, 1))
    A.free(mbc)
    A.free(rbc)
    A.free(t2s)

    x1 = A.alloc(16)
    sc2 = A.alloc(1)

    def o_evac(bank, dabs, g0):
        c = dabs // 128
        b.op("vector", lambda e, c=c, g0=g0, bank=bank: e.tensor_tensor(
            sc2.bf(0, 0, 512), p.pb[bank][:, 0:512], osbc.bf(0, g0, 512), AL.mult),
            reads=[(p.pb[bank], None)] + osbc.keys(), writes=sc2.keys())
        b.op("vector", lambda e, c=c, g0=g0: e.tensor_tensor(
            x1.bf(c, g0, 512), sc2.bf(0, 0, 512), ln0.bf(c, g0, 512), AL.add),
            reads=sc2.keys() + ln0.keys(c, 1), writes=x1.keys(c, 1))

    proj(b, p, p.wt["ow"], D, D,
         lambda c, g0: (ctx.bf(c, g0, 512), [ctx.keys(c, 1)[0]]),
         o_evac, slab=256, banks=(0, 1, 2, 3))
    A.free(ctx)
    A.free(ln0)
    A.free(osbc)
    A.free(sc2)
    dbg["x1"] = x1
    tap(b, p, "x1", x1, 16)

    if getattr(p, "stg_hook", None):
        p.stg_hook()
        p.stg_hook = None

    sumrow(b, p, lambda c, g0, gn: x1.bf(c, g0, gn), NT, SL_S, x1.keys(),
           square=True, sq_scratch=sqs)
    rop(b, p, "vector", lambda e, o, a: e.tensor_scalar(o, a, 1.0 / D, 1e-5, AL.mult, AL.add),
        [SL_S], SL_RRMS)
    rop(b, p, "scalar", lambda e, o, v: e.activation(o, v, AF.Sqrt), [SL_RRMS], SL_RRMS)
    rop(b, p, "vector", lambda e, o, v: e.reciprocal(o, v), [SL_RRMS], SL_RRMS)

    x2 = A.alloc(16)
    IN2 = 4096
    for th in range(2):
        t0 = th * 512
        r2bc = A.alloc(1)
        bcast_row(b, p, SL_RRMS, 512, [(lambda g0, gn: r2bc.bf(0, g0, gn), r2bc.keys())],
                  t0=t0)
        treg = A.alloc(16)
        def tka(ci, treg=treg):
            return (treg.ar.t, ("tch", treg.base, ci))
        def tapn(ci, g0=0, gn=512, treg=treg):
            return AP(treg.ar.t, treg.base * 1024 + ci * 512 + g0,
                      [[treg.ar.nslots * 1024, 128], [1, gn]])
        wg = [A.alloc(2), A.alloc(2)]
        wv2 = [A.alloc(2), A.alloc(2)]
        sg = A.alloc(1)
        sv = A.alloc(1)
        for di, d0 in enumerate(range(0, IN2, 128)):
            rg_ = wg[di % 2]
            rv_ = wv2[di % 2]
            for (reg, base) in ((rg_, d0), (rv_, IN2 + d0)):
                src = AP(p.wt["up0"].tensor, base, [[8192, 128], [128 * 8192, 16], [1, 128]])
                dst = AP(reg.ar.t, reg.base * 1024,
                         [[reg.ar.nslots * 1024, 128], [128, 16], [1, 128]])
                b.op("sync", lambda e, src=src, dst=dst: e.dma_start(dst, src),
                     reads=[p.wt["up0"]], writes=reg.keys(), dma=True)
            bkA, bkB = di % 2, 2 + di % 2
            for c in range(16):
                for (reg, bk) in ((rg_, bkA), (rv_, bkB)):
                    wap = AP(reg.ar.t, reg.base * 1024 + c * 128,
                             [[reg.ar.nslots * 1024, 128], [1, 128]])
                    b.op("tensor", lambda e, wap=wap, c=c, bk=bk, t0=t0: e.matmul(
                        p.pb[bk][:, 0:512], wap, x1.bf(c, t0, 512),
                        start=(c == 0), stop=(c == 15)),
                        reads=reg.keys() + [x1.keys(c, 1)[0]], writes=[(p.pb[bk], None)],
                        inc=(c == 15))
            b.op("vector", lambda e, bkA=bkA, sg=sg, r2bc=r2bc: e.tensor_tensor(
                sg.bf(0, 0, 512), p.pb[bkA][:, 0:512], r2bc.bf(0, 0, 512), AL.mult),
                reads=[(p.pb[bkA], None)] + r2bc.keys(), writes=sg.keys())
            b.op("scalar", lambda e, sg=sg: e.activation(
                sg.bf(0, 512, 512), sg.bf(0, 0, 512), AF.Silu),
                reads=sg.keys(), writes=sg.keys())
            b.op("vector", lambda e, bkB=bkB, sv=sv, r2bc=r2bc: e.tensor_tensor(
                sv.bf(0, 0, 512), p.pb[bkB][:, 0:512], r2bc.bf(0, 0, 512), AL.mult),
                reads=[(p.pb[bkB], None)] + r2bc.keys(), writes=sv.keys())
            b.op("vector", lambda e, di=di, sg=sg, sv=sv, tapn=tapn: e.tensor_tensor(
                tapn(di), sg.bf(0, 512, 512), sv.bf(0, 0, 512), AL.mult),
                reads=sg.keys() + sv.keys(), writes=[tka(di)])
        for r in wg + wv2 + [sg, sv]:
            A.free(r)

        colmax_abs(b, p, lambda ci: tapn(ci), 32, SL_T2, sqs,
                   [tka(ci) for ci in range(32)], T=512)
        rop(b, p, "vector", lambda e, o, a: e.tensor_scalar(o, a, 1e-5, None, AL.max),
            [SL_T2], SL_T2)
        rop(b, p, "vector", lambda e, o, a: e.reciprocal(o, a), [SL_T2], SL_MULT)
        rop(b, p, "vector", lambda e, o, a: e.tensor_scalar(o, a, 127.0, None, AL.mult),
            [SL_MULT], SL_MULT)
        b.op("vector", lambda e: e.tensor_copy(rowap(p, SL_P0, 0, 512),
                                               rowap(p, SL_T2, 0, 512)),
             reads=[rkey(p, SL_T2)], writes=[rkey(p, SL_P0)])
        b.op("vector", lambda e: e.tensor_scalar(
            rowap(p, SL_P0, 0, 512), rowap(p, SL_P0, 0, 512), p.swrow[0:1, 10:11],
            1.0 / 127.0, AL.mult, AL.mult),
            reads=[rkey(p, SL_P0), (p.swrow, "m2")], writes=[rkey(p, SL_P0)])
        mb2 = A.alloc(1)
        ob2 = A.alloc(1)
        bcast_row(b, p, SL_MULT, 512, [(lambda g0, gn: mb2.f32(0, g0, gn), mb2.keys())])
        bcast_row(b, p, SL_P0, 512, [(lambda g0, gn: ob2.bf(0, g0, gn), ob2.keys())])
        tq4 = A.alloc(2)
        for ci in range(32):
            tf = tq4.f32(ci % 2, 0, 512)
            tk2 = [tq4.keys(ci % 2, 1)[0]]
            b.op("vector", lambda e, ci=ci, tf=tf, tapn=tapn, mb2=mb2: e.tensor_tensor(
                tf, tapn(ci), mb2.f32(0, 0, 512), AL.mult),
                reads=[tka(ci)] + mb2.keys(), writes=tk2)
            b.op("vector", lambda e, ci=ci, tf=tf, tapn=tapn: e.tensor_scalar(
                tapn(ci), tf, MAGIC, -MAGIC, AL.add, AL.add),
                reads=tk2, writes=[tka(ci)])
        A.free(mb2)
        A.free(tq4)

        sc3 = A.alloc(1)

        def dn_evac(bank, dabs, g0, th=th, t0=t0, ob2=ob2, sc3=sc3):
            c = dabs // 128
            b.op("vector", lambda e, bank=bank: e.tensor_tensor(
                sc3.bf(0, 0, 512), p.pb[bank][:, 0:512], ob2.bf(0, 0, 512), AL.mult),
                reads=[(p.pb[bank], None)] + ob2.keys(), writes=sc3.keys())
            b.op("vector", lambda e, c=c: e.tensor_tensor(
                x2.bf(c, t0, 512), sc3.bf(0, 0, 512), x1.bf(c, t0, 512), AL.add),
                reads=sc3.keys() + x1.keys(c, 1), writes=[x2.keys(c, 1)[0]])

        projT(b, p, p.wt["dn0"], IN2, D,
              lambda c, g0: (tapn(c), [tka(c)]), dn_evac, 512, slab=128,
              banks=(0, 1, 2, 3))
        A.free(sc3)
        A.free(ob2)
        A.free(r2bc)
        A.free(treg)
    A.free(x1)
    dbg["x2"] = x2
    tap(b, p, "x2", x2, 16)

    sumrow(b, p, lambda c, g0, gn: x2.bf(c, g0, gn), NT, SL_S, x2.keys())
    sumrow(b, p, lambda c, g0, gn: x2.bf(c, g0, gn), NT, SL_T2, x2.keys(),
           square=True, sq_scratch=sqs)
    ln_rows(b, p, SL_S, SL_T2, SL_M1, SL_R1, D)
    mbc2 = A.alloc(2)
    rbc2 = A.alloc(1)
    bcast_row(b, p, SL_M1, T, [(lambda g0, gn: mbc2.f32(0, g0, gn), mbc2.keys())])
    bcast_row(b, p, SL_R1, T, [(lambda g0, gn: rbc2.bf(0, g0, gn), rbc2.keys())])
    t3s = A.alloc(4)
    for c in range(NT):
        tf = t3s.f32(2 * (c % 2), 0, T)
        tk3 = t3s.keys(2 * (c % 2), 2)
        b.op("vector", lambda e, c=c, tf=tf: e.tensor_tensor(
            tf, x2.bf(c, 0, T), mbc2.f32(0, 0, T), AL.subtract),
            reads=x2.keys(c, 1) + mbc2.keys(), writes=tk3)
        b.op("vector", lambda e, c=c, tf=tf: e.tensor_tensor(
            tf, tf, rbc2.bf(0, 0, T), AL.mult),
            reads=tk3 + rbc2.keys(), writes=tk3)
        b.op("vector", lambda e, c=c, tf=tf: e.tensor_tensor(
            xap(c, 0, T), xap(c, 0, T), tf, AL.add),
            reads=tk3 + [(p.xT, c)], writes=[(p.xT, c)])
    A.free(mbc2)
    A.free(rbc2)
    A.free(t3s)
    A.free(x2)
    A.free(sqs)


def projT(b, p, wt, CIN, DOUT, xr, evac, TT_, slab=128, banks=(0, 1, 2, 3)):
    ncin = CIN // 128
    slabslots = cdiv(ncin * slab, 1024)
    wbuf = [p.arena.alloc(slabslots), p.arena.alloc(slabslots)]
    si = 0
    bi = 0
    for d0 in range(0, DOUT, slab):
        sl = min(slab, DOUT - d0)
        reg = wbuf[si % 2]
        si += 1
        src = AP(wt.tensor, d0, [[DOUT, 128], [128 * DOUT, ncin], [1, sl]])
        dst = AP(reg.ar.t, reg.base * 1024,
                 [[reg.ar.nslots * 1024, 128], [sl, ncin], [1, sl]])
        b.op("sync", lambda e, src=src, dst=dst: e.dma_start(dst, src),
             reads=[wt], writes=reg.keys(), dma=True)
        for dc in range(sl // 128):
            bk = banks[bi % len(banks)]
            bi += 1
            for c in range(ncin):
                wap = AP(reg.ar.t, reg.base * 1024 + c * sl + dc * 128,
                         [[reg.ar.nslots * 1024, 128], [1, 128]])
                xap_, xkeys = xr(c, 0)
                b.op("tensor", lambda e, wap=wap, xap_=xap_, c=c, bk=bk: e.matmul(
                    p.pb[bk][:, 0:TT_], wap, xap_,
                    start=(c == 0), stop=(c == ncin - 1)),
                    reads=reg.keys() + xkeys, writes=[(p.pb[bk], None)],
                    inc=(c == ncin - 1))
            evac(bk, d0 + dc * 128, 0)
    for r in wbuf:
        p.arena.free(r)


def host_inmap(inputs, x, c, ncores, T):
    m = {"x": x[c * T:(c + 1) * T, :]}
    m.update(host_constants(c, ncores, T))
    m.update(host_shards(inputs, c, ncores))
    m["nw_attn"] = np.asarray(inputs["L0_attn_attn_norm_w"], dtype=np.float32)
    m["dri"] = np.asarray(inputs["L0_ffn_down_reorder_index"], dtype=np.int32)
    dirs = np.array([[sx_, sy_, sz_] for sx_ in (1.0, -1.0) for sy_ in (1.0, -1.0)
                     for sz_ in (1.0, -1.0)], dtype=np.float32) / np.sqrt(3.0)
    dv = np.zeros((4, 1), np.float32)
    dv[0:3, 0] = dirs[c]
    m["c_dirsh"] = np.repeat(dv.astype(ml_dtypes.bfloat16), 128, axis=1)
    m["c_hs"] = np.asarray(inputs["L1_attn_head_scales"], dtype=np.float32)[c].reshape(1, 1)
    m["s_g"] = np.asarray(inputs["L1_ffn_s_gate"], dtype=np.float32)
    m["s_u"] = np.asarray(inputs["L1_ffn_s_up"], dtype=np.float32)
    m["s_d"] = np.asarray(inputs["L1_ffn_s_down"], dtype=np.float32)
    csh = D // ncores
    m["sh_nw"] = np.asarray(inputs["L0_attn_ffn_norm_w"], dtype=np.float32)[
        None, c * csh:(c + 1) * csh]
    return m


def emit_outer_preln(b, p, out_reg, sqs):
    T, NT = p.T, 16
    A = p.arena
    SL_M1, SL_R1, SL_S, SL_T2 = 1, 2, 3, 5
    xap = lambda c, g0, gn: AP(p.xT, c * T + g0, [[NT * T, 128], [1, gn]])
    xkeys = [(p.xT, f) for f in range(NT)]
    sumrow(b, p, xap, NT, SL_S, xkeys)
    sumrow(b, p, xap, NT, SL_T2, xkeys, square=True, sq_scratch=sqs)
    ln_rows(b, p, SL_S, SL_T2, SL_M1, SL_R1, D)
    mbc = A.alloc(2)
    rbc = A.alloc(1)
    bcast_row(b, p, SL_M1, T, [(lambda g0, gn: mbc.f32(0, g0, gn), mbc.keys())])
    bcast_row(b, p, SL_R1, T, [(lambda g0, gn: rbc.bf(0, g0, gn), rbc.keys())])
    tmp4 = A.alloc(4)
    for c in range(NT):
        tf = tmp4.f32(2 * (c % 2), 0, T)
        tk = tmp4.keys(2 * (c % 2), 2)
        b.op("vector", lambda e, c=c, tf=tf, mbc=mbc: e.tensor_tensor(
            tf, xap(c, 0, T), mbc.f32(0, 0, T), AL.subtract),
            reads=[(p.xT, c)] + mbc.keys(), writes=tk)
        b.op("vector", lambda e, c=c, tf=tf, rbc=rbc, out_reg=out_reg: e.tensor_tensor(
            out_reg.bf(c, 0, T), tf, rbc.bf(0, 0, T), AL.mult),
            reads=tk + rbc.keys(), writes=out_reg.keys(c, 1))
    A.free(mbc)
    A.free(rbc)
    A.free(tmp4)


def emit_outer_postln(b, p, y_reg, sqs, nch=16):
    T, NT = p.T, 16
    A = p.arena
    SL_M1, SL_R1, SL_S, SL_T2 = 1, 2, 3, 5
    xap = lambda c, g0, gn: AP(p.xT, c * T + g0, [[NT * T, 128], [1, gn]])
    sumrow(b, p, lambda c, g0, gn: y_reg.bf(c, g0, gn), nch, SL_S, y_reg.keys())
    sumrow(b, p, lambda c, g0, gn: y_reg.bf(c, g0, gn), nch, SL_T2, y_reg.keys(),
           square=True, sq_scratch=sqs)
    ln_rows(b, p, SL_S, SL_T2, SL_M1, SL_R1, D)
    mbc = A.alloc(2)
    rbc = A.alloc(1)
    bcast_row(b, p, SL_M1, T, [(lambda g0, gn: mbc.f32(0, g0, gn), mbc.keys())])
    bcast_row(b, p, SL_R1, T, [(lambda g0, gn: rbc.bf(0, g0, gn), rbc.keys())])
    t3s = A.alloc(4)
    for c in range(NT):
        tf = t3s.f32(2 * (c % 2), 0, T)
        tk3 = t3s.keys(2 * (c % 2), 2)
        b.op("vector", lambda e, c=c, tf=tf, mbc=mbc, y_reg=y_reg: e.tensor_tensor(
            tf, y_reg.bf(c, 0, T), mbc.f32(0, 0, T), AL.subtract),
            reads=y_reg.keys(c, 1) + mbc.keys(), writes=tk3)
        b.op("vector", lambda e, c=c, tf=tf, rbc=rbc: e.tensor_tensor(
            tf, tf, rbc.bf(0, 0, T), AL.mult),
            reads=tk3 + rbc.keys(), writes=tk3)
        b.op("vector", lambda e, c=c, tf=tf: e.tensor_tensor(
            xap(c, 0, T), xap(c, 0, T), tf, AL.add),
            reads=tk3 + [(p.xT, c)], writes=[(p.xT, c)])
    A.free(mbc)
    A.free(rbc)
    A.free(t3s)


def emit_l0_ffn(b, p):
    nc, T, NT = b.nc, p.T, 16
    A = p.arena
    SL_P0, SL_M1, SL_R1 = 0, 1, 2
    SL_S, SL_RRMS, SL_T2, SL_MULT, SL_L = 3, 4, 5, 6, 7
    HID = 5632
    NH2 = HID // 128
    sqs = A.alloc(2)

    z = A.alloc(16)
    emit_outer_preln(b, p, z, sqs)

    colmax_abs(b, p, lambda c: z.bf(c, 0, T), NT, SL_T2, sqs, z.keys())
    rop(b, p, "vector", lambda e, o, a: e.reciprocal(o, a), [SL_T2], SL_MULT)
    rop(b, p, "vector", lambda e, o, a: e.tensor_scalar(o, a, 224.0, None, AL.mult),
        [SL_MULT], SL_MULT)
    rop(b, p, "vector", lambda e, o, a: e.tensor_scalar(o, a, 1.0 / 50176.0, None, AL.mult),
        [SL_T2], SL_P0)
    i224 = A.alloc(2)
    bcast_row(b, p, SL_MULT, T, [(lambda g0, gn: i224.f32(0, g0, gn), i224.keys())])
    qx8 = A.alloc(8)
    for c in range(NT):
        s8 = qx8.fp8(c // 2, (c % 2) * 1024, 1024)
        k8 = [qx8.keys(c // 2, 1)[0]]
        b.op("vector", lambda e, c=c, s8=s8, i224=i224: e.tensor_tensor(
            s8, z.bf(c, 0, T), i224.f32(0, 0, T), AL.mult),
            reads=z.keys(c, 1) + i224.keys(), writes=k8)
    A.free(i224)
    A.free(z)
    qx = qx8

    y = A.alloc(16)
    t8 = A.alloc(11)
    for th in range(2):
        t0 = th * 512
        treg = A.alloc(22)
        mr1 = A.alloc(1)
        bcast_row(b, p, SL_P0, 512, [(lambda g0, gn, mr1=mr1: mr1.f32(0, g0, gn), mr1.keys())], t0=t0)

        def tka(ci, treg=treg):
            return (treg.ar.t, treg.base + ci // 2)

        def tapn(ci, g0=0, gn=512, treg=treg):
            return AP(treg.ar.t, treg.base * 1024 + ci * 512 + g0,
                      [[treg.ar.nslots * 1024, 128], [1, gn]])

        wg = [A.alloc(2), A.alloc(2)]
        wu = [A.alloc(2), A.alloc(2)]
        sg = A.alloc(1)
        sv = A.alloc(1)
        for di in range(NH2):
            d0 = di * 128
            rg_ = wg[di % 2]
            ru_ = wu[di % 2]
            for (reg, wtn) in ((rg_, "g8"), (ru_, "u8")):
                src = AP(p.wt[wtn].tensor, d0, [[HID, 128], [128 * HID, 16], [1, 128]])
                dst = AP(reg.ar.t, reg.base * 1024,
                         [[reg.ar.nslots * 1024, 128], [128, 16], [1, 128]])
                b.op("sync", lambda e, src=src, dst=dst: e.dma_start(dst, src),
                     reads=[p.wt[wtn]], writes=reg.keys(), dma=True)
            bkA, bkB = di % 2, 2 + di % 2
            for c in range(16):
                for (reg, bk) in ((rg_, bkA), (ru_, bkB)):
                    wap = AP(reg.ar.t, reg.base * 1024 + c * 128,
                             [[reg.ar.nslots * 1024, 128], [1, 128]])
                    b.op("tensor", lambda e, wap=wap, c=c, bk=bk, t0=t0, reg=reg: e.matmul(
                        p.pb[bk][:, 0:512], wap,
                        qx.fp8(c // 2, (c % 2) * 1024 + t0, 512),
                        start=(c == 0), stop=(c == 15)),
                        reads=reg.keys() + [qx.keys(c // 2, 1)[0]], writes=[(p.pb[bk], None)],
                        inc=(c == 15))
            rc = (RMX_OFF_G["g8"] + di, RMX_OFF_G["u8"] + di)
            b.op("vector", lambda e, bkA=bkA, rc=rc, sg=sg, mr1=mr1: e.scalar_tensor_tensor(
                sg.bf(0, 0, 512), p.pb[bkA][:, 0:512], p.rmx[:, rc[0]:rc[0] + 1],
                mr1.f32(0, 0, 512), AL.mult, AL.mult),
                reads=[(p.pb[bkA], None), (p.rmx, "g")] + mr1.keys(), writes=sg.keys())
            b.op("scalar", lambda e, sg=sg: e.activation(
                sg.bf(0, 512, 512), sg.bf(0, 0, 512), AF.Silu),
                reads=sg.keys(), writes=sg.keys())
            b.op("vector", lambda e, bkB=bkB, rc=rc, sv=sv, mr1=mr1: e.scalar_tensor_tensor(
                sv.bf(0, 0, 512), p.pb[bkB][:, 0:512], p.rmx[:, rc[1]:rc[1] + 1],
                mr1.f32(0, 0, 512), AL.mult, AL.mult),
                reads=[(p.pb[bkB], None), (p.rmx, "g")] + mr1.keys(), writes=sv.keys())
            b.op("vector", lambda e, di=di, sg=sg, sv=sv, tapn=tapn: e.tensor_tensor(
                tapn(di), sg.bf(0, 512, 512), sv.bf(0, 0, 512), AL.mult),
                reads=sg.keys() + sv.keys(), writes=[tka(di)])
        for r in wg + wu + [sg, sv]:
            A.free(r)

        colmax_abs(b, p, lambda ci: tapn(ci), NH2, SL_T2, sqs,
                   [tka(ci) for ci in range(NH2)], T=512)
        rop(b, p, "vector", lambda e, o, a: e.reciprocal(
            rowap(p, SL_MULT, 0, 512), rowap(p, SL_T2, 0, 512)), [SL_T2], SL_MULT)
        rop(b, p, "vector", lambda e, o, a: e.tensor_scalar(
            rowap(p, SL_MULT, 0, 512), rowap(p, SL_MULT, 0, 512), 224.0, None, AL.mult),
            [SL_MULT], SL_MULT)
        rop(b, p, "vector", lambda e, o, a: e.tensor_scalar(
            rowap(p, SL_P0, 0, 512), rowap(p, SL_T2, 0, 512), 1.0 / 50176.0, None, AL.mult),
            [SL_T2], SL_P0)
        i224b = A.alloc(1)
        mr2 = A.alloc(1)
        bcast_row(b, p, SL_MULT, 512, [(lambda g0, gn, i224b=i224b: i224b.f32(0, g0, gn), i224b.keys())])
        bcast_row(b, p, SL_P0, 512, [(lambda g0, gn, mr2=mr2: mr2.f32(0, g0, gn), mr2.keys())])
        for ci in range(NH2):
            s8 = t8.fp8(ci // 4, (ci % 4) * 512, 512)
            k8 = [t8.keys(ci // 4, 1)[0]]
            b.op("vector", lambda e, ci=ci, s8=s8, tapn=tapn, i224b=i224b: e.tensor_tensor(
                s8, tapn(ci), i224b.f32(0, 0, 512), AL.mult),
                reads=[tka(ci)] + i224b.keys(), writes=k8)
        A.free(i224b)
        A.free(treg)

        def dn_evac(bank, dabs, g0, t0=t0, mr2=mr2):
            c = dabs // 128
            rc = RMX_OFF_G["d8"] + c // 8
            rcol = p.rmx[:, 88 + c:88 + c + 1]
            b.op("vector", lambda e, c=c, bank=bank, rcol=rcol, mr2=mr2, t0=t0: e.scalar_tensor_tensor(
                y.bf(c, t0, 512), p.pb[bank][:, 0:512], rcol, mr2.f32(0, 0, 512),
                AL.mult, AL.mult),
                reads=[(p.pb[bank], None), (p.rmx, "g")] + mr2.keys(),
                writes=[y.keys(c, 1)[0]])

        projT(b, p, p.wt["d8"], HID, D,
              lambda c, g0, t8=t8: (t8.fp8(c // 4, (c % 4) * 512, 512),
                                    [t8.keys(c // 4, 1)[0]]),
              dn_evac, 512, slab=128, banks=(0, 1, 2, 3))
        A.free(mr2)
        A.free(mr1)
    A.free(t8)
    A.free(qx)

    emit_outer_postln(b, p, y, sqs)
    A.free(y)
    A.free(sqs)


RMX_OFF_G = {"g8": 0, "u8": 44, "d8": 88}


def emit_l1_ffn(b, p):
    nc, T, NT = b.nc, p.T, 16
    A = p.arena
    HID = 5632
    NH2 = HID // 128
    sqs = A.alloc(2)

    sg_h = nc.dram_tensor("s_g", [HID], F32, kind="ExternalInput").ap()
    su_h = nc.dram_tensor("s_u", [HID], F32, kind="ExternalInput").ap()
    sd_h = nc.dram_tensor("s_d", [D], F32, kind="ExternalInput").ap()
    p.sgcol = p.sb("sgcol", [128, 44], F32)
    p.sucol = p.sb("sucol", [128, 44], F32)
    p.sdcol = p.sb("sdcol", [128, 16], F32)
    for t, h, n in ((p.sgcol, sg_h, 44), (p.sucol, su_h, 44), (p.sdcol, sd_h, 16)):
        b.op("scalar", lambda e, t=t, h=h, n=n: e.dma_start(
            AP(t, 0, [[n, 128], [1, n]]),
            AP(h.tensor, 0, [[1, 128], [128, n]]),
            allow_slow_non_contiguous=True), writes=[t], dma=True)

    tapx(b, p, "l1f_x")
    z = A.alloc(16)
    emit_outer_preln(b, p, z, sqs)
    tap(b, p, "l1f_z", z, 16)

    y = A.alloc(16)
    for th in range(2):
        t0 = th * 512
        treg = A.alloc(22)

        def tka(ci, treg=treg):
            return (treg.ar.t, treg.base + ci // 2)

        def tapn(ci, g0=0, gn=512, treg=treg):
            return AP(treg.ar.t, treg.base * 1024 + ci * 512 + g0,
                      [[treg.ar.nslots * 1024, 128], [1, gn]])

        wg = [A.alloc(2), A.alloc(2)]
        wu = [A.alloc(2), A.alloc(2)]
        sg = A.alloc(1)
        sv = A.alloc(1)
        for di in range(NH2):
            d0 = di * 128
            rg_ = wg[di % 2]
            ru_ = wu[di % 2]
            for (reg, wtn) in ((rg_, "ig"), (ru_, "iu")):
                src = AP(p.wt[wtn].tensor, d0, [[HID, 128], [128 * HID, 16], [1, 128]])
                dst = AP(reg.ar.t, reg.base * 1024,
                         [[reg.ar.nslots * 1024, 128], [128, 16], [1, 128]])
                b.op("sync", lambda e, src=src, dst=dst: e.dma_start(dst, src),
                     reads=[p.wt[wtn]], writes=reg.keys(), dma=True)
            bkA, bkB = di % 2, 2 + di % 2
            for c in range(16):
                for (reg, bk) in ((rg_, bkA), (ru_, bkB)):
                    wap = AP(reg.ar.t, reg.base * 1024 + c * 128,
                             [[reg.ar.nslots * 1024, 128], [1, 128]])
                    b.op("tensor", lambda e, wap=wap, c=c, bk=bk, t0=t0, reg=reg: e.matmul(
                        p.pb[bk][:, 0:512], wap, z.bf(c, t0, 512),
                        start=(c == 0), stop=(c == 15)),
                        reads=reg.keys() + [z.keys(c, 1)[0]], writes=[(p.pb[bk], None)],
                        inc=(c == 15))
            b.op("vector", lambda e, bkA=bkA, di=di, sg=sg: e.tensor_scalar(
                sg.bf(0, 0, 512), p.pb[bkA][:, 0:512], p.sgcol[:, di:di + 1], None, AL.mult),
                reads=[(p.pb[bkA], None), p.sgcol], writes=sg.keys())
            b.op("scalar", lambda e, sg=sg: e.activation(
                sg.bf(0, 512, 512), sg.bf(0, 0, 512), AF.Silu),
                reads=sg.keys(), writes=sg.keys())
            b.op("vector", lambda e, bkB=bkB, di=di, sv=sv: e.tensor_scalar(
                sv.bf(0, 0, 512), p.pb[bkB][:, 0:512], p.sucol[:, di:di + 1], None, AL.mult),
                reads=[(p.pb[bkB], None), p.sucol], writes=sv.keys())
            b.op("vector", lambda e, di=di, sg=sg, sv=sv, tapn=tapn: e.tensor_tensor(
                tapn(di), sg.bf(0, 512, 512), sv.bf(0, 0, 512), AL.mult),
                reads=sg.keys() + sv.keys(), writes=[tka(di)])
        for r in wg + wu + [sg, sv]:
            A.free(r)

        def dn_evac(bank, dabs, g0, t0=t0):
            c = dabs // 128
            b.op("vector", lambda e, c=c, bank=bank, t0=t0: e.tensor_scalar(
                y.bf(c, t0, 512), p.pb[bank][:, 0:512], p.sdcol[:, c:c + 1], None, AL.mult),
                reads=[(p.pb[bank], None), p.sdcol], writes=[y.keys(c, 1)[0]])

        projT(b, p, p.wt["id8"], HID, D,
              lambda c, g0: (tapn(c), [tka(c)]), dn_evac, 512, slab=128,
              banks=(0, 1, 2, 3))
        if th == 1:
            tap(b, p, "l1f_t1", treg, 22)
        A.free(treg)
    A.free(z)
    tap(b, p, "l1f_y", y, 16)
    emit_outer_postln(b, p, y, sqs)
    A.free(y)
    A.free(sqs)


def emit_l1_attn(b, p, ncores):
    nc, T, NT = b.nc, p.T, 16
    A = p.arena
    SL_P0, SL_M1, SL_R1 = 0, 1, 2
    SL_S, SL_RRMS, SL_T2, SL_MULT, SL_L = 3, 4, 5, 6, 7
    rg = [list(range(ncores))]
    DH = 256
    NG = S // 512

    dirs_h = nc.dram_tensor("c_dirsh", [4, 128], BF16, kind="ExternalInput").ap()
    hs_h = nc.dram_tensor("c_hs", [1, 1], F32, kind="ExternalInput").ap()
    cos1 = nc.dram_tensor("c_cos1", [128, T], BF16, kind="ExternalInput").ap()
    sin1 = nc.dram_tensor("c_sin1", [128, T], BF16, kind="ExternalInput").ap()
    mask01 = nc.dram_tensor("c_mask01", [4, 128, 512], BF16, kind="ExternalInput").ap()
    p.dirsv = p.sb("dirsv", [4, 128], BF16)
    p.hsv = p.sb("hsv", [1, 1], F32)
    p.cos1 = p.sb("cos1", [128, T], BF16)
    p.sin1 = p.sb("sin1", [128, T], BF16)
    p.mask01t = p.sb("mask01t", [128, 4 * 512], BF16)
    for t, h in ((p.dirsv, dirs_h), (p.hsv, hs_h), (p.cos1, cos1), (p.sin1, sin1)):
        b.op("scalar", lambda e, t=t, h=h: e.dma_start(t[:, :] if t.shape[0] > 1 else t[0:1, :], h),
             writes=[t], dma=True)
    b.op("scalar", lambda e: e.dma_start(
        AP(p.mask01t, 0, [[2048, 128], [512, 4], [1, 512]]),
        AP(mask01.tensor, 0, [[512, 128], [65536, 4], [1, 512]])),
        writes=[p.mask01t], dma=True)

    a2aq_i = nc.dram_tensor("a2aq_i", [ncores, 2, 128, T], BF16).ap()
    a2aq_o = nc.dram_tensor("a2aq_o", [ncores, 2, 128, T], BF16).ap()
    KVSLOT = 4 * 128 * T
    a2akv_i = nc.dram_tensor("a2akv_i", [ncores, KVSLOT], BF16).ap()
    a2akv_o = nc.dram_tensor("a2akv_o", [ncores, KVSLOT], BF16).ap()
    a2ac_i = nc.dram_tensor("a2ac_i", [ncores, 2, 128, T], BF16).ap()
    a2ac_o = nc.dram_tensor("a2ac_o", [ncores, 2, 128, T], BF16).ap()

    sqs = A.alloc(2)
    tapx(b, p, "l1a_x")
    z = A.alloc(16)
    emit_outer_preln(b, p, z, sqs)
    tap(b, p, "l1a_z", z, 16)

    qA = A.alloc(8)
    qB = A.alloc(8)
    kA = A.alloc(2)
    kB = A.alloc(2)
    vreg = A.alloc(4)
    vtmp = A.alloc(1)

    def xr(c, g0):
        return z.bf(c, g0, 512), [z.keys(c, 1)[0]]

    def q_evac(bank, dabs, g0):
        dc = dabs // 128
        h2, half = dc // 2, dc % 2
        dst = qA if half == 0 else qB
        b.op("vector", lambda e, dst=dst, h2=h2, g0=g0, bank=bank: e.tensor_copy(
            dst.bf(h2, g0, 512), p.pb[bank][:, 0:512]),
            reads=[(p.pb[bank], None)], writes=dst.keys(h2, 1))

    proj(b, p, p.wt["wq"], D, D, xr, q_evac, slab=256, banks=(0, 1, 2, 3))

    def k_evac(bank, dabs, g0):
        dc = dabs // 128
        kv2, half = dc // 2, dc % 2
        dst = kA if half == 0 else kB
        b.op("vector", lambda e, dst=dst, kv2=kv2, g0=g0, bank=bank: e.tensor_copy(
            dst.bf(kv2, g0, 512), p.pb[bank][:, 0:512]),
            reads=[(p.pb[bank], None)], writes=dst.keys(kv2, 1))

    proj(b, p, p.wt["wk"], D, 512, xr, k_evac, slab=256, banks=(0, 1, 2, 3))

    def v_evac(bank, dabs, g0):
        dc = dabs // 128
        b.op("vector", lambda e, g0=g0, bank=bank: e.tensor_copy(
            vtmp.bf(0, 0, 512), p.pb[bank][:, 0:512]),
            reads=[(p.pb[bank], None)], writes=vtmp.keys())
        for q4 in range(4):
            b.op("tensor", lambda e, q4=q4: e.transpose(
                p.pbtr[:, q4 * 128:(q4 + 1) * 128], vtmp.bf(0, q4 * 128, 128),
                p.ident[:, :]),
                reads=vtmp.keys() + [p.ident], writes=[(p.pb[7], None)],
                inc=(q4 == 3))
        dst = AP(vreg.ar.t, vreg.base * 1024 + (g0 // 128) * 512 + dc * 128,
                 [[vreg.ar.nslots * 1024, 128], [512, 4], [1, 128]])
        b.op("vector", lambda e, dst=dst: e.tensor_copy(dst, p.pbtr[:, 0:512]),
             reads=[(p.pb[7], None)], writes=vreg.keys())

    proj(b, p, p.wt["wv"], D, 512, xr, v_evac, slab=256, banks=(0, 1, 2, 3))
    A.free(z)

    rtmp1 = A.alloc(8)
    rtmp2 = A.alloc(8)
    for (ra, rb, t1, t2, nh) in ((qA, qB, rtmp1, rtmp2, 8), (kA, kB, rtmp1, rtmp2, 2)):
        a3 = lambda r, nh=nh: AP(r.ar.t, r.base * 1024, [[r.ar.nslots * 1024, 128], [1024, nh], [1, T]])
        cosb = AP(p.cos1, 0, [[T, 128], [0, nh], [1, T]])
        sinb = AP(p.sin1, 0, [[T, 128], [0, nh], [1, T]])
        b.op("vector", lambda e, ra=ra, t1=t1, sinb=sinb, a3=a3: e.tensor_tensor(
            a3(t1), a3(ra), sinb, AL.mult), reads=ra.keys() + [p.sin1], writes=t1.keys())
        b.op("vector", lambda e, rb=rb, t2=t2, sinb=sinb, a3=a3: e.tensor_tensor(
            a3(t2), a3(rb), sinb, AL.mult), reads=rb.keys() + [p.sin1], writes=t2.keys())
        b.op("vector", lambda e, ra=ra, cosb=cosb, a3=a3: e.tensor_tensor(
            a3(ra), a3(ra), cosb, AL.mult), reads=ra.keys() + [p.cos1], writes=ra.keys())
        b.op("vector", lambda e, rb=rb, cosb=cosb, a3=a3: e.tensor_tensor(
            a3(rb), a3(rb), cosb, AL.mult), reads=rb.keys() + [p.cos1], writes=rb.keys())
        b.op("vector", lambda e, ra=ra, t2=t2, a3=a3: e.tensor_tensor(
            a3(ra), a3(ra), a3(t2), AL.subtract), reads=ra.keys() + t2.keys(), writes=ra.keys())
        b.op("vector", lambda e, rb=rb, t1=t1, a3=a3: e.tensor_tensor(
            a3(rb), a3(rb), a3(t1), AL.add), reads=rb.keys() + t1.keys(), writes=rb.keys())
    A.free(rtmp1)
    A.free(rtmp2)
    tap(b, p, "l1a_qA", qA, 8)
    tap(b, p, "l1a_qB", qB, 8)
    tap(b, p, "l1a_kA", kA, 2)
    tap(b, p, "l1a_kB", kB, 2)
    tap(b, p, "l1a_v", vreg, 4)

    for h2 in range(ncores):
        for (reg, half) in ((qA, 0), (qB, 1)):
            dst = AP(a2aq_i.tensor, (h2 * 2 + half) * 128 * T, [[T, 128], [1, T]])
            b.op("sync", lambda e, dst=dst, reg=reg, h2=h2: e.dma_start(dst, reg.bf(h2, 0, T)),
                 reads=reg.keys(h2, 1), writes=[(a2aq_i, h2 * 2 + half)], dma=True)
    for j in range(ncores):
        kv2 = j // 4
        for (reg, half) in ((kA, 0), (kB, 1)):
            dst = AP(a2akv_i.tensor, j * KVSLOT + half * 128 * T, [[T, 128], [1, T]])
            b.op("sync", lambda e, dst=dst, reg=reg, kv2=kv2: e.dma_start(dst, reg.bf(kv2, 0, T)),
                 reads=reg.keys(kv2, 1), writes=[(a2akv_i, (j, half))], dma=True)
        src = AP(vreg.ar.t, vreg.base * 1024 + kv2 * 256,
                 [[vreg.ar.nslots * 1024, 128], [512, 8], [1, 256]])
        dst = AP(a2akv_i.tensor, j * KVSLOT + 2 * 128 * T, [[2048, 128], [256, 8], [1, 256]])
        b.op("sync", lambda e, dst=dst, src=src: e.dma_start(dst, src),
             reads=vreg.keys(), writes=[(a2akv_i, (j, "v"))], dma=True)
    b.op("gpsimd", lambda e: e.collective_compute(
        "AllToAll", AL.bypass, replica_groups=rg, ins=[a2aq_i], outs=[a2aq_o]),
        reads=[(a2aq_i, i) for i in range(2 * ncores)], writes=[a2aq_o], tag="cc")
    b.op("gpsimd", lambda e: e.collective_compute(
        "AllToAll", AL.bypass, replica_groups=rg, ins=[a2akv_i], outs=[a2akv_o]),
        reads=[(a2akv_i, (j, x)) for j in range(ncores) for x in (0, 1, "v")],
        writes=[a2akv_o], tag="cc")
    for r in (qA, qB, kA, kB, vreg, vtmp):
        A.free(r)

    kAf = A.alloc(8)
    kBf = A.alloc(8)
    vf = A.alloc(16)
    for r in range(ncores):
        for (reg, half) in ((kAf, 0), (kBf, 1)):
            src = AP(a2akv_o.tensor, r * KVSLOT + half * 128 * T, [[T, 128], [1, T]])
            dst = AP(reg.ar.t, reg.base * 1024 + r * T, [[reg.ar.nslots * 1024, 128], [1, T]])
            b.op("sync", lambda e, dst=dst, src=src: e.dma_start(dst, src),
                 reads=[a2akv_o], writes=[(reg.ar.t, reg.base + r)], dma=True)
        src = AP(a2akv_o.tensor, r * KVSLOT + 2 * 128 * T, [[2048, 128], [256, 8], [1, 256]])
        dst = AP(vf.ar.t, vf.base * 1024 + r * 8 * 256, [[vf.ar.nslots * 1024, 128], [256, 8], [1, 256]])
        b.op("sync", lambda e, dst=dst, src=src: e.dma_start(dst, src),
             reads=[a2akv_o], writes=vf.keys(2 * r, 2), dma=True)
    kkeys = kAf.keys() + kBf.keys()
    tap(b, p, "l1a_kAf", kAf, 8)
    tap(b, p, "l1a_vf", vf, 16)

    p.kpcol = p.sb("kpcol", [128, 64], F32)
    NKT = S // 128
    for kt in range(NKT):
        b.op("tensor", lambda e, kt=kt: e.matmul(
            p.pb[4][:, kt % 32:kt % 32 + 1],
            AP(kAf.ar.t, kAf.base * 1024 + kt * 128, [[kAf.ar.nslots * 1024, 3], [1, 128]]),
            p.dirsv[0:3, 0:1], start=True, stop=True, skip_group_check=True),
            reads=kkeys + [p.dirsv], writes=[(p.pb[4], None)], inc=(kt % 32 == 31))
        if kt % 32 == 31:
            b.op("vector", lambda e, kt=kt: e.tensor_copy(
                p.kpcol[:, kt - 31 + 0:kt + 1], p.pb[4][:, 0:32]),
                reads=[(p.pb[4], None)], writes=[p.kpcol])
    p.scol = p.sb("scol", [128, 1], F32)
    b.op("tensor", lambda e: e.matmul(
        p.pb[4][:, 0:1], p.onesf[0:1, 0:128], p.hsv[0:1, 0:1],
        start=True, stop=True, skip_group_check=True),
        reads=[p.hsv, p.onesf], writes=[(p.pb[4], None)])
    b.op("vector", lambda e: e.tensor_copy(p.scol[:, :], p.pb[4][:, 0:1]),
         reads=[(p.pb[4], None)], writes=[p.scol])

    ctxA = A.alloc(8)
    ctxB = A.alloc(8)
    qg = A.alloc(2)
    pstrip = A.alloc(4)
    qpb = A.alloc(2)
    outm = A.alloc(1)
    ibc = A.alloc(2)
    SL_QP = 3
    for g in range(NG):
        r, half = g // 2, g % 2
        gs = g % 2
        for cchunk in range(2):
            src = AP(a2aq_o.tensor, (r * 2 + cchunk) * 128 * T + half * 512,
                     [[T, 128], [1, 512]])
            b.op("sync", lambda e, src=src, cchunk=cchunk, gs=gs: e.dma_start(
                qg.bf(gs, cchunk * 512, 512), src),
                reads=[a2aq_o], writes=[(qg.ar.t, (qg.base + gs, cchunk))], dma=True)
        qgk = [(qg.ar.t, (qg.base + gs, 0)), (qg.ar.t, (qg.base + gs, 1))]
        b.op("tensor", lambda e, gs=gs: e.matmul(
            p.pb[5][:, 0:512], p.dirsv[0:3, 0:128],
            AP(qg.ar.t, (qg.base + gs) * 1024, [[qg.ar.nslots * 1024, 3], [1, 512]]),
            start=True, stop=True, skip_group_check=True),
            reads=qgk + [p.dirsv], writes=[(p.pb[5], None)])
        b.op("vector", lambda e, gs=gs: e.tensor_copy(qpb.bf(gs, 0, 512), p.pb[5][:, 0:512]),
             reads=[(p.pb[5], None)], writes=qpb.keys(gs, 1))
        ktmax = 4 * (g + 1)
        for path in ("main", "geo"):
            NSL = 4

            def sc_emit(kt, gs=gs, ktmax=ktmax):
                sb_ = kt % 2
                diag = kt >= ktmax - 4
                di = kt - (ktmax - 4)
                b.op("tensor", lambda e, kt=kt, sb_=sb_, diag=diag, gs=gs: e.matmul(
                    p.pb[sb_][:, 0:512],
                    AP(kAf.ar.t, kAf.base * 1024 + kt * 128, [[kAf.ar.nslots * 1024, 128], [1, 128]]),
                    qg.bf(gs, 0, 512), start=True, stop=False),
                    reads=kkeys + qgk, writes=[(p.pb[sb_], None)], inc=False)
                b.op("tensor", lambda e, kt=kt, sb_=sb_, diag=diag, gs=gs: e.matmul(
                    p.pb[sb_][:, 0:512],
                    AP(kBf.ar.t, kBf.base * 1024 + kt * 128, [[kBf.ar.nslots * 1024, 128], [1, 128]]),
                    qg.bf(gs, 512, 512), start=False, stop=not diag),
                    reads=kkeys + qgk, writes=[(p.pb[sb_], None)], inc=not diag)
                if diag:
                    b.op("tensor", lambda e, di=di, sb_=sb_: e.matmul(
                        p.pb[sb_][:, 0:512], p.ident[:, :],
                        p.maskt[:, di * 512:(di + 1) * 512],
                        start=False, stop=True),
                        reads=[p.maskt, p.ident], writes=[(p.pb[sb_], None)])

            def ex_emit(kt, path=path, gs=gs, ktmax=ktmax):
                sb_ = kt % 2
                diag = kt >= ktmax - 4
                di = kt - (ktmax - 4)
                pst = pstrip.bf(kt % NSL, 0, 512)
                pkey = [pstrip.keys(kt % NSL, 1)[0]]
                if path == "main":
                    b.op("scalar", lambda e, sb_=sb_, pst=pst: e.activation(
                        pst, p.pb[sb_][:, 0:512], AF.Exp, scale=float(1.0 / 16.0)),
                        reads=[(p.pb[sb_], None)], writes=pkey)
                else:
                    b.op("scalar", lambda e, kt=kt, pst=pst, gs=gs: e.activation(
                        pst, qpb.bf(gs, 0, 512), AF.Exp, scale=p.kpcol[:, kt:kt + 1]),
                        reads=qpb.keys(gs, 1) + [p.kpcol], writes=pkey)
                    if diag:
                        b.op("vector", lambda e, di=di, pst=pst: e.tensor_tensor(
                            pst, pst, p.mask01t[:, di * 512:(di + 1) * 512], AL.mult),
                            reads=pkey + [p.mask01t], writes=pkey)

            def lav_emit(kt, ktmax=ktmax):
                pst = pstrip.bf(kt % NSL, 0, 512)
                pkey = [pstrip.keys(kt % NSL, 1)[0]]
                b.op("tensor", lambda e, pst=pst, kt=kt, ktmax=ktmax: e.matmul(
                    p.pb[4][:, 0:512], p.ones[:, 0:128], pst,
                    start=(kt == 0), stop=(kt == ktmax - 1), skip_group_check=True),
                    reads=pkey + [p.ones], writes=[(p.pb[4], None)])
                for hc in range(2):
                    vap = AP(vf.ar.t, vf.base * 1024 + kt * 256 + hc * 128,
                             [[vf.ar.nslots * 1024, 128], [1, 128]])
                    b.op("tensor", lambda e, vap=vap, pst=pst, kt=kt, ktmax=ktmax, hc=hc: e.matmul(
                        p.pb[2 + hc][:, 0:512], vap, pst,
                        start=(kt == 0), stop=(kt == ktmax - 1)),
                        reads=vf.keys() + pkey, writes=[(p.pb[2 + hc], None)])

            LA = 1 if path == "main" else 2
            if path == "main":
                sc_emit(0)
            for k0 in range(min(LA, ktmax)):
                ex_emit(k0)
            for kt in range(ktmax):
                nxt = kt + LA
                if nxt < ktmax:
                    if path == "main":
                        sc_emit(nxt)
                    ex_emit(nxt)
                lav_emit(kt)
            ps_ = 0 if path == "main" else 1
            b.op("vector", lambda e, ps_=ps_: e.reciprocal(
                ibc.f32(ps_, 0, 512), p.pb[4][:, 0:512]),
                reads=[(p.pb[4], None)], writes=ibc.keys(ps_, 1))
            for hc in range(2):
                if path == "main":
                    b.op("vector", lambda e, hc=hc, ps_=ps_: e.tensor_tensor(
                        outm.bf(0, hc * 512, 512), p.pb[2 + hc][:, 0:512],
                        ibc.f32(ps_, 0, 512), AL.mult),
                        reads=[(p.pb[2 + hc], None)] + ibc.keys(ps_, 1), writes=outm.keys())
                else:
                    dstc = (ctxA if hc == 0 else ctxB)
                    pk0 = [pstrip.keys(0, 1)[0]]
                    b.op("vector", lambda e, hc=hc, ps_=ps_: e.tensor_tensor(
                        pstrip.bf(0, hc * 512, 512), p.pb[2 + hc][:, 0:512],
                        ibc.f32(ps_, 0, 512), AL.mult),
                        reads=[(p.pb[2 + hc], None)] + ibc.keys(ps_, 1), writes=pk0)
                    b.op("vector", lambda e, hc=hc: e.tensor_tensor(
                        pstrip.bf(0, hc * 512, 512), pstrip.bf(0, hc * 512, 512),
                        outm.bf(0, hc * 512, 512), AL.subtract),
                        reads=pk0 + outm.keys(), writes=pk0)
                    b.op("vector", lambda e, hc=hc, dstc=dstc, g=g: e.scalar_tensor_tensor(
                        dstc.bf(g // 2, (g % 2) * 512, 512),
                        pstrip.bf(0, hc * 512, 512), p.scol[:, 0:1],
                        outm.bf(0, hc * 512, 512), AL.mult, AL.add),
                        reads=pk0 + outm.keys() + [p.scol],
                        writes=[(dstc.ar.t, dstc.base + g // 2)])
        if g % 2 == 1:
            c = g // 2
            for (reg, half) in ((ctxA, 0), (ctxB, 1)):
                srcx = AP(reg.ar.t, reg.base * 1024 + c * T, [[reg.ar.nslots * 1024, 128], [1, T]])
                dstx = AP(a2ac_i.tensor, (c * 2 + half) * 128 * T, [[T, 128], [1, T]])
                b.op("sync", lambda e, dstx=dstx, srcx=srcx: e.dma_start(dstx, srcx),
                     reads=[(reg.ar.t, reg.base + c)], writes=[(a2ac_i, (c, half))], dma=True)
    tap(b, p, "l1a_ctxA", ctxA, 8)
    tap(b, p, "l1a_ctxB", ctxB, 8)
    for r in (qg, pstrip, qpb, outm, ibc, kAf, kBf, vf):
        A.free(r)

    b.op("gpsimd", lambda e: e.collective_compute(
        "AllToAll", AL.bypass, replica_groups=rg, ins=[a2ac_i], outs=[a2ac_o]),
        reads=[(a2ac_i, (c, x)) for c in range(ncores) for x in (0, 1)],
        writes=[a2ac_o], tag="cc")
    A.free(ctxA)
    A.free(ctxB)

    ctxf = A.alloc(16)
    for r in range(ncores):
        for half in range(2):
            src = AP(a2ac_o.tensor, (r * 2 + half) * 128 * T, [[T, 128], [1, T]])
            b.op("sync", lambda e, src=src, r=r, half=half: e.dma_start(
                ctxf.bf(r * 2 + half, 0, T), src),
                reads=[a2ac_o], writes=ctxf.keys(r * 2 + half, 1), dma=True)
    y = A.alloc(16)

    def wo_evac(bank, dabs, g0):
        c = dabs // 128
        b.op("vector", lambda e, c=c, g0=g0, bank=bank: e.tensor_copy(
            y.bf(c, g0, 512), p.pb[bank][:, 0:512]),
            reads=[(p.pb[bank], None)], writes=[y.keys(c, 1)[0]])

    tap(b, p, "l1a_ctxf", ctxf, 16)
    proj(b, p, p.wt["wo"], D, D,
         lambda c, g0: (ctxf.bf(c, g0, 512), [ctxf.keys(c, 1)[0]]),
         wo_evac, slab=256, banks=(0, 1, 2, 3))
    tap(b, p, "l1a_y", y, 16)
    A.free(ctxf)
    emit_outer_postln(b, p, y, sqs)
    A.free(y)
    A.free(sqs)


# version 27
# speedup vs baseline: 1.0356x; 1.0090x over previous
import numpy as np
import ml_dtypes
import concourse.bass as bass
import concourse.mybir as mybir
from concourse.bass_utils import run_bass_kernel_spmd

F32 = mybir.dt.float32
BF16 = mybir.dt.bfloat16
FP8 = mybir.dt.float8e4
I32 = mybir.dt.int32
AL = mybir.AluOpType
AF = mybir.ActivationFunctionType
AX = mybir.AxisListType.X
AP = bass.AP

S = 8192
D = 2048
MAGIC = float(np.float32(1.5 * 2 ** 23))

ENGINES = ("sync", "scalar", "vector", "tensor", "gpsimd")
DEBUG_TAPS = False


class Builder:

    NDMA = 12

    def __init__(self, nc):
        self.nc = nc
        self.q = {e: [] for e in ENGINES}
        self.sem_count = {}
        self.watermark = {}
        self.buf = {}
        self.sems = {}
        self.dma_rot = {e: 0 for e in ENGINES}
        self.pending_reads = {e: [] for e in ENGINES}
        self.pending_writes = {e: [] for e in ENGINES}
        self.ctx = None
        for e in ENGINES:
            self.sem_count["c_" + e] = 0
            for k in range(self.NDMA):
                self.sem_count[f"d_{e}_{k}"] = 0

    def _need_wait(self, eng, tok):
        if tok is None:
            return False
        sem, val = tok
        return self.watermark.get((eng, sem), 0) < val

    def _mark_wait(self, eng, tok):
        sem, val = tok
        if self.watermark.get((eng, sem), 0) < val:
            self.watermark[(eng, sem)] = val

    @staticmethod
    def _key(x, key=None):
        if isinstance(x, tuple):
            x, key = x
        t = getattr(x, "tensor", x)
        return (t.name, key)

    def op(self, eng, fn, reads=(), writes=(), dma=False, inc=True, tag=None):
        wmax = {}

        def addw(tok):
            if self._need_wait(eng, tok):
                wmax[tok[0]] = max(wmax.get(tok[0], 0), tok[1])

        for r in reads:
            k = self._key(r)
            st = self.buf.get(k)
            if st and st["w"]:
                addw(st["w"])
        wkeys = []
        for w in writes:
            k = self._key(w)
            wkeys.append(k)
            st = self.buf.get(k)
            if st:
                if st["w"]:
                    addw(st["w"])
                for sem2, val2 in st["r"].items():
                    addw((sem2, val2))
        if dma:
            kk = self.dma_rot[eng] % self.NDMA
            self.dma_rot[eng] += 1
            sem = f"d_{eng}_{kk}"
            incv = 16
            addw((sem, self.sem_count[sem]))
        elif tag == "cc":
            self.ncc = getattr(self, "ncc", 0) + 1
            sem = f"cc{self.ncc}"
            self.sem_count.setdefault(sem, 0)
            incv = 1
        else:
            sem = "c_" + eng
            incv = 1
        waits = sorted(wmax.items())
        for tokw in waits:
            self._mark_wait(eng, tokw)
        if inc:
            self.sem_count[sem] += incv
            tok = (sem, self.sem_count[sem])
        else:
            tok = None
        b = self

        def emit(e, fn=fn, waits=tuple(waits), sem=sem, incv=incv, inc=inc):
            for s, v in waits:
                e.wait_ge(b.sems[s], v)
            inst = fn(e)
            if inc:
                inst.then_inc(b.sems[sem], incv)
            return inst

        self.q[eng].append(emit)
        if inc:
            rl = list(reads) + self.pending_reads[eng]
            wl = wkeys + self.pending_writes[eng]
            self.pending_reads[eng] = []
            self.pending_writes[eng] = []
            for r in rl:
                k = self._key(r)
                st = self.buf.setdefault(k, {"w": None, "r": {}})
                st["r"][tok[0]] = max(st["r"].get(tok[0], 0), tok[1])
            for k in wl:
                self.buf[k] = {"w": tok, "r": {}}
        else:
            self.pending_reads[eng] += list(reads)
            self.pending_writes[eng] += wkeys
        return tok

    def wait(self, eng, tok):
        if not self._need_wait(eng, tok):
            return
        self._mark_wait(eng, tok)
        b = self

        def emit(e, tok=tok):
            e.wait_ge(b.sems[tok[0]], tok[1])
        self.q[eng].append(emit)

    def drain_dma(self, wait_eng, issue_eng):
        for k in range(self.NDMA):
            sem = f"d_{issue_eng}_{k}"
            v = self.sem_count[sem]
            if v:
                self.wait(wait_eng, (sem, v))

    def tok_of(self, x, key=None):
        st = self.buf.get(self._key(x, key))
        return st["w"] if st else None

    def flush(self):
        nc = self.nc
        for name in list(self.sem_count):
            self.sems[name] = self.ctx.enter_context(nc.semaphore(name))
        with nc.Block() as block:
            for ename in ENGINES:
                queue = self.q[ename]
                section = getattr(block, ename)

                def body(e, queue=queue):
                    for f in queue:
                        f(e)
                section(body)


def cdiv(a, b):
    return (a + b - 1) // b


WSPECS = [
    ("qkv", "L0_attn_qkv_w", "tern", 6144, 2048),
    ("ow",  "L0_attn_o_w", "tern", 2048, 2048),
    ("dn0", "L0_attn_ffn_down_w", "tern", 2048, 4096),
    ("up0", "L0_attn_ffn_up_w", "cast", 8192, 2048),
    ("g8",  "L0_ffn_w_gate", "fp8", 5632, 2048),
    ("u8",  "L0_ffn_w_up", "fp8", 5632, 2048),
    ("d8",  "L0_ffn_w_down", "fp8", 2048, 5632),
    ("wq",  "L1_attn_Wq", "rowcast", 2048, 2048),
    ("wk",  "L1_attn_Wk", "rowcast", 2048, 512),
    ("wv",  "L1_attn_Wv", "rowcast", 2048, 512),
    ("wo",  "L1_attn_Wo", "rowcast", 2048, 2048),
    ("ig",  "L1_ffn_wq_gate", "int8", 5632, 2048),
    ("iu",  "L1_ffn_wq_up", "int8", 5632, 2048),
    ("id8", "L1_ffn_wq_down", "int8", 2048, 5632),
]
TERN = [w for w in WSPECS if w[2] == "tern"]
FP8W = [w for w in WSPECS if w[2] == "fp8"]


def host_constants(core, ncores, T=1024):
    bf = ml_dtypes.bfloat16
    c = {}
    c["c_ident"] = np.eye(128, dtype=bf)
    c["c_identf"] = np.eye(128, dtype=np.float32)
    c["c_ones"] = np.ones((128, 8), dtype=bf)
    masks = np.zeros((4, 128, 512), dtype=np.float32)
    for di, delta in enumerate((0, 128, 256, 384)):
        k = np.arange(128)[:, None] + delta
        q = np.arange(512)[None, :]
        masks[di] = np.where(k > q, -1e30, 0.0)
    c["c_masks"] = masks.astype(bf)
    m01 = np.zeros((4, 128, 512), dtype=np.float32)
    for di, delta in enumerate((0, 128, 256, 384)):
        k = np.arange(128)[:, None] + delta
        q = np.arange(512)[None, :]
        m01[di] = np.where(k > q, 0.0, 1.0)
    c["c_mask01"] = m01.astype(bf)
    pos = np.arange(core * T, (core + 1) * T, dtype=np.float64)
    inv0 = 1.0 / (10000.0 ** (np.arange(0, 128, 2, dtype=np.float64) / 128.0))
    f0 = inv0[:, None] * pos[None, :]
    c["c_cos0"] = np.concatenate([np.cos(f0), np.cos(f0)], 0).astype(bf)
    c["c_sin0"] = np.concatenate([-np.sin(f0), np.sin(f0)], 0).astype(bf)
    rot = np.zeros((128, 128), dtype=np.float32)
    for i in range(128):
        rot[(i + 64) % 128, i] = 1.0
    c["c_rot"] = rot.astype(bf)
    inv1 = 1.0 / (10000.0 ** (np.arange(0, 256, 2, dtype=np.float64) / 256.0))
    f1 = inv1[:, None] * pos[None, :]
    c["c_cos1"] = np.cos(f1).astype(bf)
    c["c_sin1"] = np.sin(f1).astype(bf)
    return c


def host_shards(inputs, core, ncores):
    sh = {}
    for nm, src, kind, d0, d1 in WSPECS:
        w = np.asarray(inputs[src])
        if kind == "rowcast":
            r = d0 // ncores
            sh["sh_" + nm] = np.ascontiguousarray(w[core * r:(core + 1) * r, :])
        else:
            cshard = d1 // ncores
            sh["sh_" + nm] = np.ascontiguousarray(w[:, core * cshard:(core + 1) * cshard])
    return sh



class P:
    pass


class P:
    pass


class Arena:

    def __init__(self, t, nslots):
        self.t = t
        self.t32 = t.ap().bitcast(F32).tensor
        self.t8 = t.ap().bitcast(FP8).tensor
        self.nslots = nslots
        self.free_ = list(range(nslots))

    def alloc(self, n):
        fs = sorted(self.free_)
        runs = []
        start, prev = None, None
        for v in fs:
            if start is None:
                start, prev = v, v
            elif v == prev + 1:
                prev = v
            else:
                runs.append((start, prev - start + 1))
                start, prev = v, v
        if start is not None:
            runs.append((start, prev - start + 1))
        cands = [r for r in runs if r[1] >= n]
        if not cands:
            raise RuntimeError(f"arena: no {n} contiguous slots free (runs={runs})")
        base = min(cands, key=lambda r: r[1])[0]
        for s in range(base, base + n):
            self.free_.remove(s)
        return Region(self, base, n)

    def free(self, r):
        self.free_ += list(range(r.base, r.base + r.n))


class Region:
    def __init__(self, ar, base, n):
        self.ar, self.base, self.n = ar, base, n

    def keys(self, c0=0, nc=None):
        nc = self.n if nc is None else nc
        return [(self.ar.t, self.base + c0 + i) for i in range(nc)]

    def bf(self, chunk=0, t0=0, nt=None, nchunk=1):
        ar = self.ar
        nt = 1024 if nt is None else nt
        if nchunk == 1:
            return AP(ar.t, (self.base + chunk) * 1024 + t0,
                      [[ar.nslots * 1024, 128], [1, nt]])
        return AP(ar.t, (self.base + chunk) * 1024 + t0,
                  [[ar.nslots * 1024, 128], [1024, nchunk], [1, nt]])

    def f32(self, chunk=0, t0=0, nt=None, nchunk=1):
        ar = self.ar
        nt = 512 if nt is None else nt
        if nchunk == 1:
            return AP(ar.t32, (self.base + chunk) * 512 + t0,
                      [[ar.nslots * 512, 128], [1, nt]])
        return AP(ar.t32, (self.base + chunk) * 512 + t0,
                  [[ar.nslots * 512, 128], [512, nchunk], [1, nt]])

    def fp8(self, chunk=0, t0=0, nt=None):
        ar = self.ar
        nt = 2048 if nt is None else nt
        return AP(ar.t8, (self.base + chunk) * 2048 + t0,
                  [[ar.nslots * 2048, 128], [1, nt]])


def emit_staging_setup(b, p, ncores):
    nc = b.nc
    rg = [list(range(ncores))]

    p.wt, p.agin, p.shin = {}, {}, {}
    for nm, src, kind, d0, d1 in WSPECS:
        if kind == "rowcast":
            r = d0 // ncores
            p.shin[nm] = nc.dram_tensor("sh_" + nm, [r, d1], F32, kind="ExternalInput").ap()
            p.agin[nm] = nc.dram_tensor("agi_" + nm, [r, d1], BF16).ap()
            p.wt[nm] = nc.dram_tensor("ag_" + nm, [d0, d1], BF16,
                                      addr_space="Shared" if ncores == 8 else "Local").ap()
        else:
            csh = d1 // ncores
            dt = I32 if kind == "int8" else F32
            p.shin[nm] = nc.dram_tensor("sh_" + nm, [d0, csh], dt, kind="ExternalInput").ap()
            p.agin[nm] = nc.dram_tensor("agi_" + nm, [csh, d0], BF16).ap()
            p.wt[nm] = nc.dram_tensor("ag_" + nm, [d1, d0], BF16,
                                      addr_space="Shared" if ncores == 8 else "Local").ap()
    dri = nc.dram_tensor("dri", [5632], I32, kind="ExternalInput").ap()
    p.dricol = p.sb("dricol", [128, 44], I32)
    b.op("scalar", lambda e: e.dma_start(
        AP(p.dricol, 0, [[44, 128], [1, 44]]),
        AP(dri.tensor, 0, [[1, 128], [128, 44]]),
        allow_slow_non_contiguous=True), writes=[p.dricol], dma=True)
    ar_add_i = nc.dram_tensor("ar_add_i", [1, 8], F32).ap()
    ar_add_o = nc.dram_tensor("ar_add_o", [1, 8], F32).ap()
    NMX = 104
    ar_max_i = nc.dram_tensor("ar_max_i", [128, NMX], F32).ap()
    ar_max_o = nc.dram_tensor("ar_max_o", [128, NMX], F32).ap()

    nwsh = nc.dram_tensor("sh_nw", [1, D // ncores], F32, kind="ExternalInput").ap()
    nwreg = p.arena.alloc(1)
    BUF = {}

    def _alloc_bufs():
        BUF["in"] = [p.arena.alloc(4) for _ in range(2)]
        BUF["f2"] = [p.arena.alloc(4) for _ in range(2)]
        BUF["q"] = [p.arena.alloc(2) for _ in range(2)]
        BUF["t"] = [p.arena.alloc(1) for _ in range(2)]
    _alloc_bufs()
    rtmp = p.sb("st_rtmp", [128, 64], F32)
    acc_sum = p.sb("st_acc", [128, 4], F32)
    rmx = p.sb("st_rmx", [128, NMX], F32)
    rinv = p.sb("st_rinv", [128, NMX], F32)
    swcol = p.sb("st_swcol", [128, 4], F32)
    srow = p.sb("st_srow", [1, 16], F32)
    p.rmx, p.rinv, p.swcol, p.swrow = rmx, rinv, swcol, srow

    RMX_OFF = {"g8": 0, "u8": 44, "d8": 88}
    TIDX = {nm: i for i, (nm, *_r) in enumerate(TERN)}
    NUMEL = {nm: d0 * d1 for nm, _s, _k, d0, d1 in WSPECS}

    def chunk_iter(d0, csh):
        assert csh <= 2048, "staging chunker: shard too wide (use 8 cores)"
        jtot = d0 // 128
        jg = max(1, min(2048 // csh, jtot))
        for j0 in range(0, jtot, jg):
            yield j0, min(jg, jtot - j0)

    b.op("vector", lambda e: e.memset(srow[0:1, 0:16], 0.0), writes=[(srow, "sums"), (srow, "gsums")])

    PERM = {"g8", "u8"}

    def load_chunk(nm, kind, j0, jn, csh, inb, d0, q="scalar"):
        if nm in PERM:
            for j in range(jn):
                dst = inb.f32(0, j * csh, csh)
                b.op("gpsimd", lambda e, dst=dst, j=j, j0=j0, csh=csh, nm=nm: e.indirect_dma_start(
                    dst, None, p.shin[nm],
                    bass.IndirectOffsetOnAxis(p.dricol[:, j0 + j:j0 + j + 1], 0)),
                    reads=[p.shin[nm], p.dricol], writes=inb.keys(), dma=True)
        else:
            src_ap = AP(p.shin[nm].tensor, j0 * 128 * csh,
                        [[csh, 128], [128 * csh, jn], [1, csh]])
            if kind == "int8":
                d = inb.f32(0, 0, jn * csh)
                d = AP(d.tensor, d.offset, d.ap).bitcast(I32)
            else:
                d = inb.f32(0, 0, jn * csh)
            b.op(q, lambda e, a=src_ap, d=d: e.dma_start(d, a),
                 writes=inb.keys(), dma=True)

    ptr = {"i": 0}

    def stage_one(nm, src, kind, d0, d1):
        csh = d1 // ncores if kind != "rowcast" else d1
        if kind == "rowcast":
            b.op("gpsimd", lambda e: e.dma_start(p.agin[nm], p.shin[nm]),
                 reads=[p.shin[nm]], writes=[p.agin[nm]], dma=True)
            return
        ti = TIDX.get(nm)
        for j0, jn in chunk_iter(d0, csh):
            bi = ptr["i"] % 2
            ptr["i"] += 1
            inb, f2b, qb, tb = BUF["in"][bi], BUF["f2"][bi], BUF["q"][bi], BUF["t"][bi]
            n = jn * csh
            load_chunk(nm, kind, j0, jn, csh, inb, d0, q="gpsimd")
            if kind == "int8":
                src_i = inb.f32(0, 0, n)
                src_i = AP(src_i.tensor, src_i.offset, src_i.ap).bitcast(I32)
                b.op("vector", lambda e, s=src_i, dq=qb.bf(0, 0, n): e.tensor_scalar(
                    dq, s, -128.0, None, AL.add), reads=inb.keys(), writes=qb.keys())
            else:
                if kind == "cast":
                    s3 = inb.f32(0, 0, csh)
                    s3 = AP(s3.tensor, s3.offset, [s3.ap[0], [csh, jn], [1, csh]])
                    nb = nwreg.f32(0, 0, csh)
                    nb = AP(nb.tensor, nb.offset, [nb.ap[0], [0, jn], [1, csh]])
                    d3 = qb.bf(0, 0, csh)
                    d3 = AP(d3.tensor, d3.offset, [d3.ap[0], [csh, jn], [1, csh]])
                    b.op("vector", lambda e, s3=s3, nb=nb, d3=d3: e.tensor_tensor(
                        d3, s3, nb, AL.mult),
                        reads=inb.keys() + nwreg.keys(), writes=qb.keys())
                elif kind == "tern":
                    b.op("vector", lambda e, s=inb.f32(0, 0, n), d2=f2b.f32(0, 0, n), ti=ti:
                         e.tensor_scalar(d2, s, swcol[:, ti:ti + 1], MAGIC, AL.mult, AL.add),
                         reads=inb.keys() + [(swcol, None)], writes=f2b.keys())
                    b.op("vector", lambda e, d2=f2b.f32(0, 0, n): e.tensor_scalar(
                        d2, d2, -MAGIC, 1.0, AL.add, AL.min),
                        reads=f2b.keys(), writes=f2b.keys())
                    b.op("vector", lambda e, d2=f2b.f32(0, 0, n), dq=qb.bf(0, 0, n):
                         e.tensor_scalar(dq, d2, -1.0, None, AL.max),
                         reads=f2b.keys(), writes=qb.keys())
                else:
                    off = RMX_OFF[nm]
                    for j in range(jn):
                        b.op("scalar", lambda e, o=off + j0 + j,
                             d8=f2b.fp8(0, j * csh, csh), s3=inb.f32(0, j * csh, csh):
                             e.activation(d8, s3, AF.Copy, scale=rinv[:, o:o + 1]),
                             reads=inb.keys() + [(rinv, None)], writes=f2b.keys())
                    b.op("scalar", lambda e, s8=f2b.fp8(0, 0, n), dq=qb.bf(0, 0, n):
                         e.activation(dq, s8, AF.Copy), reads=f2b.keys(), writes=qb.keys())
            ncc = cdiv(csh, 128)
            for cc in range(ncc):
                cn = min(128, csh - cc * 128)
                for j in range(jn):
                    b.op("tensor", lambda e, j=j, cc=cc, cn=cn, qb=qb, csh=csh: e.transpose(
                        p.pbtr[0:cn, j * 128:(j + 1) * 128],
                        qb.bf(0, j * csh + cc * 128, cn), p.ident[:, :]),
                        reads=qb.keys() + [p.ident], writes=[(p.pb[7], None)],
                        inc=(j == jn - 1))
                b.op("vector", lambda e, cn=cn, jn=jn, tb=tb: e.tensor_copy(
                    AP(tb.ar.t, tb.base * 1024, [[tb.ar.nslots * 1024, cn], [1, jn * 128]]),
                    p.pbtr[0:cn, 0:jn * 128]),
                    reads=[(p.pb[7], None)], writes=tb.keys())
                dst = AP(p.agin[nm].tensor, (cc * 128) * d0 + j0 * 128,
                         [[d0, cn], [1, jn * 128]])
                src_t = AP(tb.ar.t, tb.base * 1024,
                           [[tb.ar.nslots * 1024, cn], [1, jn * 128]])
                b.op("scalar", lambda e, dst=dst, src_t=src_t: e.dma_start(dst, src_t),
                     reads=tb.keys(), writes=[(p.agin[nm], (cc, j0))], dma=True)

    WD = {w[0]: w for w in WSPECS}

    def ag2(names):
        reads = []
        for nm in names:
            _, _, kind, d0, d1 = WD[nm]
            if kind == "rowcast":
                reads.append(p.agin[nm])
            else:
                csh = d1 // ncores
                for cc in range(cdiv(csh, 128)):
                    for j0, _jn in chunk_iter(d0, csh):
                        reads.append((p.agin[nm], (cc, j0)))
        for nm in names:
            b.op("gpsimd", lambda e, nm=nm: e.collective_compute(
                "AllGather", AL.bypass, replica_groups=rg,
                ins=[p.agin[nm]], outs=[p.wt[nm]]),
                reads=reads, writes=[p.wt[nm]], tag="cc")


    def _passA():
        par = 0
        for nm, src, kind, d0, d1 in [w for w in WSPECS if w[2] == "tern"]:
            csh = d1 // ncores
            ti = TIDX.get(nm)
            first = True
            for j0, jn in chunk_iter(d0, csh):
                bi = par % 2
                par += 1
                inb = BUF["in"][bi]
                load_chunk(nm, kind, j0, jn, csh, inb, d0)
                c3 = inb.f32(0, 0, csh)
                c3 = AP(c3.tensor, c3.offset, [c3.ap[0], [csh, jn], [1, csh]])
                b.op("vector", lambda e, c3=c3, jn=jn: e.reduce_sum(
                    rtmp[:, 0:jn], c3, axis=AX, apply_absolute_value=True),
                    reads=inb.keys(), writes=[(rtmp, "a")])
                b.op("vector", lambda e, jn=jn: e.reduce_sum(
                    rtmp[:, 48:49], rtmp[:, 0:jn], axis=AX),
                    reads=[(rtmp, "a")], writes=[(rtmp, "b")])
                if first:
                    b.op("vector", lambda e, ti=ti: e.tensor_copy(
                        acc_sum[:, ti:ti + 1], rtmp[:, 48:49]),
                        reads=[(rtmp, "b")], writes=[(acc_sum, ti)])
                else:
                    b.op("vector", lambda e, ti=ti: e.tensor_add(
                        acc_sum[:, ti:ti + 1], acc_sum[:, ti:ti + 1], rtmp[:, 48:49]),
                        reads=[(rtmp, "b"), (acc_sum, ti)], writes=[(acc_sum, ti)])
                first = False
        for ti in range(len(TERN)):
            b.op("tensor", lambda e, ti=ti: e.matmul(
                p.pb[6][0:1, ti:ti + 1], p.onesf[:, 0:1], acc_sum[:, ti:ti + 1],
                start=(ti == 0), stop=(ti == len(TERN) - 1), skip_group_check=True),
                reads=[(acc_sum, ti), p.onesf], writes=[(p.pb[6], None)],
                inc=(ti == len(TERN) - 1))
        b.op("vector", lambda e: e.tensor_copy(srow[0:1, 0:4], p.pb[6][0:1, 0:4]),
             reads=[(p.pb[6], None)], writes=[(srow, "sums")])
        b.op("gpsimd", lambda e: e.dma_start(ar_add_i[0:1, 0:8], srow[0:1, 0:8]),
             reads=[(srow, "sums")], writes=[ar_add_i], dma=True)
        b.op("gpsimd", lambda e: e.collective_compute(
            "AllReduce", AL.add, replica_groups=rg, ins=[ar_add_i], outs=[ar_add_o]),
            reads=[ar_add_i], writes=[ar_add_o], tag="cc")
        b.op("gpsimd", lambda e: e.dma_start(srow[0:1, 4:8], ar_add_o[0:1, 0:4]),
             reads=[ar_add_o], writes=[(srow, "gsums")], dma=True)

        for nm, src, kind, d0, d1 in [w for w in WSPECS if w[2] == "fp8"]:
            csh = d1 // ncores
            for j0, jn in chunk_iter(d0, csh):
                bi = par % 2
                par += 1
                inb = BUF["in"][bi]
                load_chunk(nm, kind, j0, jn, csh, inb, d0)
                c3 = inb.f32(0, 0, csh)
                c3 = AP(c3.tensor, c3.offset, [c3.ap[0], [csh, jn], [1, csh]])
                off = RMX_OFF[nm]
                b.op("vector", lambda e, c3=c3, jn=jn, o=off + j0: e.reduce_max(
                    AP(rmx, o, [[NMX, 128], [1, jn]]), c3,
                    axis=AX, apply_absolute_value=True),
                    reads=inb.keys(), writes=[(rmx, (nm, j0))])
        b.op("gpsimd", lambda e: e.dma_start(ar_max_i, rmx[:, :]),
             reads=[(rmx, (nm2, j2)) for nm2, _s, k2, d02, d12 in WSPECS if k2 == "fp8"
                    for j2, _jn in chunk_iter(d02, d12 // ncores)],
             writes=[ar_max_i], dma=True)
        b.op("gpsimd", lambda e: e.collective_compute(
            "AllReduce", AL.max, replica_groups=rg, ins=[ar_max_i], outs=[ar_max_o]),
            reads=[ar_max_i], writes=[ar_max_o], tag="cc")
        b.op("gpsimd", lambda e: e.dma_start(rmx[:, :], ar_max_o),
             reads=[ar_max_o], writes=[(rmx, "g")], dma=True)

    def _scales():
        for ti, (nm, _s, _k, d0, d1) in enumerate(TERN):
            b.op("vector", lambda e, ti=ti, n=NUMEL[TERN[ti][0]]: e.tensor_scalar(
                srow[0:1, 8 + ti:9 + ti], srow[0:1, 4 + ti:5 + ti], 1.0 / n, 1e-5,
                AL.mult, AL.max), reads=[(srow, "gsums")], writes=[(srow, "m" + str(ti))])
        NTN = len(TERN)
        b.op("vector", lambda e: e.reciprocal(srow[0:1, 12:12 + NTN], srow[0:1, 8:8 + NTN]),
             reads=[(srow, "m0"), (srow, "m1"), (srow, "m2")], writes=[(srow, "recip")])
        b.op("tensor", lambda e: e.matmul(
            p.pb[6][:, 0:NTN], p.onesf[0:1, 0:128], srow[0:1, 12:12 + NTN],
            start=True, stop=True, skip_group_check=True),
            reads=[(srow, "recip"), p.onesf], writes=[(p.pb[6], None)])
        b.op("vector", lambda e: e.tensor_copy(swcol[:, 0:NTN], p.pb[6][:, 0:NTN]),
             reads=[(p.pb[6], None)], writes=[swcol])
        b.op("vector", lambda e: e.reciprocal(rinv[:, :], rmx[:, :]),
             reads=[(rmx, "g")], writes=[(rinv, "a")])
        b.op("vector", lambda e: e.tensor_scalar(rinv[:, :], rinv[:, :], 224.0, None, AL.mult),
             reads=[(rinv, "a")], writes=[rinv])

        cshD = D // ncores
        b.op("scalar", lambda e: e.dma_start(p.rowp[0:1, 0:cshD], nwsh),
             writes=[rkey(p, 0)], dma=True)
        for g0 in range(0, cshD, 512):
            gn = min(512, cshD - g0)
            b.op("tensor", lambda e, g0=g0, gn=gn: e.matmul(
                p.pb[5][:, 0:gn], p.onesf[0:1, 0:128], p.rowp[0:1, g0:g0 + gn],
                start=True, stop=True, skip_group_check=True),
                reads=[rkey(p, 0), p.onesf], writes=[(p.pb[5], None)])
            b.op("vector", lambda e, g0=g0, gn=gn: e.tensor_copy(
                nwreg.f32(0, g0, gn), p.pb[5][:, 0:gn]),
                reads=[(p.pb[5], None)], writes=nwreg.keys())

    def _quant(names):
        for nm in names:
            stage_one(*WD[nm])
        ag2(names)

    def _free(nw=False):
        for r in BUF["in"] + BUF["f2"] + BUF["q"] + BUF["t"] + ([nwreg] if nw else []):
            p.arena.free(r)

    p.stg = {"passA": _passA, "scales": _scales, "quant": _quant, "free": _free,
             "realloc": _alloc_bufs}



def build_program(ncores=8, T=1024, upto="all"):
    assert T == 1024
    from contextlib import ExitStack
    nc = bass.Bass("TRN2", target_bir_lowering=False)
    b = Builder(nc)
    ctx = ExitStack()
    b.ctx = ctx
    p = P()
    p.ctx = ctx
    p.T = T
    p.ncores = ncores

    x_in = nc.dram_tensor("x", [T, D], F32, kind="ExternalInput").ap()
    out = nc.dram_tensor("out", [T, D], F32, kind="ExternalOutput").ap()
    ident_h = nc.dram_tensor("c_ident", [128, 128], BF16, kind="ExternalInput").ap()
    identf_h = nc.dram_tensor("c_identf", [128, 128], F32, kind="ExternalInput").ap()

    def sb(name, shape, dt=F32):
        return ctx.enter_context(nc.sbuf_tensor(name, shape, dt))
    p.sb = sb

    NT = D // 128
    p.ident = sb("ident", [128, 128], BF16)
    p.identf = sb("identf", [128, 128], F32)
    p.ones = sb("ones", [128, 128], BF16)
    p.onesf = sb("onesf", [128, 128], F32)
    p.xT = sb("xT", [128, NT * T], BF16)
    NSLOT = 71
    p.arena = Arena(sb("arena", [128, NSLOT * 1024], BF16), NSLOT)
    p.rowp = sb("rowp", [128, 3072], F32)
    p.pb = [ctx.enter_context(nc.psum_tensor(f"pb{i}", [128, 512], F32)) for i in range(8)]
    p.pbtr = p.pb[7].ap().bitcast(BF16).tensor
    p.tr8 = sb("tr8", [128, 128], F32)
    p.mcol = sb("mcol", [128, 16], F32)

    def xTap(c0, nchunk=1, t0=0, nt=None):
        nt = T if nt is None else nt
        if nchunk == 1:
            return AP(p.xT, c0 * T + t0, [[NT * T, 128], [1, nt]])
        return AP(p.xT, c0 * T + t0, [[NT * T, 128], [T, nchunk], [1, nt]])
    p.xTap = xTap
    p.rowp = p.rowp


    with ctx:
        b.op("sync", lambda e: e.dma_start(p.ident[:, :], ident_h), writes=[p.ident], dma=True)
        b.op("sync", lambda e: e.dma_start(p.identf[:, :], identf_h), writes=[p.identf], dma=True)
        b.op("vector", lambda e: e.memset(p.ones[:, :], 1.0), writes=[p.ones])
        b.op("vector", lambda e: e.memset(p.onesf[:, :], 1.0), writes=[p.onesf])

        xio = p.arena.alloc(4)
        xio2 = p.arena.alloc(4)
        for i in range(T // 128):
            buf = xio if i % 2 == 0 else xio2
            b.op("sync", lambda e, i=i, buf=buf: e.dma_start(
                buf.f32(0, 0, D), x_in[i * 128:(i + 1) * 128, :]),
                writes=buf.keys(), dma=True)
            for f in range(NT):
                b.op("tensor", lambda e, i=i, f=f, buf=buf: e.transpose(
                    p.pb[6][:, 0:128], buf.f32(0, f * 128, 128), p.identf[:, :]),
                    reads=buf.keys() + [p.identf], writes=[(p.pb[6], None)])
                b.op("vector", lambda e, i=i, f=f: e.tensor_copy(
                    xTap(f, 1, i * 128, 128), p.pb[6][:, 0:128]),
                    reads=[(p.pb[6], None)], writes=[(p.xT, f)])
        p.arena.free(xio)
        p.arena.free(xio2)

        do_l0a = upto in ("l0a", "l0f", "all", "l1f_test")
        if upto != "m0":
            emit_staging_setup(b, p, ncores)
            p.stg["passA"]()
        if upto != "m0":
            p.stg["scales"]()
            p.stg["quant"](["qkv"])
        if do_l0a:
            dbg = emit_l0_attn_head(b, p)
            emit_l0_attn_quant(b, p)
        if upto != "m0":
            p.stg["quant"](["ow", "up0", "dn0"])
            p.stg["quant"](["wq", "wk", "wv", "wo"])
            p.stg["free"](nw=True)
        if do_l0a:
            emit_l0_attn_waves(b, p)
            emit_l0_attn_rest(b, p, dbg)
            p.dbg = dbg
            tapx(b, p, "x_l0a")
        if upto != "m0":
            p.stg["realloc"]()
            p.stg["quant"](["g8", "u8", "d8"])
            p.stg["free"]()
        if upto in ("l0f", "all", "l1f_test"):
            emit_l0_ffn(b, p)
        if upto != "m0":
            p.stg["realloc"]()
            p.stg["quant"](["ig", "iu", "id8"])
            p.stg["free"]()
        if upto in ("all",):
            emit_l1_attn(b, p, ncores)
        if upto in ("l1f_test", "all"):
            emit_l1_ffn(b, p)

        xo = p.arena.alloc(4)
        for i in range(T // 128):
            for f in range(NT):
                b.op("tensor", lambda e, i=i, f=f: e.transpose(
                    p.pbtr[:, 0:128], xTap(f, 1, i * 128, 128), p.ident[:, :]),
                    reads=[(p.xT, f), p.ident], writes=[(p.pb[7], None)])
                b.op("vector", lambda e, f=f: e.tensor_copy(
                    xo.f32(0, f * 128, 128), p.pbtr[:, 0:128]),
                    reads=[(p.pb[7], None)], writes=xo.keys())
            b.op("sync", lambda e, i=i, xo=xo: e.dma_start(
                out[i * 128:(i + 1) * 128, :], xo.f32(0, 0, D)),
                reads=xo.keys(), dma=True)
        p.arena.free(xo)
        for eng in ENGINES:
            b.drain_dma(eng, eng)
        from concourse.aot_env import allow_ici_all_to_all
        with allow_ici_all_to_all():
            b.flush()
    return nc


def input_names(nc):
    names = set()
    for a in nc.m.functions[0].allocations:
        if isinstance(a, mybir.MemoryLocationSet) and a.kind == "ExternalInput":
            nm = a.memorylocations[0].name
            if nm != "partition_id":
                names.add(nm)
    return names


def kernel(**inputs):
    ncores, T = 8, 1024
    x = np.ascontiguousarray(np.asarray(inputs["L0_attn_x"], dtype=np.float32).reshape(S, D))
    nc = build_program(ncores=ncores, T=T)
    names = input_names(nc)
    in_maps = []
    for c in range(ncores):
        m = host_inmap(inputs, x, c, ncores, T)
        m = {k: v for k, v in m.items() if k in names}
        missing = names - set(m)
        assert not missing, f"missing inputs: {missing}"
        in_maps.append(m)
    res = run_bass_kernel_spmd(nc, in_maps, core_ids=list(range(ncores)))
    out = np.concatenate([np.asarray(r["out"]) for r in res.results], axis=0)
    return out.reshape(1, S, D)


def rowap(p, slot, g0=0, gn=1024):
    pb_ = 32 * (slot // 3)
    off = 1024 * (slot % 3)
    return p.rowp[pb_:pb_ + 1, off + g0:off + g0 + gn]


def rkey(p, slot):
    return (p.rowp, ("row", slot))


def row_from_col(b, p, col_ap, ncol, slot, rdkeys):
    b.op("tensor", lambda e: e.transpose(p.pb[6][0:ncol, 0:128], col_ap, p.identf[:, :]),
         reads=rdkeys + [p.identf], writes=[(p.pb[6], None)])
    b.op("vector", lambda e: e.tensor_copy(p.tr8[0:ncol, 0:128], p.pb[6][0:ncol, 0:128]),
         reads=[(p.pb[6], None)], writes=[p.tr8])
    pb_ = 32 * (slot // 3)
    off = 1024 * (slot % 3)
    dst = p.rowp[pb_:pb_ + 1, off:off + ncol * 128]
    dst = AP(dst.tensor, dst.offset, [dst.ap[0], [128, ncol], [1, 128]])
    b.op("scalar", lambda e: e.dma_start(dst, p.tr8[0:ncol, 0:128]),
         reads=[p.tr8], writes=[rkey(p, slot)], dma=True)


def rop(b, p, eng, fn, rslots, wslot):
    outs = rowap(p, wslot)
    ins = [rowap(p, s) for s in rslots]
    b.op(eng, lambda e: fn(e, outs, *ins),
         reads=[rkey(p, s) for s in rslots], writes=[rkey(p, wslot)])


def bcast_row(b, p, slot, nt, dsts, t0=0, tok_off=0):
    pb_ = 32 * (slot // 3)
    for g0 in range(0, nt, 512):
        gn = min(512, nt - g0)
        b.op("tensor", lambda e, g0=g0, gn=gn: e.matmul(
            p.pb[5][:, 0:gn], p.onesf[pb_:pb_ + 1, 0:128],
            rowap(p, slot, t0 + g0, gn),
            start=True, stop=True, skip_group_check=True),
            reads=[rkey(p, slot), p.onesf], writes=[(p.pb[5], None)])
        for apf, wkeys in dsts:
            b.op("vector", lambda e, apf=apf, g0=g0, gn=gn: e.tensor_copy(
                apf(g0 + tok_off, gn), p.pb[5][:, 0:gn]),
                reads=[(p.pb[5], None)], writes=list(wkeys))


def colmax_abs(b, p, apfn, nch, slot, scratch, rdkeys, eng2="vector", T=None):
    T = T or p.T
    t_ = scratch.ar.t
    off0 = scratch.base * 1024
    mx = t_[0:128, off0:off0 + T]
    mn = t_[0:128, off0 + 1024:off0 + 1024 + T]
    aps = [apfn(c) for c in range(nch)]
    kmx, kmn = [scratch.keys(0, 1)[0]], [scratch.keys(1, 1)[0]]
    b.op("vector", lambda e, a0=aps[0], a1=aps[1]: e.tensor_max(mx, a0, a1),
         reads=rdkeys, writes=kmx)
    b.op(eng2, lambda e, a0=aps[0], a1=aps[1]: e.tensor_tensor(mn, a0, a1, AL.min),
         reads=rdkeys, writes=kmn)
    for c in range(2, nch):
        b.op("vector", lambda e, a=aps[c]: e.tensor_max(mx, mx, a),
             reads=rdkeys + kmx, writes=kmx)
        b.op(eng2, lambda e, a=aps[c]: e.tensor_tensor(mn, mn, a, AL.min),
             reads=rdkeys + kmn, writes=kmn)
    b.op(eng2, lambda e: e.tensor_scalar(mn, mn, -1.0, None, AL.mult), reads=kmn, writes=kmn)
    b.op("vector", lambda e: e.tensor_max(mx, mx, mn), reads=kmx + kmn, writes=kmx)
    ntt = T // 128
    for tt in range(ntt):
        b.op("tensor", lambda e, tt=tt: e.transpose(
            p.pbtr[0:128, tt * 128:(tt + 1) * 128],
            t_[0:128, off0 + tt * 128:off0 + (tt + 1) * 128], p.ident[:, :]),
            reads=kmx + [p.ident], writes=[(p.pb[7], None)], inc=(tt == ntt - 1))
    b.op("vector", lambda e: e.reduce_max(
        p.mcol[:, 0:ntt], AP(p.pbtr, 0, [[1024, 128], [128, ntt], [1, 128]]), axis=AX),
        reads=[(p.pb[7], None)], writes=[p.mcol])
    row_from_col(b, p, p.mcol[:, 0:ntt], ntt, slot, [(p.mcol, None)])


def sumrow(b, p, apfn, nch, slot, rdkeys, square=False, sq_scratch=None, T=None):
    T = T or p.T
    pb_ = 32 * (slot // 3)
    off = 1024 * (slot % 3)
    for g0 in range(0, T, 512):
        for c in range(nch):
            if square:
                sl = sq_scratch.bf(c % 2, 0, 512)
                b.op("scalar", lambda e, sl=sl, a=apfn(c, g0, 512): e.activation(
                    sl, a, AF.Square),
                    reads=rdkeys, writes=[sq_scratch.keys(c % 2, 1)[0]])
                rhs, rks = sl, [sq_scratch.keys(c % 2, 1)[0]]
            else:
                rhs, rks = apfn(c, g0, 512), rdkeys
            b.op("tensor", lambda e, rhs=rhs, c=c: e.matmul(
                p.pb[6][:, 0:512], p.ones[:, 0:128], rhs,
                start=(c == 0), stop=(c == nch - 1), skip_group_check=True),
                reads=rks + [p.ones], writes=[(p.pb[6], None)])
        b.op("vector", lambda e, g0=g0: e.tensor_copy(
            p.rowp[pb_:pb_ + 1, off + g0:off + g0 + 512], p.pb[6][0:1, 0:512]),
            reads=[(p.pb[6], None)], writes=[rkey(p, slot)])


def quant_int8_inplace(b, p, reg, nch, mult_bc, mult_keys, tmp, T=None, c0=0):
    T = T or p.T
    for ci in range(nch):
        c = c0 + ci
        tf = tmp.f32(2 * (ci % 2), 0, T)
        tk = tmp.keys(2 * (ci % 2), 2)
        b.op("vector", lambda e, c=c, tf=tf: e.tensor_tensor(
            tf, reg.bf(c, 0, T), mult_bc, AL.mult),
            reads=reg.keys(c, 1) + list(mult_keys), writes=tk)
        b.op("vector", lambda e, c=c, tf=tf: e.tensor_scalar(
            reg.bf(c, 0, T), tf, MAGIC, -MAGIC, AL.add, AL.add),
            reads=tk, writes=reg.keys(c, 1))


def proj(b, p, wt, CIN, DOUT, xr, evac, dout_ranges=None, slab=512, banks=(0, 1, 2, 3),
         wt_key=None):
    T = p.T
    ncin = CIN // 128
    slabslots = cdiv(ncin * slab, 1024)
    wbuf = [p.arena.alloc(slabslots), p.arena.alloc(slabslots)]
    ranges = dout_ranges or [(0, DOUT)]
    si = 0
    bi = 0
    ngr = T // 512
    for (r0, r1) in ranges:
        for d0 in range(r0, r1, slab):
            sl = min(slab, r1 - d0)
            reg = wbuf[si % 2]
            si += 1
            src = AP(wt.tensor, d0, [[DOUT, 128], [128 * DOUT, ncin], [1, sl]])
            dst = AP(reg.ar.t, reg.base * 1024,
                     [[reg.ar.nslots * 1024, 128], [sl, ncin], [1, sl]])
            b.op("sync", lambda e, src=src, dst=dst: e.dma_start(dst, src),
                 reads=[wt if wt_key is None else wt_key], writes=reg.keys(), dma=True)
            for dc in range(sl // 128):
                bks = [banks[(bi + g) % len(banks)] for g in range(ngr)]
                bi += ngr
                for c in range(ncin):
                    wap = AP(reg.ar.t, reg.base * 1024 + c * sl + dc * 128,
                             [[reg.ar.nslots * 1024, 128], [1, 128]])
                    for g in range(ngr):
                        xap, xkeys = xr(c, g * 512)
                        b.op("tensor", lambda e, wap=wap, xap=xap, c=c, g=g, bk=bks[g]: e.matmul(
                            p.pb[bk][:, 0:512], wap, xap,
                            start=(c == 0), stop=(c == ncin - 1)),
                            reads=reg.keys() + xkeys, writes=[(p.pb[bks[g]], None)],
                            inc=(c == ncin - 1))
                for g in range(ngr):
                    evac(bks[g], d0 + dc * 128, g * 512)
    for r in wbuf:
        p.arena.free(r)


def ln_rows(b, p, sl_sum, sl_sq, sl_m, sl_r, n, eps=1e-5):
    rop(b, p, "vector", lambda e, o, a: e.tensor_scalar(o, a, 1.0 / n, None, AL.mult),
        [sl_sum], sl_m)
    rop(b, p, "vector", lambda e, o, a: e.tensor_scalar(o, a, 1.0 / n, None, AL.mult),
        [sl_sq], sl_sq)
    assert sl_sum // 3 == sl_sq // 3
    rop(b, p, "vector", lambda e, o, m: e.tensor_tensor(o, m, m, AL.mult), [sl_m], sl_sum)
    rop(b, p, "vector", lambda e, o, sq, m2: e.tensor_sub(o, sq, m2), [sl_sq, sl_sum], sl_r)
    rop(b, p, "vector", lambda e, o, v: e.tensor_scalar(o, v, eps, None, AL.add), [sl_r], sl_r)
    rop(b, p, "scalar", lambda e, o, v: e.activation(o, v, AF.Sqrt), [sl_r], sl_r)
    rop(b, p, "vector", lambda e, o, sd: e.reciprocal(o, sd), [sl_r], sl_r)


def tapx(b, p, name):
    if not DEBUG_TAPS:
        return
    T, NT = p.T, 16
    d = b.nc.dram_tensor("dbg_" + name, [128, NT * T], BF16, kind="ExternalOutput").ap()
    for c in range(NT):
        b.op("gpsimd", lambda e, c=c: e.dma_start(
            AP(d.tensor, c * T, [[NT * T, 128], [1, T]]),
            AP(p.xT, c * T, [[NT * T, 128], [1, T]])),
            reads=[(p.xT, c)], dma=True)


def tap(b, p, name, reg, nslots, t0=0, nt=None):
    if not DEBUG_TAPS:
        return
    nt = nt if nt is not None else 1024
    d = b.nc.dram_tensor("dbg_" + name, [128, nslots * 1024], BF16, kind="ExternalOutput").ap()
    for c in range(nslots):
        b.op("gpsimd", lambda e, c=c: e.dma_start(
            AP(d.tensor, c * 1024, [[nslots * 1024, 128], [1, 1024]]),
            reg.bf(c, 0, 1024)), reads=[reg.keys(c, 1)[0]], dma=True)


def emit_l0_attn_head(b, p):
    nc, T, NT = b.nc, p.T, 16
    A = p.arena
    dbg = {}

    nwa = nc.dram_tensor("nw_attn", [D], F32, kind="ExternalInput").ap()
    p.nwa_c = p.sb("nwa_c", [128, 16], F32)
    b.op("scalar", lambda e: e.dma_start(
        AP(p.nwa_c, 0, [[16, 128], [1, 16]]),
        AP(nwa.tensor, 0, [[1, 128], [128, 16]]),
        allow_slow_non_contiguous=True), writes=[p.nwa_c], dma=True)

    cos0 = nc.dram_tensor("c_cos0", [128, T], BF16, kind="ExternalInput").ap()
    sin0 = nc.dram_tensor("c_sin0", [128, T], BF16, kind="ExternalInput").ap()
    rot0 = nc.dram_tensor("c_rot", [128, 128], BF16, kind="ExternalInput").ap()
    p.cos0 = p.sb("cos0", [128, T], BF16)
    p.sin0 = p.sb("sin0", [128, T], BF16)
    p.rotm = p.sb("rotm", [128, 128], BF16)
    for t, h in ((p.cos0, cos0), (p.sin0, sin0), (p.rotm, rot0)):
        b.op("scalar", lambda e, t=t, h=h: e.dma_start(t[:, :], h), writes=[t], dma=True)
    maskh = nc.dram_tensor("c_masks", [4, 128, 512], BF16, kind="ExternalInput").ap()
    p.maskt = p.sb("maskt", [128, 4 * 512], BF16)
    b.op("scalar", lambda e: e.dma_start(
        AP(p.maskt, 0, [[2048, 128], [512, 4], [1, 512]]),
        AP(maskh.tensor, 0, [[512, 128], [65536, 4], [1, 512]])),
        writes=[p.maskt], dma=True)

    SL_P0, SL_M1, SL_R1 = 0, 1, 2
    SL_S, SL_RRMS, SL_T2, SL_MULT, SL_L = 3, 4, 5, 6, 7
    sqs = A.alloc(2)
    xap = lambda c, g0, gn: AP(p.xT, c * T + g0, [[NT * T, 128], [1, gn]])
    xkeys = [(p.xT, f) for f in range(NT)]
    sumrow(b, p, xap, NT, SL_S, xkeys)
    sumrow(b, p, xap, NT, SL_T2, xkeys, square=True, sq_scratch=sqs)
    ln_rows(b, p, SL_S, SL_T2, SL_M1, SL_R1, D)

    ln0 = A.alloc(16)
    mbc = A.alloc(2)
    rbc = A.alloc(1)
    bcast_row(b, p, SL_M1, T, [(lambda g0, gn: mbc.f32(0, g0, gn), mbc.keys())])
    bcast_row(b, p, SL_R1, T, [(lambda g0, gn: rbc.bf(0, g0, gn), rbc.keys())])
    tmp4 = A.alloc(4)
    for c in range(NT):
        tf = tmp4.f32(2 * (c % 2), 0, T)
        tk = tmp4.keys(2 * (c % 2), 2)
        b.op("vector", lambda e, c=c, tf=tf: e.tensor_tensor(
            tf, xap(c, 0, T), mbc.f32(0, 0, T), AL.subtract),
            reads=[(p.xT, c)] + mbc.keys(), writes=tk)
        b.op("vector", lambda e, c=c, tf=tf: e.tensor_tensor(
            ln0.bf(c, 0, T), tf, rbc.bf(0, 0, T), AL.mult),
            reads=tk + rbc.keys(), writes=ln0.keys(c, 1))
    A.free(mbc)
    A.free(rbc)
    dbg["ln0"] = ln0
    tap(b, p, "ln0", ln0, 16)
    dbg["sqs"] = sqs
    dbg["ln0"] = ln0
    dbg["tmp4"] = tmp4
    p.l0dbg = dbg
    return dbg


def emit_l0_attn_quant(b, p):
    nc, T, NT = b.nc, p.T, 16
    A = p.arena
    dbg = p.l0dbg
    SL_P0, SL_M1, SL_R1 = 0, 1, 2
    SL_S, SL_RRMS, SL_T2, SL_MULT, SL_L = 3, 4, 5, 6, 7
    xap = lambda c, g0, gn: AP(p.xT, c * T + g0, [[NT * T, 128], [1, gn]])
    xkeys = [(p.xT, f) for f in range(NT)]
    sqs = dbg["sqs"]

    ln0 = dbg["ln0"]
    tmp4 = dbg["tmp4"]

    sumrow(b, p, lambda c, g0, gn: ln0.bf(c, g0, gn), NT, SL_S, ln0.keys(),
           square=True, sq_scratch=sqs)
    rop(b, p, "vector", lambda e, o, a: e.tensor_scalar(o, a, 1.0 / D, 1e-5, AL.mult, AL.add),
        [SL_S], SL_RRMS)
    rop(b, p, "scalar", lambda e, o, v: e.activation(o, v, AF.Sqrt), [SL_RRMS], SL_RRMS)
    rop(b, p, "vector", lambda e, o, v: e.reciprocal(o, v), [SL_RRMS], SL_RRMS)
    for c in range(NT):
        b.op("vector", lambda e, c=c: e.tensor_scalar(
            ln0.bf(c, 0, T), ln0.bf(c, 0, T), p.nwa_c[:, c:c + 1], None, AL.mult),
            reads=ln0.keys(c, 1) + [p.nwa_c], writes=ln0.keys(c, 1))
    hq = ln0
    dbg["hw"] = hq

    colmax_abs(b, p, lambda c: hq.bf(c, 0, T), NT, SL_T2, sqs, hq.keys())
    rop(b, p, "vector", lambda e, o, mx, rr: e.tensor_tensor(o, mx, rr, AL.mult),
        [SL_T2, SL_RRMS], SL_T2)
    rop(b, p, "vector", lambda e, o, a: e.tensor_scalar(o, a, 1e-5, None, AL.max),
        [SL_T2], SL_T2)
    rop(b, p, "vector", lambda e, o, a: e.reciprocal(o, a), [SL_T2], SL_S)
    rop(b, p, "vector", lambda e, o, a, rr: e.scalar_tensor_tensor(
        o, a, 127.0, rr, AL.mult, AL.mult), [SL_S, SL_RRMS], SL_MULT)
    mulbc = A.alloc(2)
    bcast_row(b, p, SL_MULT, T, [(lambda g0, gn: mulbc.f32(0, g0, gn), mulbc.keys())])
    quant_int8_inplace(b, p, hq, NT, mulbc.f32(0, 0, T), mulbc.keys(), tmp4)
    A.free(tmp4)
    A.free(mulbc)
    dbg["hq"] = hq
    tap(b, p, "hq", hq, 16)

    return dbg

def emit_l0_attn_waves(b, p):
    nc, T, NT = b.nc, p.T, 16
    A = p.arena
    dbg = p.l0dbg
    SL_P0, SL_M1, SL_R1 = 0, 1, 2
    SL_S, SL_RRMS, SL_T2, SL_MULT, SL_L = 3, 4, 5, 6, 7
    xap = lambda c, g0, gn: AP(p.xT, c * T + g0, [[NT * T, 128], [1, gn]])
    xkeys = [(p.xT, f) for f in range(NT)]
    sqs = dbg["sqs"]

    hq = dbg["hq"]
    b.op("vector", lambda e: e.tensor_copy(rowap(p, SL_P0), rowap(p, SL_T2)),
         reads=[rkey(p, SL_T2)], writes=[rkey(p, SL_P0)])
    b.op("vector", lambda e: e.tensor_scalar(
        rowap(p, SL_P0), rowap(p, SL_P0), p.swrow[0:1, 8:9], 1.0 / 127.0,
        AL.mult, AL.mult),
        reads=[rkey(p, SL_P0), (p.swrow, "m0")], writes=[rkey(p, SL_P0)])
    osbc = A.alloc(1)
    bcast_row(b, p, SL_P0, T, [(lambda g0, gn: osbc.bf(0, g0, gn), osbc.keys())])
    dbg["osbc"] = osbc
    ctx = A.alloc(16)
    qw_ = A.alloc(4)
    kw_ = A.alloc(4)
    vw_ = A.alloc(4)
    rotw = A.alloc(4)
    vtmp = A.alloc(1)
    pstrip = A.alloc(2)
    ibc = A.alloc(1)
    hkeys = hq.keys()

    def xr(c, g0):
        return hq.bf(c, g0, 512), [hq.keys(c, 1)[0]]

    for w in range(4):
        def qkv_evac(bank, dabs, g0, w=w):
            qkvi = dabs // 2048
            h = (dabs % 2048) // 128
            sl = h - w * 4
            if qkvi < 2:
                dst = (qw_ if qkvi == 0 else kw_)
                b.op("vector", lambda e, dst=dst, sl=sl, g0=g0, bank=bank: e.tensor_tensor(
                    dst.bf(sl, g0, 512), p.pb[bank][:, 0:512], osbc.bf(0, g0, 512), AL.mult),
                    reads=[(p.pb[bank], None)] + osbc.keys(), writes=dst.keys(sl, 1))
            else:
                b.op("vector", lambda e, g0=g0, bank=bank: e.tensor_tensor(
                    vtmp.bf(0, 0, 512), p.pb[bank][:, 0:512], osbc.bf(0, g0, 512), AL.mult),
                    reads=[(p.pb[bank], None)] + osbc.keys(), writes=vtmp.keys())
                for q4 in range(4):
                    b.op("tensor", lambda e, q4=q4: e.transpose(
                        p.pbtr[:, q4 * 128:(q4 + 1) * 128], vtmp.bf(0, q4 * 128, 128),
                        p.ident[:, :]),
                        reads=vtmp.keys() + [p.ident], writes=[(p.pb[7], None)],
                        inc=(q4 == 3))
                dst = AP(vw_.ar.t, (vw_.base) * 1024 + ((g0 // 128) * 4 + sl) * 128,
                         [[vw_.ar.nslots * 1024, 128], [512, 4], [1, 128]])
                b.op("vector", lambda e, dst=dst: e.tensor_copy(dst, p.pbtr[:, 0:512]),
                     reads=[(p.pb[7], None)], writes=vw_.keys())

        ranges = [(i * 2048 + w * 512, i * 2048 + (w + 1) * 512) for i in range(3)]
        proj(b, p, p.wt["qkv"], D, 6144, xr, qkv_evac, dout_ranges=ranges,
             slab=256, banks=(0, 1, 2, 3))

        for reg in (qw_, kw_):
            for sl in range(4):
                for g0 in (0, 512):
                    b.op("tensor", lambda e, reg=reg, sl=sl, g0=g0: e.matmul(
                        p.pb[4][:, 0:512], p.rotm[:, :], reg.bf(sl, g0, 512),
                        start=True, stop=True),
                        reads=reg.keys(sl, 1) + [p.rotm], writes=[(p.pb[4], None)])
                    b.op("vector", lambda e, reg=reg, sl=sl, g0=g0: e.tensor_copy(
                        rotw.bf(sl, g0, 512), p.pb[4][:, 0:512]),
                        reads=[(p.pb[4], None)], writes=rotw.keys(sl, 1))
            ap4 = lambda r, c0: AP(r.ar.t, r.base * 1024, [[r.ar.nslots * 1024, 128], [1024, 4], [1, 1024]])
            cosb = AP(p.cos0, 0, [[T, 128], [0, 4], [1, T]])
            sinb = AP(p.sin0, 0, [[T, 128], [0, 4], [1, T]])
            b.op("vector", lambda e, reg=reg: e.tensor_tensor(
                ap4(rotw, 0), ap4(rotw, 0), sinb, AL.mult),
                reads=rotw.keys() + [p.sin0], writes=rotw.keys())
            b.op("vector", lambda e, reg=reg: e.tensor_tensor(
                ap4(reg, 0), ap4(reg, 0), cosb, AL.mult),
                reads=reg.keys() + [p.cos0], writes=reg.keys())
            b.op("vector", lambda e, reg=reg: e.tensor_tensor(
                ap4(reg, 0), ap4(reg, 0), ap4(rotw, 0), AL.add),
                reads=reg.keys() + rotw.keys(), writes=reg.keys())

        if w == 0:
            tap(b, p, "qw0", qw_, 4)
            tap(b, p, "kw0", kw_, 4)
            tap(b, p, "vw0", vw_, 4)
        for hw2 in range(4):
            h = w * 4 + hw2
            for tg in range(2):
                g0 = tg * 512
                ktmax = (tg + 1) * 4

                def sc_emit(kt, hw2=hw2, g0=g0, tg=tg):
                    sb_ = kt % 2
                    diag = kt >= 4 * tg
                    b.op("tensor", lambda e, hw2=hw2, kt=kt, g0=g0, sb_=sb_, tg=tg: e.matmul(
                        p.pb[sb_][:, 0:512], kw_.bf(hw2, kt * 128, 128),
                        qw_.bf(hw2, g0, 512), start=True, stop=not (kt >= 4 * tg)),
                        reads=kw_.keys(hw2, 1) + qw_.keys(hw2, 1),
                        writes=[(p.pb[sb_], None)], inc=not diag)
                    if diag:
                        di = kt - 4 * tg
                        b.op("tensor", lambda e, di=di, sb_=sb_: e.matmul(
                            p.pb[sb_][:, 0:512], p.ident[:, :],
                            p.maskt[:, di * 512:(di + 1) * 512],
                            start=False, stop=True),
                            reads=[p.maskt, p.ident], writes=[(p.pb[sb_], None)])

                def ex_emit(kt):
                    sb_ = kt % 2
                    pst = pstrip.bf(kt % 2, 0, 512)
                    pkey = [pstrip.keys(kt % 2, 1)[0]]
                    b.op("scalar", lambda e, sb_=sb_, pst=pst: e.activation(
                        pst, p.pb[sb_][:, 0:512], AF.Exp, scale=float(1.0 / np.sqrt(128.0))),
                        reads=[(p.pb[sb_], None)], writes=pkey)

                def lav_emit(kt, hw2=hw2, ktmax=ktmax):
                    pst = pstrip.bf(kt % 2, 0, 512)
                    pkey = [pstrip.keys(kt % 2, 1)[0]]
                    b.op("tensor", lambda e, pst=pst, kt=kt, ktmax=ktmax: e.matmul(
                        p.pb[4][:, 0:512], p.ones[:, 0:128], pst,
                        start=(kt == 0), stop=(kt == ktmax - 1), skip_group_check=True),
                        reads=pkey + [p.ones], writes=[(p.pb[4], None)])
                    vap = AP(vw_.ar.t, vw_.base * 1024 + (kt * 4 + hw2) * 128,
                             [[vw_.ar.nslots * 1024, 128], [1, 128]])
                    b.op("tensor", lambda e, vap=vap, pst=pst, kt=kt, ktmax=ktmax: e.matmul(
                        p.pb[3][:, 0:512], vap, pst,
                        start=(kt == 0), stop=(kt == ktmax - 1)),
                        reads=vw_.keys() + pkey, writes=[(p.pb[3], None)])

                sc_emit(0)
                ex_emit(0)
                for kt in range(ktmax):
                    if kt + 1 < ktmax:
                        sc_emit(kt + 1)
                        ex_emit(kt + 1)
                    lav_emit(kt)
                b.op("vector", lambda e: e.reciprocal(ibc.f32(0, 0, 512), p.pb[4][:, 0:512]),
                     reads=[(p.pb[4], None)], writes=ibc.keys())
                b.op("vector", lambda e, h=h, g0=g0: e.tensor_tensor(
                    ctx.bf(h, g0, 512), p.pb[3][:, 0:512], ibc.f32(0, 0, 512), AL.mult),
                    reads=[(p.pb[3], None)] + ibc.keys(), writes=ctx.keys(h, 1))
    dbg["ctx"] = ctx
    tap(b, p, "ctx", ctx, 16)
    dbg["sqs"] = sqs
    dbg["osbc_qkv"] = osbc

    for r in (qw_, kw_, vw_, rotw, vtmp, pstrip, ibc, osbc):
        A.free(r)
    return dbg

def emit_l0_attn_rest(b, p, dbg):
    nc, T, NT = b.nc, p.T, 16
    A = p.arena
    SL_P0, SL_M1, SL_R1 = 0, 1, 2
    SL_S, SL_RRMS, SL_T2, SL_MULT, SL_L = 3, 4, 5, 6, 7
    ctx = dbg["ctx"]
    hq = dbg["hq"]
    sqs = dbg["sqs"]
    xap = lambda c, g0, gn: AP(p.xT, c * T + g0, [[NT * T, 128], [1, gn]])
    A.free(hq)

    colmax_abs(b, p, lambda c: ctx.bf(c, 0, T), NT, SL_T2, sqs, ctx.keys())
    rop(b, p, "vector", lambda e, o, a: e.tensor_scalar(o, a, 1e-5, None, AL.max),
        [SL_T2], SL_T2)
    rop(b, p, "vector", lambda e, o, a: e.reciprocal(o, a), [SL_T2], SL_MULT)
    rop(b, p, "vector", lambda e, o, a: e.tensor_scalar(o, a, 127.0, None, AL.mult),
        [SL_MULT], SL_MULT)
    b.op("vector", lambda e: e.tensor_copy(rowap(p, SL_P0), rowap(p, SL_T2)),
         reads=[rkey(p, SL_T2)], writes=[rkey(p, SL_P0)])
    b.op("vector", lambda e: e.tensor_scalar(
        rowap(p, SL_P0), rowap(p, SL_P0), p.swrow[0:1, 9:10], 1.0 / 127.0,
        AL.mult, AL.mult),
        reads=[rkey(p, SL_P0), (p.swrow, "m1")], writes=[rkey(p, SL_P0)])
    mulbc = A.alloc(2)
    osbc = A.alloc(1)
    tmp4 = A.alloc(4)
    bcast_row(b, p, SL_MULT, T, [(lambda g0, gn: mulbc.f32(0, g0, gn), mulbc.keys())])
    bcast_row(b, p, SL_P0, T, [(lambda g0, gn: osbc.bf(0, g0, gn), osbc.keys())])
    quant_int8_inplace(b, p, ctx, NT, mulbc.f32(0, 0, T), mulbc.keys(), tmp4)
    A.free(mulbc)
    A.free(tmp4)
    dbg["ctxq"] = ctx

    ln0 = A.alloc(16)
    mbc = A.alloc(2)
    rbc = A.alloc(1)
    bcast_row(b, p, SL_M1, T, [(lambda g0, gn: mbc.f32(0, g0, gn), mbc.keys())])
    bcast_row(b, p, SL_R1, T, [(lambda g0, gn: rbc.bf(0, g0, gn), rbc.keys())])
    t2s = A.alloc(4)
    for c in range(NT):
        tf = t2s.f32(2 * (c % 2), 0, T)
        tk = t2s.keys(2 * (c % 2), 2)
        b.op("vector", lambda e, c=c, tf=tf: e.tensor_tensor(
            tf, xap(c, 0, T), mbc.f32(0, 0, T), AL.subtract),
            reads=[(p.xT, c)] + mbc.keys(), writes=tk)
        b.op("vector", lambda e, c=c, tf=tf: e.tensor_tensor(
            ln0.bf(c, 0, T), tf, rbc.bf(0, 0, T), AL.mult),
            reads=tk + rbc.keys(), writes=ln0.keys(c, 1))
    A.free(mbc)
    A.free(rbc)
    A.free(t2s)

    x1 = A.alloc(16)
    sc2 = A.alloc(1)

    def o_evac(bank, dabs, g0):
        c = dabs // 128
        b.op("vector", lambda e, c=c, g0=g0, bank=bank: e.tensor_tensor(
            sc2.bf(0, 0, 512), p.pb[bank][:, 0:512], osbc.bf(0, g0, 512), AL.mult),
            reads=[(p.pb[bank], None)] + osbc.keys(), writes=sc2.keys())
        b.op("vector", lambda e, c=c, g0=g0: e.tensor_tensor(
            x1.bf(c, g0, 512), sc2.bf(0, 0, 512), ln0.bf(c, g0, 512), AL.add),
            reads=sc2.keys() + ln0.keys(c, 1), writes=x1.keys(c, 1))

    proj(b, p, p.wt["ow"], D, D,
         lambda c, g0: (ctx.bf(c, g0, 512), [ctx.keys(c, 1)[0]]),
         o_evac, slab=256, banks=(0, 1, 2, 3))
    A.free(ctx)
    A.free(ln0)
    A.free(osbc)
    A.free(sc2)
    dbg["x1"] = x1
    tap(b, p, "x1", x1, 16)

    if getattr(p, "stg_hook", None):
        p.stg_hook()
        p.stg_hook = None

    sumrow(b, p, lambda c, g0, gn: x1.bf(c, g0, gn), NT, SL_S, x1.keys(),
           square=True, sq_scratch=sqs)
    rop(b, p, "vector", lambda e, o, a: e.tensor_scalar(o, a, 1.0 / D, 1e-5, AL.mult, AL.add),
        [SL_S], SL_RRMS)
    rop(b, p, "scalar", lambda e, o, v: e.activation(o, v, AF.Sqrt), [SL_RRMS], SL_RRMS)
    rop(b, p, "vector", lambda e, o, v: e.reciprocal(o, v), [SL_RRMS], SL_RRMS)

    x2 = A.alloc(16)
    IN2 = 4096
    for th in range(2):
        t0 = th * 512
        r2bc = A.alloc(1)
        bcast_row(b, p, SL_RRMS, 512, [(lambda g0, gn: r2bc.bf(0, g0, gn), r2bc.keys())],
                  t0=t0)
        treg = A.alloc(16)
        def tka(ci, treg=treg):
            return (treg.ar.t, ("tch", treg.base, ci))
        def tapn(ci, g0=0, gn=512, treg=treg):
            return AP(treg.ar.t, treg.base * 1024 + ci * 512 + g0,
                      [[treg.ar.nslots * 1024, 128], [1, gn]])
        wg = [A.alloc(2), A.alloc(2)]
        wv2 = [A.alloc(2), A.alloc(2)]
        sg = A.alloc(1)
        sv = A.alloc(1)
        for di, d0 in enumerate(range(0, IN2, 128)):
            rg_ = wg[di % 2]
            rv_ = wv2[di % 2]
            for (reg, base) in ((rg_, d0), (rv_, IN2 + d0)):
                src = AP(p.wt["up0"].tensor, base, [[8192, 128], [128 * 8192, 16], [1, 128]])
                dst = AP(reg.ar.t, reg.base * 1024,
                         [[reg.ar.nslots * 1024, 128], [128, 16], [1, 128]])
                b.op("sync", lambda e, src=src, dst=dst: e.dma_start(dst, src),
                     reads=[p.wt["up0"]], writes=reg.keys(), dma=True)
            bkA, bkB = di % 2, 2 + di % 2
            for c in range(16):
                for (reg, bk) in ((rg_, bkA), (rv_, bkB)):
                    wap = AP(reg.ar.t, reg.base * 1024 + c * 128,
                             [[reg.ar.nslots * 1024, 128], [1, 128]])
                    b.op("tensor", lambda e, wap=wap, c=c, bk=bk, t0=t0: e.matmul(
                        p.pb[bk][:, 0:512], wap, x1.bf(c, t0, 512),
                        start=(c == 0), stop=(c == 15)),
                        reads=reg.keys() + [x1.keys(c, 1)[0]], writes=[(p.pb[bk], None)],
                        inc=(c == 15))
            b.op("vector", lambda e, bkA=bkA, sg=sg, r2bc=r2bc: e.tensor_tensor(
                sg.bf(0, 0, 512), p.pb[bkA][:, 0:512], r2bc.bf(0, 0, 512), AL.mult),
                reads=[(p.pb[bkA], None)] + r2bc.keys(), writes=sg.keys())
            b.op("scalar", lambda e, sg=sg: e.activation(
                sg.bf(0, 512, 512), sg.bf(0, 0, 512), AF.Silu),
                reads=sg.keys(), writes=sg.keys())
            b.op("vector", lambda e, bkB=bkB, sv=sv, r2bc=r2bc: e.tensor_tensor(
                sv.bf(0, 0, 512), p.pb[bkB][:, 0:512], r2bc.bf(0, 0, 512), AL.mult),
                reads=[(p.pb[bkB], None)] + r2bc.keys(), writes=sv.keys())
            b.op("vector", lambda e, di=di, sg=sg, sv=sv, tapn=tapn: e.tensor_tensor(
                tapn(di), sg.bf(0, 512, 512), sv.bf(0, 0, 512), AL.mult),
                reads=sg.keys() + sv.keys(), writes=[tka(di)])
        for r in wg + wv2 + [sg, sv]:
            A.free(r)

        colmax_abs(b, p, lambda ci: tapn(ci), 32, SL_T2, sqs,
                   [tka(ci) for ci in range(32)], T=512)
        rop(b, p, "vector", lambda e, o, a: e.tensor_scalar(o, a, 1e-5, None, AL.max),
            [SL_T2], SL_T2)
        rop(b, p, "vector", lambda e, o, a: e.reciprocal(o, a), [SL_T2], SL_MULT)
        rop(b, p, "vector", lambda e, o, a: e.tensor_scalar(o, a, 127.0, None, AL.mult),
            [SL_MULT], SL_MULT)
        b.op("vector", lambda e: e.tensor_copy(rowap(p, SL_P0, 0, 512),
                                               rowap(p, SL_T2, 0, 512)),
             reads=[rkey(p, SL_T2)], writes=[rkey(p, SL_P0)])
        b.op("vector", lambda e: e.tensor_scalar(
            rowap(p, SL_P0, 0, 512), rowap(p, SL_P0, 0, 512), p.swrow[0:1, 10:11],
            1.0 / 127.0, AL.mult, AL.mult),
            reads=[rkey(p, SL_P0), (p.swrow, "m2")], writes=[rkey(p, SL_P0)])
        mb2 = A.alloc(1)
        ob2 = A.alloc(1)
        bcast_row(b, p, SL_MULT, 512, [(lambda g0, gn: mb2.f32(0, g0, gn), mb2.keys())])
        bcast_row(b, p, SL_P0, 512, [(lambda g0, gn: ob2.bf(0, g0, gn), ob2.keys())])
        tq4 = A.alloc(2)
        for ci in range(32):
            tf = tq4.f32(ci % 2, 0, 512)
            tk2 = [tq4.keys(ci % 2, 1)[0]]
            b.op("vector", lambda e, ci=ci, tf=tf, tapn=tapn, mb2=mb2: e.tensor_tensor(
                tf, tapn(ci), mb2.f32(0, 0, 512), AL.mult),
                reads=[tka(ci)] + mb2.keys(), writes=tk2)
            b.op("vector", lambda e, ci=ci, tf=tf, tapn=tapn: e.tensor_scalar(
                tapn(ci), tf, MAGIC, -MAGIC, AL.add, AL.add),
                reads=tk2, writes=[tka(ci)])
        A.free(mb2)
        A.free(tq4)

        sc3 = A.alloc(1)

        def dn_evac(bank, dabs, g0, th=th, t0=t0, ob2=ob2, sc3=sc3):
            c = dabs // 128
            b.op("vector", lambda e, bank=bank: e.tensor_tensor(
                sc3.bf(0, 0, 512), p.pb[bank][:, 0:512], ob2.bf(0, 0, 512), AL.mult),
                reads=[(p.pb[bank], None)] + ob2.keys(), writes=sc3.keys())
            b.op("vector", lambda e, c=c: e.tensor_tensor(
                x2.bf(c, t0, 512), sc3.bf(0, 0, 512), x1.bf(c, t0, 512), AL.add),
                reads=sc3.keys() + x1.keys(c, 1), writes=[x2.keys(c, 1)[0]])

        projT(b, p, p.wt["dn0"], IN2, D,
              lambda c, g0: (tapn(c), [tka(c)]), dn_evac, 512, slab=128,
              banks=(0, 1, 2, 3))
        A.free(sc3)
        A.free(ob2)
        A.free(r2bc)
        A.free(treg)
    A.free(x1)
    dbg["x2"] = x2
    tap(b, p, "x2", x2, 16)

    sumrow(b, p, lambda c, g0, gn: x2.bf(c, g0, gn), NT, SL_S, x2.keys())
    sumrow(b, p, lambda c, g0, gn: x2.bf(c, g0, gn), NT, SL_T2, x2.keys(),
           square=True, sq_scratch=sqs)
    ln_rows(b, p, SL_S, SL_T2, SL_M1, SL_R1, D)
    mbc2 = A.alloc(2)
    rbc2 = A.alloc(1)
    bcast_row(b, p, SL_M1, T, [(lambda g0, gn: mbc2.f32(0, g0, gn), mbc2.keys())])
    bcast_row(b, p, SL_R1, T, [(lambda g0, gn: rbc2.bf(0, g0, gn), rbc2.keys())])
    t3s = A.alloc(4)
    for c in range(NT):
        tf = t3s.f32(2 * (c % 2), 0, T)
        tk3 = t3s.keys(2 * (c % 2), 2)
        b.op("vector", lambda e, c=c, tf=tf: e.tensor_tensor(
            tf, x2.bf(c, 0, T), mbc2.f32(0, 0, T), AL.subtract),
            reads=x2.keys(c, 1) + mbc2.keys(), writes=tk3)
        b.op("vector", lambda e, c=c, tf=tf: e.tensor_tensor(
            tf, tf, rbc2.bf(0, 0, T), AL.mult),
            reads=tk3 + rbc2.keys(), writes=tk3)
        b.op("vector", lambda e, c=c, tf=tf: e.tensor_tensor(
            xap(c, 0, T), xap(c, 0, T), tf, AL.add),
            reads=tk3 + [(p.xT, c)], writes=[(p.xT, c)])
    A.free(mbc2)
    A.free(rbc2)
    A.free(t3s)
    A.free(x2)
    A.free(sqs)


def projT(b, p, wt, CIN, DOUT, xr, evac, TT_, slab=128, banks=(0, 1, 2, 3)):
    ncin = CIN // 128
    slabslots = cdiv(ncin * slab, 1024)
    wbuf = [p.arena.alloc(slabslots), p.arena.alloc(slabslots)]
    si = 0
    bi = 0
    for d0 in range(0, DOUT, slab):
        sl = min(slab, DOUT - d0)
        reg = wbuf[si % 2]
        si += 1
        src = AP(wt.tensor, d0, [[DOUT, 128], [128 * DOUT, ncin], [1, sl]])
        dst = AP(reg.ar.t, reg.base * 1024,
                 [[reg.ar.nslots * 1024, 128], [sl, ncin], [1, sl]])
        b.op("sync", lambda e, src=src, dst=dst: e.dma_start(dst, src),
             reads=[wt], writes=reg.keys(), dma=True)
        for dc in range(sl // 128):
            bk = banks[bi % len(banks)]
            bi += 1
            for c in range(ncin):
                wap = AP(reg.ar.t, reg.base * 1024 + c * sl + dc * 128,
                         [[reg.ar.nslots * 1024, 128], [1, 128]])
                xap_, xkeys = xr(c, 0)
                b.op("tensor", lambda e, wap=wap, xap_=xap_, c=c, bk=bk: e.matmul(
                    p.pb[bk][:, 0:TT_], wap, xap_,
                    start=(c == 0), stop=(c == ncin - 1)),
                    reads=reg.keys() + xkeys, writes=[(p.pb[bk], None)],
                    inc=(c == ncin - 1))
            evac(bk, d0 + dc * 128, 0)
    for r in wbuf:
        p.arena.free(r)


def host_inmap(inputs, x, c, ncores, T):
    m = {"x": x[c * T:(c + 1) * T, :]}
    m.update(host_constants(c, ncores, T))
    m.update(host_shards(inputs, c, ncores))
    m["nw_attn"] = np.asarray(inputs["L0_attn_attn_norm_w"], dtype=np.float32)
    m["dri"] = np.asarray(inputs["L0_ffn_down_reorder_index"], dtype=np.int32)
    dirs = np.array([[sx_, sy_, sz_] for sx_ in (1.0, -1.0) for sy_ in (1.0, -1.0)
                     for sz_ in (1.0, -1.0)], dtype=np.float32) / np.sqrt(3.0)
    dv = np.zeros((4, 1), np.float32)
    dv[0:3, 0] = dirs[c]
    m["c_dirsh"] = np.repeat(dv.astype(ml_dtypes.bfloat16), 128, axis=1)
    m["c_hs"] = np.asarray(inputs["L1_attn_head_scales"], dtype=np.float32)[c].reshape(1, 1)
    m["s_g"] = np.asarray(inputs["L1_ffn_s_gate"], dtype=np.float32)
    m["s_u"] = np.asarray(inputs["L1_ffn_s_up"], dtype=np.float32)
    m["s_d"] = np.asarray(inputs["L1_ffn_s_down"], dtype=np.float32)
    csh = D // ncores
    m["sh_nw"] = np.asarray(inputs["L0_attn_ffn_norm_w"], dtype=np.float32)[
        None, c * csh:(c + 1) * csh]
    return m


def emit_outer_preln(b, p, out_reg, sqs):
    T, NT = p.T, 16
    A = p.arena
    SL_M1, SL_R1, SL_S, SL_T2 = 1, 2, 3, 5
    xap = lambda c, g0, gn: AP(p.xT, c * T + g0, [[NT * T, 128], [1, gn]])
    xkeys = [(p.xT, f) for f in range(NT)]
    sumrow(b, p, xap, NT, SL_S, xkeys)
    sumrow(b, p, xap, NT, SL_T2, xkeys, square=True, sq_scratch=sqs)
    ln_rows(b, p, SL_S, SL_T2, SL_M1, SL_R1, D)
    mbc = A.alloc(2)
    rbc = A.alloc(1)
    bcast_row(b, p, SL_M1, T, [(lambda g0, gn: mbc.f32(0, g0, gn), mbc.keys())])
    bcast_row(b, p, SL_R1, T, [(lambda g0, gn: rbc.bf(0, g0, gn), rbc.keys())])
    tmp4 = A.alloc(4)
    for c in range(NT):
        tf = tmp4.f32(2 * (c % 2), 0, T)
        tk = tmp4.keys(2 * (c % 2), 2)
        b.op("vector", lambda e, c=c, tf=tf, mbc=mbc: e.tensor_tensor(
            tf, xap(c, 0, T), mbc.f32(0, 0, T), AL.subtract),
            reads=[(p.xT, c)] + mbc.keys(), writes=tk)
        b.op("vector", lambda e, c=c, tf=tf, rbc=rbc, out_reg=out_reg: e.tensor_tensor(
            out_reg.bf(c, 0, T), tf, rbc.bf(0, 0, T), AL.mult),
            reads=tk + rbc.keys(), writes=out_reg.keys(c, 1))
    A.free(mbc)
    A.free(rbc)
    A.free(tmp4)


def emit_outer_postln(b, p, y_reg, sqs, nch=16):
    T, NT = p.T, 16
    A = p.arena
    SL_M1, SL_R1, SL_S, SL_T2 = 1, 2, 3, 5
    xap = lambda c, g0, gn: AP(p.xT, c * T + g0, [[NT * T, 128], [1, gn]])
    sumrow(b, p, lambda c, g0, gn: y_reg.bf(c, g0, gn), nch, SL_S, y_reg.keys())
    sumrow(b, p, lambda c, g0, gn: y_reg.bf(c, g0, gn), nch, SL_T2, y_reg.keys(),
           square=True, sq_scratch=sqs)
    ln_rows(b, p, SL_S, SL_T2, SL_M1, SL_R1, D)
    mbc = A.alloc(2)
    rbc = A.alloc(1)
    bcast_row(b, p, SL_M1, T, [(lambda g0, gn: mbc.f32(0, g0, gn), mbc.keys())])
    bcast_row(b, p, SL_R1, T, [(lambda g0, gn: rbc.bf(0, g0, gn), rbc.keys())])
    t3s = A.alloc(4)
    for c in range(NT):
        tf = t3s.f32(2 * (c % 2), 0, T)
        tk3 = t3s.keys(2 * (c % 2), 2)
        b.op("vector", lambda e, c=c, tf=tf, mbc=mbc, y_reg=y_reg: e.tensor_tensor(
            tf, y_reg.bf(c, 0, T), mbc.f32(0, 0, T), AL.subtract),
            reads=y_reg.keys(c, 1) + mbc.keys(), writes=tk3)
        b.op("vector", lambda e, c=c, tf=tf, rbc=rbc: e.tensor_tensor(
            tf, tf, rbc.bf(0, 0, T), AL.mult),
            reads=tk3 + rbc.keys(), writes=tk3)
        b.op("vector", lambda e, c=c, tf=tf: e.tensor_tensor(
            xap(c, 0, T), xap(c, 0, T), tf, AL.add),
            reads=tk3 + [(p.xT, c)], writes=[(p.xT, c)])
    A.free(mbc)
    A.free(rbc)
    A.free(t3s)


def emit_l0_ffn(b, p):
    nc, T, NT = b.nc, p.T, 16
    A = p.arena
    SL_P0, SL_M1, SL_R1 = 0, 1, 2
    SL_S, SL_RRMS, SL_T2, SL_MULT, SL_L = 3, 4, 5, 6, 7
    HID = 5632
    NH2 = HID // 128
    sqs = A.alloc(2)

    z = A.alloc(16)
    emit_outer_preln(b, p, z, sqs)

    colmax_abs(b, p, lambda c: z.bf(c, 0, T), NT, SL_T2, sqs, z.keys())
    rop(b, p, "vector", lambda e, o, a: e.reciprocal(o, a), [SL_T2], SL_MULT)
    rop(b, p, "vector", lambda e, o, a: e.tensor_scalar(o, a, 224.0, None, AL.mult),
        [SL_MULT], SL_MULT)
    rop(b, p, "vector", lambda e, o, a: e.tensor_scalar(o, a, 1.0 / 50176.0, None, AL.mult),
        [SL_T2], SL_P0)
    i224 = A.alloc(2)
    bcast_row(b, p, SL_MULT, T, [(lambda g0, gn: i224.f32(0, g0, gn), i224.keys())])
    qx8 = A.alloc(8)
    for c in range(NT):
        s8 = qx8.fp8(c // 2, (c % 2) * 1024, 1024)
        k8 = [qx8.keys(c // 2, 1)[0]]
        b.op("vector", lambda e, c=c, s8=s8, i224=i224: e.tensor_tensor(
            s8, z.bf(c, 0, T), i224.f32(0, 0, T), AL.mult),
            reads=z.keys(c, 1) + i224.keys(), writes=k8)
    A.free(i224)
    A.free(z)
    qx = qx8

    y = A.alloc(16)
    t8 = A.alloc(11)
    for th in range(2):
        t0 = th * 512
        treg = A.alloc(22)
        mr1 = A.alloc(1)
        bcast_row(b, p, SL_P0, 512, [(lambda g0, gn, mr1=mr1: mr1.f32(0, g0, gn), mr1.keys())], t0=t0)

        def tka(ci, treg=treg):
            return (treg.ar.t, treg.base + ci // 2)

        def tapn(ci, g0=0, gn=512, treg=treg):
            return AP(treg.ar.t, treg.base * 1024 + ci * 512 + g0,
                      [[treg.ar.nslots * 1024, 128], [1, gn]])

        wg = [A.alloc(2), A.alloc(2)]
        wu = [A.alloc(2), A.alloc(2)]
        sg = A.alloc(1)
        sv = A.alloc(1)
        for di in range(NH2):
            d0 = di * 128
            rg_ = wg[di % 2]
            ru_ = wu[di % 2]
            for (reg, wtn) in ((rg_, "g8"), (ru_, "u8")):
                src = AP(p.wt[wtn].tensor, d0, [[HID, 128], [128 * HID, 16], [1, 128]])
                dst = AP(reg.ar.t, reg.base * 1024,
                         [[reg.ar.nslots * 1024, 128], [128, 16], [1, 128]])
                b.op("sync", lambda e, src=src, dst=dst: e.dma_start(dst, src),
                     reads=[p.wt[wtn]], writes=reg.keys(), dma=True)
            bkA, bkB = di % 2, 2 + di % 2
            for c in range(16):
                for (reg, bk) in ((rg_, bkA), (ru_, bkB)):
                    wap = AP(reg.ar.t, reg.base * 1024 + c * 128,
                             [[reg.ar.nslots * 1024, 128], [1, 128]])
                    b.op("tensor", lambda e, wap=wap, c=c, bk=bk, t0=t0, reg=reg: e.matmul(
                        p.pb[bk][:, 0:512], wap,
                        qx.fp8(c // 2, (c % 2) * 1024 + t0, 512),
                        start=(c == 0), stop=(c == 15)),
                        reads=reg.keys() + [qx.keys(c // 2, 1)[0]], writes=[(p.pb[bk], None)],
                        inc=(c == 15))
            rc = (RMX_OFF_G["g8"] + di, RMX_OFF_G["u8"] + di)
            b.op("vector", lambda e, bkA=bkA, rc=rc, sg=sg, mr1=mr1: e.scalar_tensor_tensor(
                sg.bf(0, 0, 512), p.pb[bkA][:, 0:512], p.rmx[:, rc[0]:rc[0] + 1],
                mr1.f32(0, 0, 512), AL.mult, AL.mult),
                reads=[(p.pb[bkA], None), (p.rmx, "g")] + mr1.keys(), writes=sg.keys())
            b.op("scalar", lambda e, sg=sg: e.activation(
                sg.bf(0, 512, 512), sg.bf(0, 0, 512), AF.Silu),
                reads=sg.keys(), writes=sg.keys())
            b.op("vector", lambda e, bkB=bkB, rc=rc, sv=sv, mr1=mr1: e.scalar_tensor_tensor(
                sv.bf(0, 0, 512), p.pb[bkB][:, 0:512], p.rmx[:, rc[1]:rc[1] + 1],
                mr1.f32(0, 0, 512), AL.mult, AL.mult),
                reads=[(p.pb[bkB], None), (p.rmx, "g")] + mr1.keys(), writes=sv.keys())
            b.op("vector", lambda e, di=di, sg=sg, sv=sv, tapn=tapn: e.tensor_tensor(
                tapn(di), sg.bf(0, 512, 512), sv.bf(0, 0, 512), AL.mult),
                reads=sg.keys() + sv.keys(), writes=[tka(di)])
        for r in wg + wu + [sg, sv]:
            A.free(r)

        colmax_abs(b, p, lambda ci: tapn(ci), NH2, SL_T2, sqs,
                   [tka(ci) for ci in range(NH2)], T=512)
        rop(b, p, "vector", lambda e, o, a: e.reciprocal(
            rowap(p, SL_MULT, 0, 512), rowap(p, SL_T2, 0, 512)), [SL_T2], SL_MULT)
        rop(b, p, "vector", lambda e, o, a: e.tensor_scalar(
            rowap(p, SL_MULT, 0, 512), rowap(p, SL_MULT, 0, 512), 224.0, None, AL.mult),
            [SL_MULT], SL_MULT)
        rop(b, p, "vector", lambda e, o, a: e.tensor_scalar(
            rowap(p, SL_P0, 0, 512), rowap(p, SL_T2, 0, 512), 1.0 / 50176.0, None, AL.mult),
            [SL_T2], SL_P0)
        i224b = A.alloc(1)
        mr2 = A.alloc(1)
        bcast_row(b, p, SL_MULT, 512, [(lambda g0, gn, i224b=i224b: i224b.f32(0, g0, gn), i224b.keys())])
        bcast_row(b, p, SL_P0, 512, [(lambda g0, gn, mr2=mr2: mr2.f32(0, g0, gn), mr2.keys())])
        for ci in range(NH2):
            s8 = t8.fp8(ci // 4, (ci % 4) * 512, 512)
            k8 = [t8.keys(ci // 4, 1)[0]]
            b.op("vector", lambda e, ci=ci, s8=s8, tapn=tapn, i224b=i224b: e.tensor_tensor(
                s8, tapn(ci), i224b.f32(0, 0, 512), AL.mult),
                reads=[tka(ci)] + i224b.keys(), writes=k8)
        A.free(i224b)
        A.free(treg)

        def dn_evac(bank, dabs, g0, t0=t0, mr2=mr2):
            c = dabs // 128
            rc = RMX_OFF_G["d8"] + c // 8
            rcol = p.rmx[:, 88 + c:88 + c + 1]
            b.op("vector", lambda e, c=c, bank=bank, rcol=rcol, mr2=mr2, t0=t0: e.scalar_tensor_tensor(
                y.bf(c, t0, 512), p.pb[bank][:, 0:512], rcol, mr2.f32(0, 0, 512),
                AL.mult, AL.mult),
                reads=[(p.pb[bank], None), (p.rmx, "g")] + mr2.keys(),
                writes=[y.keys(c, 1)[0]])

        projT(b, p, p.wt["d8"], HID, D,
              lambda c, g0, t8=t8: (t8.fp8(c // 4, (c % 4) * 512, 512),
                                    [t8.keys(c // 4, 1)[0]]),
              dn_evac, 512, slab=128, banks=(0, 1, 2, 3))
        A.free(mr2)
        A.free(mr1)
    A.free(t8)
    A.free(qx)

    emit_outer_postln(b, p, y, sqs)
    A.free(y)
    A.free(sqs)


RMX_OFF_G = {"g8": 0, "u8": 44, "d8": 88}


def emit_l1_ffn(b, p):
    nc, T, NT = b.nc, p.T, 16
    A = p.arena
    HID = 5632
    NH2 = HID // 128
    sqs = A.alloc(2)

    sg_h = nc.dram_tensor("s_g", [HID], F32, kind="ExternalInput").ap()
    su_h = nc.dram_tensor("s_u", [HID], F32, kind="ExternalInput").ap()
    sd_h = nc.dram_tensor("s_d", [D], F32, kind="ExternalInput").ap()
    p.sgcol = p.sb("sgcol", [128, 44], F32)
    p.sucol = p.sb("sucol", [128, 44], F32)
    p.sdcol = p.sb("sdcol", [128, 16], F32)
    for t, h, n in ((p.sgcol, sg_h, 44), (p.sucol, su_h, 44), (p.sdcol, sd_h, 16)):
        b.op("scalar", lambda e, t=t, h=h, n=n: e.dma_start(
            AP(t, 0, [[n, 128], [1, n]]),
            AP(h.tensor, 0, [[1, 128], [128, n]]),
            allow_slow_non_contiguous=True), writes=[t], dma=True)

    tapx(b, p, "l1f_x")
    z = A.alloc(16)
    emit_outer_preln(b, p, z, sqs)
    tap(b, p, "l1f_z", z, 16)

    y = A.alloc(16)
    for th in range(2):
        t0 = th * 512
        treg = A.alloc(22)

        def tka(ci, treg=treg):
            return (treg.ar.t, treg.base + ci // 2)

        def tapn(ci, g0=0, gn=512, treg=treg):
            return AP(treg.ar.t, treg.base * 1024 + ci * 512 + g0,
                      [[treg.ar.nslots * 1024, 128], [1, gn]])

        wg = [A.alloc(2), A.alloc(2)]
        wu = [A.alloc(2), A.alloc(2)]
        sg = A.alloc(1)
        sv = A.alloc(1)
        for di in range(NH2):
            d0 = di * 128
            rg_ = wg[di % 2]
            ru_ = wu[di % 2]
            for (reg, wtn) in ((rg_, "ig"), (ru_, "iu")):
                src = AP(p.wt[wtn].tensor, d0, [[HID, 128], [128 * HID, 16], [1, 128]])
                dst = AP(reg.ar.t, reg.base * 1024,
                         [[reg.ar.nslots * 1024, 128], [128, 16], [1, 128]])
                b.op("sync", lambda e, src=src, dst=dst: e.dma_start(dst, src),
                     reads=[p.wt[wtn]], writes=reg.keys(), dma=True)
            bkA, bkB = di % 2, 2 + di % 2
            for c in range(16):
                for (reg, bk) in ((rg_, bkA), (ru_, bkB)):
                    wap = AP(reg.ar.t, reg.base * 1024 + c * 128,
                             [[reg.ar.nslots * 1024, 128], [1, 128]])
                    b.op("tensor", lambda e, wap=wap, c=c, bk=bk, t0=t0, reg=reg: e.matmul(
                        p.pb[bk][:, 0:512], wap, z.bf(c, t0, 512),
                        start=(c == 0), stop=(c == 15)),
                        reads=reg.keys() + [z.keys(c, 1)[0]], writes=[(p.pb[bk], None)],
                        inc=(c == 15))
            b.op("vector", lambda e, bkA=bkA, di=di, sg=sg: e.tensor_scalar(
                sg.bf(0, 0, 512), p.pb[bkA][:, 0:512], p.sgcol[:, di:di + 1], None, AL.mult),
                reads=[(p.pb[bkA], None), p.sgcol], writes=sg.keys())
            b.op("scalar", lambda e, sg=sg: e.activation(
                sg.bf(0, 512, 512), sg.bf(0, 0, 512), AF.Silu),
                reads=sg.keys(), writes=sg.keys())
            b.op("vector", lambda e, bkB=bkB, di=di, sv=sv: e.tensor_scalar(
                sv.bf(0, 0, 512), p.pb[bkB][:, 0:512], p.sucol[:, di:di + 1], None, AL.mult),
                reads=[(p.pb[bkB], None), p.sucol], writes=sv.keys())
            b.op("vector", lambda e, di=di, sg=sg, sv=sv, tapn=tapn: e.tensor_tensor(
                tapn(di), sg.bf(0, 512, 512), sv.bf(0, 0, 512), AL.mult),
                reads=sg.keys() + sv.keys(), writes=[tka(di)])
        for r in wg + wu + [sg, sv]:
            A.free(r)

        def dn_evac(bank, dabs, g0, t0=t0):
            c = dabs // 128
            b.op("vector", lambda e, c=c, bank=bank, t0=t0: e.tensor_scalar(
                y.bf(c, t0, 512), p.pb[bank][:, 0:512], p.sdcol[:, c:c + 1], None, AL.mult),
                reads=[(p.pb[bank], None), p.sdcol], writes=[y.keys(c, 1)[0]])

        projT(b, p, p.wt["id8"], HID, D,
              lambda c, g0: (tapn(c), [tka(c)]), dn_evac, 512, slab=128,
              banks=(0, 1, 2, 3))
        if th == 1:
            tap(b, p, "l1f_t1", treg, 22)
        A.free(treg)
    A.free(z)
    tap(b, p, "l1f_y", y, 16)
    emit_outer_postln(b, p, y, sqs)
    A.free(y)
    A.free(sqs)


def emit_l1_attn(b, p, ncores):
    nc, T, NT = b.nc, p.T, 16
    A = p.arena
    SL_P0, SL_M1, SL_R1 = 0, 1, 2
    SL_S, SL_RRMS, SL_T2, SL_MULT, SL_L = 3, 4, 5, 6, 7
    rg = [list(range(ncores))]
    DH = 256
    NG = S // 512

    dirs_h = nc.dram_tensor("c_dirsh", [4, 128], BF16, kind="ExternalInput").ap()
    hs_h = nc.dram_tensor("c_hs", [1, 1], F32, kind="ExternalInput").ap()
    cos1 = nc.dram_tensor("c_cos1", [128, T], BF16, kind="ExternalInput").ap()
    sin1 = nc.dram_tensor("c_sin1", [128, T], BF16, kind="ExternalInput").ap()
    mask01 = nc.dram_tensor("c_mask01", [4, 128, 512], BF16, kind="ExternalInput").ap()
    p.dirsv = p.sb("dirsv", [4, 128], BF16)
    p.hsv = p.sb("hsv", [1, 1], F32)
    p.cos1 = p.sb("cos1", [128, T], BF16)
    p.sin1 = p.sb("sin1", [128, T], BF16)
    p.mask01t = p.sb("mask01t", [128, 4 * 512], BF16)
    for t, h in ((p.dirsv, dirs_h), (p.hsv, hs_h), (p.cos1, cos1), (p.sin1, sin1)):
        b.op("scalar", lambda e, t=t, h=h: e.dma_start(t[:, :] if t.shape[0] > 1 else t[0:1, :], h),
             writes=[t], dma=True)
    b.op("scalar", lambda e: e.dma_start(
        AP(p.mask01t, 0, [[2048, 128], [512, 4], [1, 512]]),
        AP(mask01.tensor, 0, [[512, 128], [65536, 4], [1, 512]])),
        writes=[p.mask01t], dma=True)

    a2aq_i = nc.dram_tensor("a2aq_i", [ncores, 2, 128, T], BF16).ap()
    a2aq_o = nc.dram_tensor("a2aq_o", [ncores, 2, 128, T], BF16).ap()
    KVSLOT = 4 * 128 * T
    a2akv_i = nc.dram_tensor("a2akv_i", [ncores, KVSLOT], BF16).ap()
    a2akv_o = nc.dram_tensor("a2akv_o", [ncores, KVSLOT], BF16).ap()
    a2ac_i = nc.dram_tensor("a2ac_i", [ncores, 2, 128, T], BF16).ap()
    a2ac_o = nc.dram_tensor("a2ac_o", [ncores, 2, 128, T], BF16).ap()

    sqs = A.alloc(2)
    tapx(b, p, "l1a_x")
    z = A.alloc(16)
    emit_outer_preln(b, p, z, sqs)
    tap(b, p, "l1a_z", z, 16)

    qA = A.alloc(8)
    qB = A.alloc(8)
    kA = A.alloc(2)
    kB = A.alloc(2)
    vreg = A.alloc(4)
    vtmp = A.alloc(1)

    def xr(c, g0):
        return z.bf(c, g0, 512), [z.keys(c, 1)[0]]

    def q_evac(bank, dabs, g0):
        dc = dabs // 128
        h2, half = dc // 2, dc % 2
        dst = qA if half == 0 else qB
        b.op("vector", lambda e, dst=dst, h2=h2, g0=g0, bank=bank: e.tensor_copy(
            dst.bf(h2, g0, 512), p.pb[bank][:, 0:512]),
            reads=[(p.pb[bank], None)], writes=dst.keys(h2, 1))

    proj(b, p, p.wt["wq"], D, D, xr, q_evac, slab=256, banks=(0, 1, 2, 3))

    def k_evac(bank, dabs, g0):
        dc = dabs // 128
        kv2, half = dc // 2, dc % 2
        dst = kA if half == 0 else kB
        b.op("vector", lambda e, dst=dst, kv2=kv2, g0=g0, bank=bank: e.tensor_copy(
            dst.bf(kv2, g0, 512), p.pb[bank][:, 0:512]),
            reads=[(p.pb[bank], None)], writes=dst.keys(kv2, 1))

    proj(b, p, p.wt["wk"], D, 512, xr, k_evac, slab=256, banks=(0, 1, 2, 3))

    def v_evac(bank, dabs, g0):
        dc = dabs // 128
        b.op("vector", lambda e, g0=g0, bank=bank: e.tensor_copy(
            vtmp.bf(0, 0, 512), p.pb[bank][:, 0:512]),
            reads=[(p.pb[bank], None)], writes=vtmp.keys())
        for q4 in range(4):
            b.op("tensor", lambda e, q4=q4: e.transpose(
                p.pbtr[:, q4 * 128:(q4 + 1) * 128], vtmp.bf(0, q4 * 128, 128),
                p.ident[:, :]),
                reads=vtmp.keys() + [p.ident], writes=[(p.pb[7], None)],
                inc=(q4 == 3))
        dst = AP(vreg.ar.t, vreg.base * 1024 + (g0 // 128) * 512 + dc * 128,
                 [[vreg.ar.nslots * 1024, 128], [512, 4], [1, 128]])
        b.op("vector", lambda e, dst=dst: e.tensor_copy(dst, p.pbtr[:, 0:512]),
             reads=[(p.pb[7], None)], writes=vreg.keys())

    proj(b, p, p.wt["wv"], D, 512, xr, v_evac, slab=256, banks=(0, 1, 2, 3))
    A.free(z)

    rtmp1 = A.alloc(8)
    rtmp2 = A.alloc(8)
    for (ra, rb, t1, t2, nh) in ((qA, qB, rtmp1, rtmp2, 8), (kA, kB, rtmp1, rtmp2, 2)):
        a3 = lambda r, nh=nh: AP(r.ar.t, r.base * 1024, [[r.ar.nslots * 1024, 128], [1024, nh], [1, T]])
        cosb = AP(p.cos1, 0, [[T, 128], [0, nh], [1, T]])
        sinb = AP(p.sin1, 0, [[T, 128], [0, nh], [1, T]])
        b.op("vector", lambda e, ra=ra, t1=t1, sinb=sinb, a3=a3: e.tensor_tensor(
            a3(t1), a3(ra), sinb, AL.mult), reads=ra.keys() + [p.sin1], writes=t1.keys())
        b.op("vector", lambda e, rb=rb, t2=t2, sinb=sinb, a3=a3: e.tensor_tensor(
            a3(t2), a3(rb), sinb, AL.mult), reads=rb.keys() + [p.sin1], writes=t2.keys())
        b.op("vector", lambda e, ra=ra, cosb=cosb, a3=a3: e.tensor_tensor(
            a3(ra), a3(ra), cosb, AL.mult), reads=ra.keys() + [p.cos1], writes=ra.keys())
        b.op("vector", lambda e, rb=rb, cosb=cosb, a3=a3: e.tensor_tensor(
            a3(rb), a3(rb), cosb, AL.mult), reads=rb.keys() + [p.cos1], writes=rb.keys())
        b.op("vector", lambda e, ra=ra, t2=t2, a3=a3: e.tensor_tensor(
            a3(ra), a3(ra), a3(t2), AL.subtract), reads=ra.keys() + t2.keys(), writes=ra.keys())
        b.op("vector", lambda e, rb=rb, t1=t1, a3=a3: e.tensor_tensor(
            a3(rb), a3(rb), a3(t1), AL.add), reads=rb.keys() + t1.keys(), writes=rb.keys())
    A.free(rtmp1)
    A.free(rtmp2)
    tap(b, p, "l1a_qA", qA, 8)
    tap(b, p, "l1a_qB", qB, 8)
    tap(b, p, "l1a_kA", kA, 2)
    tap(b, p, "l1a_kB", kB, 2)
    tap(b, p, "l1a_v", vreg, 4)

    for h2 in range(ncores):
        for (reg, half) in ((qA, 0), (qB, 1)):
            dst = AP(a2aq_i.tensor, (h2 * 2 + half) * 128 * T, [[T, 128], [1, T]])
            b.op("sync", lambda e, dst=dst, reg=reg, h2=h2: e.dma_start(dst, reg.bf(h2, 0, T)),
                 reads=reg.keys(h2, 1), writes=[(a2aq_i, h2 * 2 + half)], dma=True)
    for j in range(ncores):
        kv2 = j // 4
        for (reg, half) in ((kA, 0), (kB, 1)):
            dst = AP(a2akv_i.tensor, j * KVSLOT + half * 128 * T, [[T, 128], [1, T]])
            b.op("sync", lambda e, dst=dst, reg=reg, kv2=kv2: e.dma_start(dst, reg.bf(kv2, 0, T)),
                 reads=reg.keys(kv2, 1), writes=[(a2akv_i, (j, half))], dma=True)
        src = AP(vreg.ar.t, vreg.base * 1024 + kv2 * 256,
                 [[vreg.ar.nslots * 1024, 128], [512, 8], [1, 256]])
        dst = AP(a2akv_i.tensor, j * KVSLOT + 2 * 128 * T, [[2048, 128], [256, 8], [1, 256]])
        b.op("sync", lambda e, dst=dst, src=src: e.dma_start(dst, src),
             reads=vreg.keys(), writes=[(a2akv_i, (j, "v"))], dma=True)
    b.op("gpsimd", lambda e: e.collective_compute(
        "AllToAll", AL.bypass, replica_groups=rg, ins=[a2aq_i], outs=[a2aq_o]),
        reads=[(a2aq_i, i) for i in range(2 * ncores)], writes=[a2aq_o], tag="cc")
    b.op("gpsimd", lambda e: e.collective_compute(
        "AllToAll", AL.bypass, replica_groups=rg, ins=[a2akv_i], outs=[a2akv_o]),
        reads=[(a2akv_i, (j, x)) for j in range(ncores) for x in (0, 1, "v")],
        writes=[a2akv_o], tag="cc")
    for r in (qA, qB, kA, kB, vreg, vtmp):
        A.free(r)

    kAf = A.alloc(8)
    kBf = A.alloc(8)
    vf = A.alloc(16)
    for r in range(ncores):
        for (reg, half) in ((kAf, 0), (kBf, 1)):
            src = AP(a2akv_o.tensor, r * KVSLOT + half * 128 * T, [[T, 128], [1, T]])
            dst = AP(reg.ar.t, reg.base * 1024 + r * T, [[reg.ar.nslots * 1024, 128], [1, T]])
            b.op("sync", lambda e, dst=dst, src=src: e.dma_start(dst, src),
                 reads=[a2akv_o], writes=[(reg.ar.t, reg.base + r)], dma=True)
        src = AP(a2akv_o.tensor, r * KVSLOT + 2 * 128 * T, [[2048, 128], [256, 8], [1, 256]])
        dst = AP(vf.ar.t, vf.base * 1024 + r * 8 * 256, [[vf.ar.nslots * 1024, 128], [256, 8], [1, 256]])
        b.op("sync", lambda e, dst=dst, src=src: e.dma_start(dst, src),
             reads=[a2akv_o], writes=vf.keys(2 * r, 2), dma=True)
    kkeys = kAf.keys() + kBf.keys()
    tap(b, p, "l1a_kAf", kAf, 8)
    tap(b, p, "l1a_vf", vf, 16)

    p.kpcol = p.sb("kpcol", [128, 64], F32)
    NKT = S // 128
    for kt in range(NKT):
        b.op("tensor", lambda e, kt=kt: e.matmul(
            p.pb[4][:, kt % 32:kt % 32 + 1],
            AP(kAf.ar.t, kAf.base * 1024 + kt * 128, [[kAf.ar.nslots * 1024, 3], [1, 128]]),
            p.dirsv[0:3, 0:1], start=True, stop=True, skip_group_check=True),
            reads=kkeys + [p.dirsv], writes=[(p.pb[4], None)], inc=(kt % 32 == 31))
        if kt % 32 == 31:
            b.op("vector", lambda e, kt=kt: e.tensor_copy(
                p.kpcol[:, kt - 31 + 0:kt + 1], p.pb[4][:, 0:32]),
                reads=[(p.pb[4], None)], writes=[p.kpcol])
    p.scol = p.sb("scol", [128, 1], F32)
    b.op("tensor", lambda e: e.matmul(
        p.pb[4][:, 0:1], p.onesf[0:1, 0:128], p.hsv[0:1, 0:1],
        start=True, stop=True, skip_group_check=True),
        reads=[p.hsv, p.onesf], writes=[(p.pb[4], None)])
    b.op("vector", lambda e: e.tensor_copy(p.scol[:, :], p.pb[4][:, 0:1]),
         reads=[(p.pb[4], None)], writes=[p.scol])

    ctxA = A.alloc(8)
    ctxB = A.alloc(8)
    qg = A.alloc(2)
    pstrip = A.alloc(4)
    qpb = A.alloc(2)
    outm = A.alloc(1)
    ibc = A.alloc(2)
    SL_QP = 3
    for g in range(NG):
        r, half = g // 2, g % 2
        gs = g % 2
        for cchunk in range(2):
            src = AP(a2aq_o.tensor, (r * 2 + cchunk) * 128 * T + half * 512,
                     [[T, 128], [1, 512]])
            b.op("sync", lambda e, src=src, cchunk=cchunk, gs=gs: e.dma_start(
                qg.bf(gs, cchunk * 512, 512), src),
                reads=[a2aq_o], writes=[(qg.ar.t, (qg.base + gs, cchunk))], dma=True)
        qgk = [(qg.ar.t, (qg.base + gs, 0)), (qg.ar.t, (qg.base + gs, 1))]
        b.op("tensor", lambda e, gs=gs: e.matmul(
            p.pb[5][:, 0:512], p.dirsv[0:3, 0:128],
            AP(qg.ar.t, (qg.base + gs) * 1024, [[qg.ar.nslots * 1024, 3], [1, 512]]),
            start=True, stop=True, skip_group_check=True),
            reads=qgk + [p.dirsv], writes=[(p.pb[5], None)])
        b.op("vector", lambda e, gs=gs: e.tensor_copy(qpb.bf(gs, 0, 512), p.pb[5][:, 0:512]),
             reads=[(p.pb[5], None)], writes=qpb.keys(gs, 1))
        ktmax = 4 * (g + 1)
        for path in ("main", "geo"):
            NSL = 4

            def sc_emit(kt, gs=gs, ktmax=ktmax):
                sb_ = kt % 2
                diag = kt >= ktmax - 4
                di = kt - (ktmax - 4)
                b.op("tensor", lambda e, kt=kt, sb_=sb_, diag=diag, gs=gs: e.matmul(
                    p.pb[sb_][:, 0:512],
                    AP(kAf.ar.t, kAf.base * 1024 + kt * 128, [[kAf.ar.nslots * 1024, 128], [1, 128]]),
                    qg.bf(gs, 0, 512), start=True, stop=False),
                    reads=kkeys + qgk, writes=[(p.pb[sb_], None)], inc=False)
                b.op("tensor", lambda e, kt=kt, sb_=sb_, diag=diag, gs=gs: e.matmul(
                    p.pb[sb_][:, 0:512],
                    AP(kBf.ar.t, kBf.base * 1024 + kt * 128, [[kBf.ar.nslots * 1024, 128], [1, 128]]),
                    qg.bf(gs, 512, 512), start=False, stop=not diag),
                    reads=kkeys + qgk, writes=[(p.pb[sb_], None)], inc=not diag)
                if diag:
                    b.op("tensor", lambda e, di=di, sb_=sb_: e.matmul(
                        p.pb[sb_][:, 0:512], p.ident[:, :],
                        p.maskt[:, di * 512:(di + 1) * 512],
                        start=False, stop=True),
                        reads=[p.maskt, p.ident], writes=[(p.pb[sb_], None)])

            def ex_emit(kt, path=path, gs=gs, ktmax=ktmax):
                sb_ = kt % 2
                diag = kt >= ktmax - 4
                di = kt - (ktmax - 4)
                pst = pstrip.bf(kt % NSL, 0, 512)
                pkey = [pstrip.keys(kt % NSL, 1)[0]]
                if path == "main":
                    b.op("scalar", lambda e, sb_=sb_, pst=pst: e.activation(
                        pst, p.pb[sb_][:, 0:512], AF.Exp, scale=float(1.0 / 16.0)),
                        reads=[(p.pb[sb_], None)], writes=pkey)
                else:
                    b.op("scalar", lambda e, kt=kt, pst=pst, gs=gs: e.activation(
                        pst, qpb.bf(gs, 0, 512), AF.Exp, scale=p.kpcol[:, kt:kt + 1]),
                        reads=qpb.keys(gs, 1) + [p.kpcol], writes=pkey)
                    if diag:
                        b.op("vector", lambda e, di=di, pst=pst: e.tensor_tensor(
                            pst, pst, p.mask01t[:, di * 512:(di + 1) * 512], AL.mult),
                            reads=pkey + [p.mask01t], writes=pkey)

            def lav_emit(kt, ktmax=ktmax):
                pst = pstrip.bf(kt % NSL, 0, 512)
                pkey = [pstrip.keys(kt % NSL, 1)[0]]
                b.op("tensor", lambda e, pst=pst, kt=kt, ktmax=ktmax: e.matmul(
                    p.pb[4][:, 0:512], p.ones[:, 0:128], pst,
                    start=(kt == 0), stop=(kt == ktmax - 1), skip_group_check=True),
                    reads=pkey + [p.ones], writes=[(p.pb[4], None)])
                for hc in range(2):
                    vap = AP(vf.ar.t, vf.base * 1024 + kt * 256 + hc * 128,
                             [[vf.ar.nslots * 1024, 128], [1, 128]])
                    b.op("tensor", lambda e, vap=vap, pst=pst, kt=kt, ktmax=ktmax, hc=hc: e.matmul(
                        p.pb[2 + hc][:, 0:512], vap, pst,
                        start=(kt == 0), stop=(kt == ktmax - 1)),
                        reads=vf.keys() + pkey, writes=[(p.pb[2 + hc], None)])

            LA = 1 if path == "main" else 3
            if path == "main":
                sc_emit(0)
            for k0 in range(min(LA, ktmax)):
                ex_emit(k0)
            for kt in range(ktmax):
                nxt = kt + LA
                if nxt < ktmax:
                    if path == "main":
                        sc_emit(nxt)
                    ex_emit(nxt)
                lav_emit(kt)
            ps_ = 0 if path == "main" else 1
            b.op("vector", lambda e, ps_=ps_: e.reciprocal(
                ibc.f32(ps_, 0, 512), p.pb[4][:, 0:512]),
                reads=[(p.pb[4], None)], writes=ibc.keys(ps_, 1))
            for hc in range(2):
                if path == "main":
                    b.op("vector", lambda e, hc=hc, ps_=ps_: e.tensor_tensor(
                        outm.bf(0, hc * 512, 512), p.pb[2 + hc][:, 0:512],
                        ibc.f32(ps_, 0, 512), AL.mult),
                        reads=[(p.pb[2 + hc], None)] + ibc.keys(ps_, 1), writes=outm.keys())
                else:
                    dstc = (ctxA if hc == 0 else ctxB)
                    pk0 = [pstrip.keys(0, 1)[0]]
                    b.op("vector", lambda e, hc=hc, ps_=ps_: e.tensor_tensor(
                        pstrip.bf(0, hc * 512, 512), p.pb[2 + hc][:, 0:512],
                        ibc.f32(ps_, 0, 512), AL.mult),
                        reads=[(p.pb[2 + hc], None)] + ibc.keys(ps_, 1), writes=pk0)
                    b.op("vector", lambda e, hc=hc: e.tensor_tensor(
                        pstrip.bf(0, hc * 512, 512), pstrip.bf(0, hc * 512, 512),
                        outm.bf(0, hc * 512, 512), AL.subtract),
                        reads=pk0 + outm.keys(), writes=pk0)
                    b.op("vector", lambda e, hc=hc, dstc=dstc, g=g: e.scalar_tensor_tensor(
                        dstc.bf(g // 2, (g % 2) * 512, 512),
                        pstrip.bf(0, hc * 512, 512), p.scol[:, 0:1],
                        outm.bf(0, hc * 512, 512), AL.mult, AL.add),
                        reads=pk0 + outm.keys() + [p.scol],
                        writes=[(dstc.ar.t, dstc.base + g // 2)])
        if g % 2 == 1:
            c = g // 2
            for (reg, half) in ((ctxA, 0), (ctxB, 1)):
                srcx = AP(reg.ar.t, reg.base * 1024 + c * T, [[reg.ar.nslots * 1024, 128], [1, T]])
                dstx = AP(a2ac_i.tensor, (c * 2 + half) * 128 * T, [[T, 128], [1, T]])
                b.op("sync", lambda e, dstx=dstx, srcx=srcx: e.dma_start(dstx, srcx),
                     reads=[(reg.ar.t, reg.base + c)], writes=[(a2ac_i, (c, half))], dma=True)
    tap(b, p, "l1a_ctxA", ctxA, 8)
    tap(b, p, "l1a_ctxB", ctxB, 8)
    for r in (qg, pstrip, qpb, outm, ibc, kAf, kBf, vf):
        A.free(r)

    b.op("gpsimd", lambda e: e.collective_compute(
        "AllToAll", AL.bypass, replica_groups=rg, ins=[a2ac_i], outs=[a2ac_o]),
        reads=[(a2ac_i, (c, x)) for c in range(ncores) for x in (0, 1)],
        writes=[a2ac_o], tag="cc")
    A.free(ctxA)
    A.free(ctxB)

    ctxf = A.alloc(16)
    for r in range(ncores):
        for half in range(2):
            src = AP(a2ac_o.tensor, (r * 2 + half) * 128 * T, [[T, 128], [1, T]])
            b.op("sync", lambda e, src=src, r=r, half=half: e.dma_start(
                ctxf.bf(r * 2 + half, 0, T), src),
                reads=[a2ac_o], writes=ctxf.keys(r * 2 + half, 1), dma=True)
    y = A.alloc(16)

    def wo_evac(bank, dabs, g0):
        c = dabs // 128
        b.op("vector", lambda e, c=c, g0=g0, bank=bank: e.tensor_copy(
            y.bf(c, g0, 512), p.pb[bank][:, 0:512]),
            reads=[(p.pb[bank], None)], writes=[y.keys(c, 1)[0]])

    tap(b, p, "l1a_ctxf", ctxf, 16)
    proj(b, p, p.wt["wo"], D, D,
         lambda c, g0: (ctxf.bf(c, g0, 512), [ctxf.keys(c, 1)[0]]),
         wo_evac, slab=256, banks=(0, 1, 2, 3))
    tap(b, p, "l1a_y", y, 16)
    A.free(ctxf)
    emit_outer_postln(b, p, y, sqs)
    A.free(y)
    A.free(sqs)
